# version 26
# baseline (speedup 1.0000x reference)
import numpy as np
import ml_dtypes
import concourse.bass as bass
import concourse.mybir as mybir
from concourse.bass_utils import run_bass_kernel_spmd

F32 = mybir.dt.float32
BF16 = mybir.dt.bfloat16
AF = mybir.ActivationFunctionType
AX = mybir.AxisListType
OP = mybir.AluOpType

T = 2048
D = 1024
NT = 16
NH = 16
DH = 64
EXP_BIAS = -12.0

BF = ml_dtypes.bfloat16


def subs_of(j):
    return 2 if j < 8 else 1


def ss_of(h, j, sub):
    return 24 * h + (2 * j if j < 8 else 8 + j) + sub


TOTAL_SS = 384

STEPS = []
for _p in range(8):
    for _Q in range(4):
        for _j in range(4 * _Q + 4):
            STEPS.append((_p, _Q, _j))
N_STEPS = len(STEPS)
STEP_IDX = {k: i for i, k in enumerate(STEPS)}
N_PHI = 32


def step_diag(p, Q, j):
    return j // 4 == Q


def qrange2(Q, j):
    qs = max(128 * j, 512 * Q)
    qe = 512 * (Q + 1)
    return qs, qe


def qrange(j, sub):
    qlo = 128 * j
    qs = qlo + 1024 * sub
    qe = min(2048, qs + 1024)
    return qs, qe


def build_nc(dbg=False):
    nc = bass.Bass("TRN2", target_bir_lowering=False, dynamic_dma_scratch_size=16384)

    xd = nc.dram_tensor("x", [T, D], F32, kind="ExternalInput").ap()
    wqd = nc.dram_tensor("wq", [D, D], F32, kind="ExternalInput").ap()
    wkd = nc.dram_tensor("wk", [D, D], F32, kind="ExternalInput").ap()
    wvd = nc.dram_tensor("wv", [D, D], F32, kind="ExternalInput").ap()
    wod = nc.dram_tensor("wo", [D, D], F32, kind="ExternalInput").ap()
    g1rd = nc.dram_tensor("g1r", [32, 4, 32], F32, kind="ExternalInput").ap()
    g1id = nc.dram_tensor("g1i", [32, 4, 32], F32, kind="ExternalInput").ap()
    g2rd = nc.dram_tensor("g2r", [4, 16, 16], F32, kind="ExternalInput").ap()
    g2id = nc.dram_tensor("g2i", [4, 16, 16], F32, kind="ExternalInput").ap()
    brd = nc.dram_tensor("br", [1, 512], F32, kind="ExternalInput").ap()
    bid = nc.dram_tensor("bi", [1, 512], F32, kind="ExternalInput").ap()
    yd = nc.dram_tensor("y", [T, D], F32, kind="ExternalOutput").ap()
    if dbg:
        dbg_out = {
            "d_xnfeat": nc.dram_tensor("d_xnfeat", [128, 8 * T], BF16, kind="ExternalOutput").ap(),
            "d_kt": nc.dram_tensor("d_kt", [128, 8 * T], BF16, kind="ExternalOutput").ap(),
            "d_v": nc.dram_tensor("d_v", [128, NT * NH * 65], BF16, kind="ExternalOutput").ap(),
            "d_ffnw": nc.dram_tensor("d_ffnw", [128, 3 * 2048], BF16, kind="ExternalOutput").ap(),
            "d_zfeat": nc.dram_tensor("d_zfeat", [128, 8 * T], BF16, kind="ExternalOutput").ap(),
            "d_lrb": nc.dram_tensor("d_lrb", [128, 2048], F32, kind="ExternalOutput").ap(),
            "d_wot": nc.dram_tensor("d_wot", [128, 8192], BF16, kind="ExternalOutput").ap(),
            "d_xin0": nc.dram_tensor("d_xin0", [128, 2048], F32, kind="ExternalOutput").ap(),
            "d_xn0": nc.dram_tensor("d_xn0", [128, 2048], BF16, kind="ExternalOutput").ap(),
            "d_stats": nc.dram_tensor("d_stats", [128, 16], F32, kind="ExternalOutput").ap(),
            "d_xnf_e": nc.dram_tensor("d_xnf_e", [128, 8 * T], BF16, kind="ExternalOutput").ap(),
            "d_kt_e": nc.dram_tensor("d_kt_e", [128, 8 * T], BF16, kind="ExternalOutput").ap(),
            "d_vdbg": nc.dram_tensor("d_vdbg", [128, 48], F32, kind="ExternalOutput").ap(),
            "d_lrow": nc.dram_tensor("d_lrow", [33, 1024], F32, kind="ExternalOutput").ap(),
            "d_lcols2": nc.dram_tensor("d_lcols2", [128, 16], F32, kind="ExternalOutput").ap(),
            "d_ubuf": nc.dram_tensor("d_ubuf", [128, 2048], BF16, kind="ExternalOutput").ap(),
            "d_zps": nc.dram_tensor("d_zps", [128, 2048], BF16, kind="ExternalOutput").ap(),
        }

    identd = nc.inline_tensor(np.eye(128, dtype=BF), "identc").ap()
    maskTd = nc.inline_tensor(
        np.triu(np.full((128, 128), -1e10, np.float32), k=1).astype(BF), "maskTc").ap()
    onesd = nc.inline_tensor(np.ones((1, 128), dtype=BF), "onesc").ap()
    rcol_np = np.zeros((32, 512), np.float32)
    rcol_np[np.arange(512) // 16, np.arange(512)] = 1.0
    rcold = nc.inline_tensor(rcol_np.astype(BF), "rcolc").ap()
    rcole_np = np.zeros((16, 512), np.float32)
    rcole_np[np.arange(512) % 16, np.arange(512)] = 1.0
    rcoled = nc.inline_tensor(rcole_np.astype(BF), "rcolec").ap()
    actc_np = np.tile(np.array([[1e-5, EXP_BIAS]], np.float32), (128, 1))
    actcd = nc.inline_tensor(actc_np, "actcc").ap()

    xnfeat = nc.alloc_sbuf_tensor("xnfeat", [128, 8 * T], BF16)
    QT = nc.alloc_sbuf_tensor("QT", [128, 8 * T], BF16)
    KT = nc.alloc_sbuf_tensor("KT", [128, 8 * T], BF16)
    V = nc.alloc_sbuf_tensor("V", [128, NT * NH * 65], BF16)
    WQT = nc.alloc_sbuf_tensor("WQT", [128, 8192], BF16)
    WKT = nc.alloc_sbuf_tensor("WKT", [128, 8192], BF16)
    WVT = nc.alloc_sbuf_tensor("WVT", [128, 8192], BF16)
    FFNW = nc.alloc_sbuf_tensor("FFNW", [128, 3 * 2048], BF16)
    ubuf = nc.alloc_sbuf_tensor("ubuf", [128, 2 * 1024], BF16)
    wraw = nc.alloc_sbuf_tensor("wraw", [128, 2 * 1024], BF16)
    bscr = nc.alloc_sbuf_tensor("bscr", [128, 1024], F32)
    identsb = nc.alloc_sbuf_tensor("identsb", [128, 128], BF16)
    maskTsb = nc.alloc_sbuf_tensor("maskTsb", [128, 128], BF16)
    onessb = nc.alloc_sbuf_tensor("onessb", [1, 128], BF16)
    rcolsb = nc.alloc_sbuf_tensor("rcolsb", [32, 512], BF16)
    rcolesb = nc.alloc_sbuf_tensor("rcolesb", [16, 512], BF16)
    actcsb = nc.alloc_sbuf_tensor("actcsb", [128, 2], F32)
    g1sb = nc.alloc_sbuf_tensor("g1sb", [32, 2, 4, 32], BF16)
    g2sb = nc.alloc_sbuf_tensor("g2sb", [16, 2, 4, 16], BF16)
    brsb = nc.alloc_sbuf_tensor("brsb", [1, 512], BF16)
    bisb = nc.alloc_sbuf_tensor("bisb", [1, 512], BF16)
    stats = nc.alloc_sbuf_tensor("stats", [128, 16], F32)
    g2exp = nc.alloc_sbuf_tensor("g2exp", [16, 4, 128], BF16)
    vdbg = nc.alloc_sbuf_tensor("vdbg", [128, 48], F32)

    zfeat = nc.alloc_sbuf_tensor_at("zfeat", [128, 8 * T], BF16,
                                    offset=nc.lookup_mloc(QT).addr)
    WOT = nc.alloc_sbuf_tensor_at("WOT", [128, 8192], BF16,
                                  offset=nc.lookup_mloc(WVT).addr)
    wqt_addr = nc.lookup_mloc(WQT).addr
    Lrow = nc.alloc_sbuf_tensor_at("Lrow", [33, 1024], F32, offset=wqt_addr)
    ubig = nc.alloc_sbuf_tensor_at("ubig", [128, 2 * 2048], BF16, offset=wqt_addr + 8192)
    outb = nc.alloc_sbuf_tensor_at("outb", [128, 2 * 1024], F32, offset=wqt_addr)
    wkt_addr = nc.lookup_mloc(WKT).addr
    Lrb = nc.alloc_sbuf_tensor_at("Lrb", [128, 2048], F32, offset=wkt_addr + 8192)
    t2sb = nc.alloc_sbuf_tensor_at("t2sb", [128, 512], F32, offset=wkt_addr + 8192)
    xin3 = nc.alloc_sbuf_tensor_at("xin3", [128, 2 * 1024], F32, offset=wkt_addr)
    Lcols = nc.alloc_sbuf_tensor("Lcols", [128, 16], F32)
    Lcols2 = nc.alloc_sbuf_tensor("Lcols2", [128, 16], F32)
    LcolsB = nc.alloc_sbuf_tensor("LcolsB", [128, 16], F32)
    Lcols2B = nc.alloc_sbuf_tensor("Lcols2B", [128, 16], F32)
    xin0 = nc.alloc_sbuf_tensor_at("xin0", [128, 2 * 1024], F32,
                                   offset=nc.lookup_mloc(FFNW).addr)
    xn0 = nc.alloc_sbuf_tensor_at("xn0", [128, 2 * 1024], BF16,
                                  offset=nc.lookup_mloc(QT).addr)
    wraw4 = nc.alloc_sbuf_tensor_at("wraw4", [128, 4 * 1024], BF16,
                                    offset=nc.lookup_mloc(KT).addr)
    tmpsb = nc.alloc_sbuf_tensor_at("tmpsb", [128, 1024], F32,
                                    offset=nc.lookup_mloc(wraw).addr)
    g1exp = nc.alloc_sbuf_tensor_at("g1exp", [32, 4, 512], BF16,
                                    offset=nc.lookup_mloc(ubuf).addr)
    LrowB = nc.alloc_sbuf_tensor_at("LrowB", [33, 1024], F32,
                                    offset=nc.lookup_mloc(bscr).addr)

    AB4 = nc.alloc_psum_tensor("AB4", [128, 2048], F32)
    A2 = AB4[:, 0:1024]
    B2 = AB4[:, 1024:2048]
    Z = nc.alloc_psum_tensor("Z", [128, 2048], F32)
    A2b = A2.bitcast(BF16)
    B2b = B2.bitcast(BF16)

    sems = {}
    for name in ("s_x s_xB s_c s_c2 s_w s_wB s_wC s_wD s_wE s_wF s_lb s_xr s_out s_outB s_stats s_rstd s_ln s_tpw s_tpx "
                 "s_wde s_xne s_qkmm s_qkev s_vmm s_vev s_vones s_fbA s_fbB "
                 "s_fbC s_fbD s_fbP s_ffnw s_st s_exp s_zu s_zev s_lsc s_lrec s_lga s_norm "
                 "s_p3mm s_fin s_dve s_dbg s_raw s_rawA s_msk").split():
        sems[name] = nc.alloc_semaphore(name)
    S = type("S", (), sems)

    WTS = [("wq", WQT), ("wk", WKT), ("wv", WVT), ("wo", WOT)]
    wdrams = {"wq": wqd, "wk": wkd, "wv": wvd, "wo": wod}

    def wdest(wt, r):
        return wt[:].rearrange("p (dc col) -> p dc col", col=1024)[:, :, r * 128:(r + 1) * 128]

    def xnf(d):
        return xnfeat[:, d * T:(d + 1) * T]

    def bk(W, cf, c):
        return (W * 4 + cf) * 4 + c

    def pool_cnt_before(W, cf, c):
        full = (W * 4 + cf)
        return full * 3 + max(0, c - 1)

    st_halves = {}

    def st_slices(j, sub):
        qs, qe = qrange(j, sub)
        L = qe - qs
        out = []
        s = 0
        while s < L:
            e = min(L, s + 512)
            out.append((s, e))
            s = e
        return out

    def z_slices(j, sub):
        qs, qe = qrange(j, sub)
        out = []
        s = qs
        while s < qe:
            e = min(qe, (s // 512 + 1) * 512)
            out.append((s, e))
            s = e
        return out

    with nc.Block() as block:

        @block.sync
        def _(sync):
            sync.dma_start(out=identsb[:], in_=identd[:]).then_inc(S.s_c, 16)
            sync.dma_start(out=maskTsb[:], in_=maskTd[:]).then_inc(S.s_c, 16)
            sync.dma_start(out=onessb[:], in_=onesd[:]).then_inc(S.s_c, 16)
            sync.dma_start(out=rcolsb[:], in_=rcold[:]).then_inc(S.s_c, 16)
            sync.dma_start(out=rcolesb[:], in_=rcoled[:]).then_inc(S.s_c, 16)
            sync.dma_start(out=actcsb[:], in_=actcd[:]).then_inc(S.s_c, 16)
            for t in range(NT):
                if t >= 2:
                    sync.wait_ge(S.s_ln, t - 1)
                sync.dma_start(out=xin0[:, (t % 2) * 1024:(t % 2 + 1) * 1024],
                               in_=xd[t * 128:(t + 1) * 128, :]
                               ).then_inc(S.s_x if t % 2 == 0 else S.s_xB, 16)
            if dbg:
                sync.wait_ge(S.s_vmm, 16)
                sync.dma_start(out=dbg_out["d_xnf_e"], in_=xnfeat[:]).then_inc(S.s_c, 16)
                sync.dma_start(out=dbg_out["d_kt_e"], in_=KT[:]).then_inc(S.s_c, 16)
            def lrview2(b):
                base = Lrow if b % 2 == 0 else LrowB
                return base[0:33:32, 0:1024].rearrange("o (p k) -> o p k", k=16)

            def lcols2(b, t2=False):
                if t2:
                    return (Lcols2 if b % 2 == 0 else Lcols2B)[:]
                return (Lcols if b % 2 == 0 else LcolsB)[:]

            def scat2(b):
                sync.wait_ge(S.s_zev, 4 * b + 3)
                if b >= 2:
                    sync.wait_ge(S.s_lrec, b - 1)
                sync.dma_start(out=lcols2(b), in_=lrview2(b)).then_inc(S.s_lsc, 16)

            def gath_bcast2(b):
                base = Lrow if b % 2 == 0 else LrowB
                sync.wait_ge(S.s_lrec, b + 1)
                sync.dma_start(out=lrview2(b), in_=lcols2(b, t2=True)
                               ).then_inc(S.s_lga, 16)
                sync.wait_ge(S.s_lga, 16 * (b + 1))
                if b >= 2:
                    sync.wait_ge(S.s_norm, 2 * (2 * b - 2))
                for phi in (2 * b, 2 * b + 1):
                    Q = phi % 4
                    co = 512 * (phi % 2)
                    sync.dma_start(out=Lrb[0:64, 512 * Q:512 * (Q + 1)],
                                   in_=base[0:1, co:co + 512].unsqueeze(1
                                   ).broadcast_to([1, 64, 512])).then_inc(S.s_lb, 16)
                    sync.dma_start(out=Lrb[64:128, 512 * Q:512 * (Q + 1)],
                                   in_=base[32:33, co:co + 512].unsqueeze(1
                                   ).broadcast_to([1, 64, 512])).then_inc(S.s_lb, 16)

            for b in range(N_PHI // 2):
                scat2(b)
                if b >= 1:
                    gath_bcast2(b - 1)
            gath_bcast2(N_PHI // 2 - 1)
            for t in range(NT):
                if t == 0:
                    sync.wait_ge(S.s_qkmm, 64)
                else:
                    sync.wait_ge(S.s_fin, t - 1)
                sync.dma_start(out=xin3[:, (t % 2) * 1024:(t % 2 + 1) * 1024],
                               in_=xd[t * 128:(t + 1) * 128, :]).then_inc(S.s_xr, 16)
                sync.wait_ge(S.s_fin, t + 1)
                sync.dma_start(out=yd[t * 128:(t + 1) * 128, :],
                               in_=outb[:, (t % 2) * 1024:(t % 2 + 1) * 1024]
                               ).then_inc(S.s_out if t % 2 == 0 else S.s_outB, 16)
            sync.wait_ge(S.s_outB, 128)
            if dbg:
                sync.wait_ge(S.s_out, 128)
                sync.dma_start(out=dbg_out["d_xnfeat"], in_=xnfeat[:]).then_inc(S.s_out, 16)
                sync.dma_start(out=dbg_out["d_kt"], in_=KT[:]).then_inc(S.s_out, 16)
                sync.dma_start(out=dbg_out["d_v"], in_=V[:]).then_inc(S.s_out, 16)
                sync.dma_start(out=dbg_out["d_ffnw"], in_=FFNW[:]).then_inc(S.s_out, 16)
                sync.dma_start(out=dbg_out["d_zfeat"], in_=zfeat[:]).then_inc(S.s_out, 16)
                sync.dma_start(out=dbg_out["d_lrb"], in_=Lrb[:]).then_inc(S.s_out, 16)
                sync.dma_start(out=dbg_out["d_wot"], in_=WOT[:]).then_inc(S.s_out, 16)
                sync.dma_start(out=dbg_out["d_xn0"], in_=xn0[:]).then_inc(S.s_out, 16)
                sync.dma_start(out=dbg_out["d_stats"], in_=stats[:]).then_inc(S.s_out, 16)
                sync.wait_ge(S.s_out, 128 + 9 * 16)
                sync.dma_start(out=dbg_out["d_vdbg"], in_=vdbg[:]).then_inc(S.s_c, 16)
                sync.dma_start(out=dbg_out["d_lrow"], in_=Lrow[:]).then_inc(S.s_c, 16)
                sync.dma_start(out=dbg_out["d_lcols2"], in_=Lcols2[:]).then_inc(S.s_c, 16)
                sync.dma_start(out=dbg_out["d_ubuf"], in_=ubuf[:]).then_inc(S.s_dbg, 16)
                sync.wait_ge(S.s_dve, 17)
                sync.dma_start(out=dbg_out["d_zps"], in_=ubuf[:]).then_inc(S.s_dbg, 16)
                sync.wait_ge(S.s_c, 96 + 3 * 16)
                sync.wait_ge(S.s_dbg, 32)
            else:
                sync.wait_ge(S.s_out, 128)

        @block.gpsimd
        def _(gp):
            gp.dma_start(out=g1sb[:, 0, :, :], in_=g1rd[:]).then_inc(S.s_c2, 16)
            gp.dma_start(out=g1sb[:, 1, :, :], in_=g1id[:]).then_inc(S.s_c2, 16)
            gp.dma_start(out=g2sb[:, 0, :, :],
                         in_=g2rd[:].rearrange("c e g -> e c g")).then_inc(S.s_c2, 16)
            gp.dma_start(out=g2sb[:, 1, :, :],
                         in_=g2id[:].rearrange("c e g -> e c g")).then_inc(S.s_c2, 16)
            gp.dma_start(out=brsb[:], in_=brd[:]).then_inc(S.s_c2, 16)
            gp.dma_start(out=bisb[:], in_=bid[:]).then_inc(S.s_c2, 16)
            gp.memset(V[:].rearrange("p (t h c) -> p t h c", h=NH, c=65)[:, :, :, 64:65],
                      1.0).then_inc(S.s_vones, 1)
            for i in range(24):
                wname = ("wq", "wk", "wv")[i // 8]
                r = i % 8
                if i >= 4:
                    gp.wait_ge(S.s_tpw, i - 3)
                gp.dma_start(out=wraw4[:, (i % 4) * 1024:(i % 4 + 1) * 1024],
                             in_=wdrams[wname][r * 128:(r + 1) * 128, :]
                             ).then_inc((S.s_w, S.s_wB, S.s_wC, S.s_wD)[i % 4], 16)
            for i in range(24, 32):
                k = i - 24
                gp.wait_ge(S.s_tpw, i - 1)
                gp.dma_start(out=wraw[:, (k % 2) * 1024:(k % 2 + 1) * 1024],
                             in_=wdrams["wo"][(i % 8) * 128:(i % 8 + 1) * 128, :]
                             ).then_inc((S.s_wE, S.s_wF)[k % 2], 16)
            for W in range(2):
                for cf in range(4):
                    for c in range(1, 4):
                        k = bk(W, cf, c)
                        gp.wait_ge(S.s_fbD, k + 1)
                        pn = pool_cnt_before(W, cf, c)
                        if pn > 0:
                            gp.wait_ge(S.s_fbP, pn)
                        accA = bscr[:, 0:512]
                        accB = bscr[:, 512:1024]
                        src = accA if c % 2 == 1 else accB
                        dst = accB if c % 2 == 1 else accA
                        tmp = tmpsb[:, (k % 2) * 512:(k % 2 + 1) * 512]
                        if c < 3:
                            gp.tensor_tensor(dst[:], src[:], tmp[:], OP.add
                                             ).then_inc(S.s_fbP, 1)
                        else:
                            wt_dst = FFNW[:, W * 2048 + cf * 512: W * 2048 + (cf + 1) * 512]
                            gp.tensor_tensor(wt_dst, src[:], tmp[:], OP.add
                                             ).then_inc(S.s_fbP, 1)
                            if W == 1:
                                gp.wait_ge(S.s_fbP, pn + 1)
                                gp.tensor_scalar_mul(
                                    FFNW[:, 2 * 2048 + cf * 512: 2 * 2048 + (cf + 1) * 512],
                                    wt_dst, -1.0).then_inc(S.s_ffnw, 1)
            for phi in range(N_PHI):
                p, Q = phi // 4, phi % 4
                gp.wait_ge(S.s_lb, 32 * (phi + 1))
                gp.wait_ge(S.s_zev, 2 * phi + 2)
                for h01 in (0, 1):
                    h = 2 * p + h01
                    rows = slice((h % 2) * 64, (h % 2) * 64 + 64)
                    zsl = zfeat[rows, p * T + 512 * Q: p * T + 512 * (Q + 1)]
                    gp.tensor_tensor(zsl, zsl, Lrb[rows, 512 * Q:512 * (Q + 1)],
                                     OP.mult).then_inc(S.s_norm, 1)

        @block.vector
        def _(vec):
            raw = [0]

            def w_evac(b):
                wname, wt = WTS[b // 8]
                r = b % 8
                vec.wait_ge(S.s_tpw, b + 1)
                src = (A2b if b % 2 == 0 else B2b)[:, 0:1024].rearrange(
                    "p (dc col) -> p dc col", col=128)
                vec.tensor_copy(wdest(wt, r), src).then_inc(S.s_wde, 1)

            for t in range(NT):
                xt = xin0[:, (t % 2) * 1024:(t % 2 + 1) * 1024]
                xnt = xn0[:, (t % 2) * 1024:(t % 2 + 1) * 1024]
                o = (t % 2) * 8
                s1 = stats[:, o + 0:o + 1]
                s2 = stats[:, o + 1:o + 2]
                m = stats[:, o + 2:o + 3]
                msq = stats[:, o + 3:o + 4]
                var = stats[:, o + 4:o + 5]
                rstd = stats[:, o + 5:o + 6]
                negmr = stats[:, o + 6:o + 7]
                vec.wait_ge(S.s_x if t % 2 == 0 else S.s_xB, 16 * (t // 2 + 1))
                if t >= 2:
                    vec.wait_ge(S.s_tpx, t - 1)
                    vec.wait_ge(S.s_rstd, t - 1)
                vec.reduce_sum(s1, xt, axis=AX.X).then_inc(S.s_raw, 1)
                raw[0] += 1
                vec.scalar_tensor_tensor(xnt, xt, 1.0, xt, OP.mult, OP.mult,
                                         accum_out=s2).then_inc(S.s_raw, 1)
                raw[0] += 1
                vec.wait_ge(S.s_raw, raw[0])
                vec.tensor_scalar_mul(m, s1, 1.0 / 1024.0).then_inc(S.s_raw, 1)
                raw[0] += 1
                vec.tensor_copy(xnt[:, 0:256], xt[:, 0:256])
                vec.wait_ge(S.s_raw, raw[0])
                vec.tensor_tensor(msq, m, m, OP.mult).then_inc(S.s_raw, 1)
                raw[0] += 1
                vec.tensor_copy(xnt[:, 256:512], xt[:, 256:512])
                vec.wait_ge(S.s_raw, raw[0])
                vec.scalar_tensor_tensor(var, s2, 1.0 / 1024.0, msq, OP.mult,
                                         OP.subtract).then_inc(S.s_raw, 1)
                raw[0] += 1
                vec.tensor_copy(xnt[:, 512:1024], xt[:, 512:1024])
                vec.wait_ge(S.s_raw, raw[0])
                vec.tensor_copy(vdbg[:, t % 2:t % 2 + 1], var
                                ).then_inc(S.s_stats, 1)
                vec.wait_ge(S.s_rstd, t + 1)
                vec.tensor_copy(vdbg[:, 4 + t % 2:5 + t % 2], rstd)
                vec.scalar_tensor_tensor(negmr, m, -1.0, rstd, OP.mult, OP.mult
                                         ).then_inc(S.s_raw, 1)
                raw[0] += 1
                vec.wait_ge(S.s_raw, raw[0])
                vec.tensor_copy(vdbg[:, 6 + t % 2:7 + t % 2], negmr)
                vec.tensor_scalar(xnt, xt, rstd, negmr, OP.mult, OP.add
                                  ).then_inc(S.s_ln, 1)
                for te_ in ([t - 2] if t >= 2 else []) + ([14, 15] if t == 15 else []):
                    vec.wait_ge(S.s_tpx, te_ + 1)
                    src = (A2b if te_ % 2 == 0 else B2b)[:, 1024:2048].rearrange(
                        "p (dc col) -> p dc col", col=128)
                    dst = xnfeat[:].rearrange("p (dc col) -> p dc col", col=T
                                              )[:, :, te_ * 128:(te_ + 1) * 128]
                    vec.tensor_copy(dst, src).then_inc(S.s_xne, 1)
                    if te_ < 16:
                        w_evac(te_)
                if t == 15:
                    for b_ in range(16, 24):
                        w_evac(b_)
            for g in range(16):
                tname, dst_t = ("Q", QT) if g < 8 else ("K", KT)
                f = g % 8
                for tb in range(4):
                    vec.wait_ge(S.s_qkmm, 4 * g + tb + 1)
                    grp = (A2, A2, B2, B2) if g % 2 == 0 else (Z, Z, Z, Z)
                    if g % 2 == 0:
                        src = grp[tb][:, (tb % 2) * 512:(tb % 2 + 1) * 512]
                    else:
                        src = Z[:, tb * 512:(tb + 1) * 512]
                    vec.tensor_copy(dst_t[:, f * T + tb * 512: f * T + (tb + 1) * 512],
                                    src).then_inc(S.s_qkev, 1)
            for j in range(NT):
                vec.wait_ge(S.s_vmm, j + 1)
                ps = A2 if j % 2 == 0 else B2
                src = ps[:].rearrange("p (h c) -> p h c", c=64)
                dst = V[:, j * 1040:(j + 1) * 1040].rearrange(
                    "p (h c) -> p h c", c=65)[:, :, 0:64]
                vec.tensor_copy(dst, src).then_inc(S.s_vev, 1)
            for b in range(24, 32):
                wname, wt = WTS[b // 8]
                r = b % 8
                vec.wait_ge(S.s_tpw, b + 1)
                src = (A2b if b % 2 == 0 else B2b)[:, 0:1024].rearrange(
                    "p (dc col) -> p dc col", col=128)
                vec.tensor_copy(wdest(wt, r), src).then_inc(S.s_wde, 1)
            vec.wait_ge(S.s_c, 96)
            vec.wait_ge(S.s_c2, 96)
            for W in range(2):
                if W == 1:
                    vec.wait_ge(S.s_fbC, 16)
                    vec.wait_ge(S.s_fbA, 16)
                for c in range(4):
                    vec.tensor_copy(
                        g1exp[:, c, :].rearrange("b (d g) -> b d g", g=16),
                        g1sb[:, W, c, :].unsqueeze(2).broadcast_to([32, 32, 16]))
                    ins = vec.tensor_copy(
                        g2exp[:, c, :].rearrange("e (d g) -> e d g", g=16),
                        g2sb[:, W, c, :].unsqueeze(1).broadcast_to([16, 8, 16]))
                    if c == 3:
                        ins.then_inc(S.s_fbB, 1)
                for cf in range(4):
                    for c in range(4):
                        k = bk(W, cf, c)
                        vec.wait_ge(S.s_fbA, k + 1)
                        if k >= 1:
                            vec.wait_ge(S.s_fbD, k)
                        vec.tensor_copy(t2sb[:], Z[:, (k % 4) * 512:(k % 4 + 1) * 512]
                                        ).then_inc(S.s_fbB, 1)
                        vec.wait_ge(S.s_fbB, (k + 2) if k < 16 else (k + 3))
                        vec.wait_ge(S.s_fbC, k + 1)
                        ps1 = A2[:, (k % 2) * 512:(k % 2 + 1) * 512]
                        if c == 0:
                            dst = bscr[:, 0:512]
                        else:
                            dst = tmpsb[:, (k % 2) * 512:(k % 2 + 1) * 512]
                        pcb = pool_cnt_before(W, cf, c)
                        if pcb > 0:
                            vec.wait_ge(S.s_fbP, pcb)
                        vec.tensor_tensor(dst, t2sb[:], ps1, OP.mult
                                          ).then_inc(S.s_fbD, 1)
            def lrow_slot(phi):
                sl = phi % 4
                base = Lrow if sl < 2 else LrowB
                co = 512 * (sl % 2)
                return base, co

            def evac_phi(phi):
                p, Q = phi // 4, phi % 4
                lrb_, co = lrow_slot(phi)
                zo = 1024 * (phi % 2)
                si_last = STEP_IDX[(p, Q, 4 * Q + 3)]
                vec.wait_ge(S.s_zu, 2 * (si_last + 1))
                if phi >= 4:
                    vec.wait_ge(S.s_lb, 32 * (phi - 3))
                vec.tensor_copy(lrb_[0:1, co:co + 512], Z[64:65, zo:zo + 512]
                                ).then_inc(S.s_raw, 1)
                raw[0] += 1
                vec.wait_ge(S.s_raw, raw[0])
                vec.tensor_copy(lrb_[32:33, co:co + 512], Z[64:65, zo + 512:zo + 1024]
                                ).then_inc(S.s_raw, 1)
                raw[0] += 1
                vec.wait_ge(S.s_raw, raw[0])
                vec.tensor_copy(vdbg[0:1, 46:47], lrb_[32:33, co + 511:co + 512]
                                ).then_inc(S.s_zev, 1)
                for h01 in (0, 1):
                    h = 2 * p + h01
                    rows = slice((h % 2) * 64, (h % 2) * 64 + 64)
                    zt = Z[0:64, zo + h01 * 512: zo + (h01 + 1) * 512]
                    vec.tensor_copy(zfeat[rows, p * T + 512 * Q: p * T + 512 * (Q + 1)],
                                    zt).then_inc(S.s_raw, 1)
                    raw[0] += 1
                    vec.wait_ge(S.s_raw, raw[0])
                vec.tensor_copy(vdbg[0:64, 44:45],
                                zfeat[0:64, p * T + 512 * Q + 511: p * T + 512 * Q + 512]
                                ).then_inc(S.s_zev, 1)

            def recip2(b):
                lc = Lcols if b % 2 == 0 else LcolsB
                lc2 = Lcols2 if b % 2 == 0 else Lcols2B
                vec.wait_ge(S.s_lsc, 16 * (b + 1))
                if b >= 2:
                    vec.wait_ge(S.s_lga, 16 * (b - 1))
                vec.reciprocal(lc2[:], lc[:]).then_inc(S.s_raw, 1)
                raw[0] += 1
                vec.wait_ge(S.s_raw, raw[0])
                vec.tensor_copy(vdbg[:, 47:48], lc2[:, 15:16]
                                ).then_inc(S.s_lrec, 1)

            for phi in range(N_PHI):
                evac_phi(phi)
                if phi >= 3 and phi % 2 == 1:
                    recip2((phi - 3) // 2)
            recip2(N_PHI // 2 - 1)
            for t in range(NT):
                vec.wait_ge(S.s_p3mm, t + 1)
                vec.wait_ge(S.s_xr, 16 * (t + 1))
                if t >= 2:
                    vec.wait_ge(S.s_out if t % 2 == 0 else S.s_outB,
                                16 * (t // 2))
                if t == 0:
                    vec.wait_ge(S.s_lb, 32 * N_PHI)
                ps = A2 if t % 2 == 0 else B2
                vec.tensor_tensor(outb[:, (t % 2) * 1024:(t % 2 + 1) * 1024],
                                  xin3[:, (t % 2) * 1024:(t % 2 + 1) * 1024],
                                  ps[:], OP.add).then_inc(S.s_raw, 1)
                raw[0] += 1
                vec.wait_ge(S.s_raw, raw[0])
                vec.tensor_copy(vdbg[:, 45:46],
                                outb[:, (t % 2) * 1024 + 1023:(t % 2) * 1024 + 1024]
                                ).then_inc(S.s_fin, 1)
            if dbg:
                vec.wait_ge(S.s_dbg, 16)
                vec.tensor_copy(ubuf[0:65, 0:2048], Z[0:65, :]).then_inc(S.s_dve, 1)

        @block.scalar
        def _(sc):
            rawA = [0]
            AB4r = AB4[:].rearrange("p (hh q) -> p hh q", q=512)
            sc.wait_ge(S.s_c, 96)
            sc.wait_ge(S.s_c2, 96)
            for t in range(NT):
                o = (t % 2) * 8
                var = stats[:, o + 4:o + 5]
                rstd = stats[:, o + 5:o + 6]
                lnv = stats[:, o + 7:o + 8]
                sc.wait_ge(S.s_stats, t + 1)
                sc.activation(lnv, var, AF.Ln, bias=actcsb[:, 0:1], scale=1.0
                              ).then_inc(S.s_rawA, 1)
                rawA[0] += 1
                sc.wait_ge(S.s_rawA, rawA[0])
                sc.activation(rstd, lnv, AF.Exp, scale=-0.5).then_inc(S.s_rawA, 1)
                rawA[0] += 1
                sc.wait_ge(S.s_rawA, rawA[0])
                sc.copy(vdbg[:, 42:43], rstd).then_inc(S.s_rstd, 1)
            last_z_cnt_a = {}
            for si in range(N_STEPS):
                p, Q, j = STEPS[si]
                qs, qe = qrange2(Q, j)
                L = qe - qs
                pg = si % 2
                sc.wait_ge(S.s_st, si + 1)
                for slot_h in (0, 1):
                    key = (j % 4) * 2 + slot_h
                    if key in last_z_cnt_a:
                        sc.wait_ge(S.s_zu, last_z_cnt_a[key])
                    last_z_cnt_a[key] = 2 * si + 1 + slot_h
                src = AB4r[:, pg:pg + 3:2, 0:L]
                dst = ubig[:, (j % 4) * 1024:(j % 4) * 1024 + 1024].rearrange(
                    "p (hh q) -> p hh q", q=512)[:, :, 0:L]
                sc.activation(dst, src, AF.Exp, bias=actcsb[:, 1:2],
                              scale=0.125).then_inc(S.s_exp, 1)

        @block.tensor
        def _(te):
            te.wait_ge(S.s_c, 96)
            te.wait_ge(S.s_c2, 96)

            def xn_tr(t):
                te.wait_ge(S.s_ln, t + 1)
                if t >= 2:
                    te.wait_ge(S.s_xne, t - 1)
                dstp = (A2b if t % 2 == 0 else B2b)
                for dc in range(8):
                    ins = te.transpose(
                        dstp[:, 1024 + dc * 128:1024 + (dc + 1) * 128],
                        xn0[:, (t % 2) * 1024 + dc * 128:(t % 2) * 1024 + (dc + 1) * 128],
                        identsb[:])
                    if dc == 7:
                        ins.then_inc(S.s_tpx, 1)

            def w_tr(b):
                te.wait_ge((S.s_w, S.s_wB, S.s_wC, S.s_wD)[b % 4], 16 * (b // 4 + 1))
                if b >= 2:
                    te.wait_ge(S.s_wde, b - 1)
                dstp = (A2b if b % 2 == 0 else B2b)
                for dc in range(8):
                    ins = te.transpose(
                        dstp[:, dc * 128:(dc + 1) * 128],
                        wraw4[:, (b % 4) * 1024 + dc * 128:(b % 4) * 1024 + (dc + 1) * 128],
                        identsb[:])
                    if dc == 7:
                        ins.then_inc(S.s_tpw, 1)

            for t in range(NT):
                xn_tr(t)
                if t < 16:
                    w_tr(t)
            for b in range(16, 24):
                w_tr(b)
            te.wait_ge(S.s_xne, 16)
            te.wait_ge(S.s_wde, 24)
            for g in range(16):
                wt = WQT if g < 8 else WKT
                f = g % 8
                if g >= 2:
                    te.wait_ge(S.s_qkev, 4 * (g - 1))
                for d in range(8):
                    for tb in range(4):
                        if g % 2 == 0:
                            ps = (A2 if tb < 2 else B2)[:, (tb % 2) * 512:(tb % 2 + 1) * 512]
                        else:
                            ps = Z[:, tb * 512:(tb + 1) * 512]
                        ins = te.matmul(ps,
                                        wt[:, d * 1024 + f * 128:d * 1024 + (f + 1) * 128],
                                        xnf(d)[:, tb * 512:(tb + 1) * 512],
                                        start=(d == 0), stop=(d == 7))
                        if d == 7:
                            ins.then_inc(S.s_qkmm, 1)
            te.wait_ge(S.s_wde, 24)
            te.wait_ge(S.s_qkev, 64)
            for j in range(NT):
                if j >= 2:
                    te.wait_ge(S.s_vev, j - 1)
                ps = A2 if j % 2 == 0 else B2
                for d in range(8):
                    for nb in range(2):
                        ins = te.matmul(ps[:, nb * 512:(nb + 1) * 512],
                                        xnf(d)[:, j * 128:(j + 1) * 128],
                                        WVT[:, d * 1024 + nb * 512:d * 1024 + (nb + 1) * 512],
                                        start=(d == 0), stop=(d == 7))
                        if d == 7 and nb == 1:
                            ins.then_inc(S.s_vmm, 1)
            for b in range(24, 32):
                k = b - 24
                te.wait_ge((S.s_wE, S.s_wF)[k % 2], 16 * (k // 2 + 1))
                te.wait_ge(S.s_wde, b - 1)
                if b == 24:
                    te.wait_ge(S.s_vev, 15)
                if b == 25:
                    te.wait_ge(S.s_vev, 16)
                dstp = (A2b if b % 2 == 0 else B2b)
                for dc in range(8):
                    ins = te.transpose(
                        dstp[:, dc * 128:(dc + 1) * 128],
                        wraw[:, (k % 2) * 1024 + dc * 128:(k % 2) * 1024 + (dc + 1) * 128],
                        identsb[:])
                    if dc == 7:
                        ins.then_inc(S.s_tpw, 1)
            te.wait_ge(S.s_wde, 32)
            for W in range(2):
                te.wait_ge(S.s_fbB, W * 17 + 1)
                for cf in range(4):
                    for c in range(4):
                        k = bk(W, cf, c)
                        if k >= 4:
                            te.wait_ge(S.s_fbB, (k - 4) + 2 + (1 if k - 4 >= 16 else 0))
                        te.matmul(Z[:, (k % 4) * 512:(k % 4 + 1) * 512],
                                  g2exp[:, c, :], rcolesb[:], start=True, stop=True
                                  ).then_inc(S.s_fbA, 1)
                        if k >= 2:
                            te.wait_ge(S.s_fbD, k - 1)
                        te.matmul(A2[:, (k % 2) * 512:(k % 2 + 1) * 512],
                                  g1exp[:, c, cf * 128:(cf + 1) * 128],
                                  rcolsb[:], start=True, stop=True
                                  ).then_inc(S.s_fbC, 1)
            te.wait_ge(S.s_vones, 1)
            te.wait_ge(S.s_vev, 16)
            AB4r = AB4[:].rearrange("p (hh q) -> p hh q", q=512)

            def emit_st_pair(si):
                p, Q, j = STEPS[si]
                diag = step_diag(p, Q, j)
                pg = si % 2
                if si >= 2:
                    te.wait_ge(S.s_exp, si - 1)
                if si == 0:
                    te.wait_ge(S.s_fbD, 32)
                qs, qe = qrange2(Q, j)
                L = qe - qs
                lhs0 = KT[0:64, p * T + j * 128: p * T + (j + 1) * 128]
                lhs1 = KT[64:128, p * T + j * 128: p * T + (j + 1) * 128]
                te.matmul(AB4[:, pg * 512: pg * 512 + L], lhs0,
                          QT[0:64, p * T + qs: p * T + qe],
                          start=True, stop=not diag, tile_position=(0, 0))
                ins = te.matmul(AB4[:, 1024 + pg * 512: 1024 + pg * 512 + L], lhs1,
                                QT[64:128, p * T + qs: p * T + qe],
                                start=True, stop=not diag, tile_position=(64, 0))
                if diag:
                    te.matmul(AB4[:, pg * 512: pg * 512 + 128], maskTsb[:], identsb[:],
                              start=False, stop=True)
                    ins = te.matmul(AB4[:, 1024 + pg * 512: 1024 + pg * 512 + 128],
                                    maskTsb[:], identsb[:], start=False, stop=True)
                ins.then_inc(S.s_st, 1)

            def emit_z_pair(si):
                p, Q, j = STEPS[si]
                phi = 4 * p + Q
                qs, qe = qrange2(Q, j)
                L = qe - qs
                te.wait_ge(S.s_exp, si + 1)
                zo = 1024 * (phi % 2)
                for h01 in (0, 1):
                    if j == 0 and h01 == 0 and phi <= 1:
                        te.wait_ge(S.s_fbB, 34)
                    if j == 0 and h01 == 0 and phi >= 2:
                        te.wait_ge(S.s_zev, 2 * (phi - 1))
                    h = 2 * p + h01
                    lhs = V[:, j * 1040 + h * 65: j * 1040 + (h + 1) * 65]
                    uoff = ((j % 4) * 2 + h01) * 512
                    lloc = qs - 512 * Q
                    te.matmul(Z[0:65, zo + h01 * 512 + lloc: zo + h01 * 512 + lloc + L],
                              lhs, ubig[:, uoff: uoff + L],
                              start=(j == 0), stop=(j == 4 * Q + 3)
                              ).then_inc(S.s_zu, 1)

            emit_st_pair(0)
            emit_st_pair(1)
            for si in range(N_STEPS):
                if si + 2 < N_STEPS:
                    emit_st_pair(si + 2)
                emit_z_pair(si)
            te.wait_ge(S.s_exp, N_STEPS)
            te.wait_ge(S.s_fbP, 24)
            te.wait_ge(S.s_ffnw, 4)
            for t in range(NT):
                if t >= 2:
                    te.wait_ge(S.s_fin, t - 1)
                ps = A2 if t % 2 == 0 else B2
                for c in range(8):
                    lhs = xnf(c)[:, t * 128:(t + 1) * 128]
                    if c < 4:
                        rhsR = FFNW[:, 0 * 2048 + c * 512:0 * 2048 + (c + 1) * 512]
                        rhsI = FFNW[:, 1 * 2048 + c * 512:1 * 2048 + (c + 1) * 512]
                    else:
                        rhsR = FFNW[:, 2 * 2048 + (c - 4) * 512:2 * 2048 + (c - 3) * 512]
                        rhsI = FFNW[:, 0 * 2048 + (c - 4) * 512:0 * 2048 + (c - 3) * 512]
                    te.matmul(ps[:, 0:512], lhs, rhsR, start=(c == 0), stop=False)
                    te.matmul(ps[:, 512:1024], lhs, rhsI, start=(c == 0), stop=False)
                for fc in range(8):
                    if t == 0:
                        te.wait_ge(S.s_norm, 8 * (fc + 1))
                    lhs = zfeat[:, fc * T + t * 128: fc * T + (t + 1) * 128]
                    for nb in range(2):
                        te.matmul(ps[:, nb * 512:(nb + 1) * 512], lhs,
                                  WOT[:, fc * 1024 + nb * 512:fc * 1024 + (nb + 1) * 512],
                                  start=False, stop=False)
                te.matmul(ps[:, 0:512], onessb[:], brsb[:], start=False, stop=True)
                te.matmul(ps[:, 512:1024], onessb[:], bisb[:], start=False, stop=True
                          ).then_inc(S.s_p3mm, 1)

    return nc


def kernel(L0_attn_x, L0_attn_W_K, L0_attn_W_Q, L0_attn_W_V, L0_attn_W_O,
           L0_ffn_G1_R, L0_ffn_G2_R, L0_ffn_G1_I, L0_ffn_G2_I,
           L0_ffn_bias_real, L0_ffn_bias_imag):
    x = np.ascontiguousarray(np.asarray(L0_attn_x, dtype=np.float32))
    wq = np.ascontiguousarray(np.asarray(L0_attn_W_Q, np.float32).reshape(D, D))
    wk = np.ascontiguousarray(np.asarray(L0_attn_W_K, np.float32).reshape(D, D))
    wv = np.ascontiguousarray(np.asarray(L0_attn_W_V, np.float32).reshape(D, D))
    wo = np.ascontiguousarray(np.asarray(L0_attn_W_O, np.float32))
    g1r = np.ascontiguousarray(np.asarray(L0_ffn_G1_R, np.float32).reshape(32, 4, 32))
    g1i = np.ascontiguousarray(np.asarray(L0_ffn_G1_I, np.float32).reshape(32, 4, 32))
    g2r = np.ascontiguousarray(np.asarray(L0_ffn_G2_R, np.float32).reshape(4, 16, 16))
    g2i = np.ascontiguousarray(np.asarray(L0_ffn_G2_I, np.float32).reshape(4, 16, 16))
    br = np.ascontiguousarray(np.asarray(L0_ffn_bias_real, np.float32).reshape(1, 512))
    bi = np.ascontiguousarray(np.asarray(L0_ffn_bias_imag, np.float32).reshape(1, 512))

    nc = build_nc()

    in_maps = []
    for b in range(8):
        in_maps.append(dict(x=x[b], wq=wq, wk=wk, wv=wv, wo=wo,
                            g1r=g1r, g1i=g1i, g2r=g2r, g2i=g2i, br=br, bi=bi))
    res = run_bass_kernel_spmd(nc, in_maps, core_ids=list(range(8)))
    out = np.stack([np.asarray(res.results[b]["y"]) for b in range(8)], axis=0)
    return out.astype(np.float32)


# version 27
# speedup vs baseline: 1.0515x; 1.0515x over previous
import numpy as np
import ml_dtypes
import concourse.bass as bass
import concourse.mybir as mybir
from concourse.bass_utils import run_bass_kernel_spmd

F32 = mybir.dt.float32
BF16 = mybir.dt.bfloat16
AF = mybir.ActivationFunctionType
AX = mybir.AxisListType
OP = mybir.AluOpType

T = 2048
D = 1024
NT = 16
NH = 16
DH = 64
EXP_BIAS = -12.0

BF = ml_dtypes.bfloat16


def subs_of(j):
    return 2 if j < 8 else 1


def ss_of(h, j, sub):
    return 24 * h + (2 * j if j < 8 else 8 + j) + sub


TOTAL_SS = 384

STEPS = []
for _p in range(8):
    for _Q in range(4):
        for _j in range(4 * _Q + 4):
            STEPS.append((_p, _Q, _j))
N_STEPS = len(STEPS)
STEP_IDX = {k: i for i, k in enumerate(STEPS)}
N_PHI = 32
ACT_TAIL = 28


def step_diag(p, Q, j):
    return j // 4 == Q


def qrange2(Q, j):
    qs = max(128 * j, 512 * Q)
    qe = 512 * (Q + 1)
    return qs, qe


def qrange(j, sub):
    qlo = 128 * j
    qs = qlo + 1024 * sub
    qe = min(2048, qs + 1024)
    return qs, qe


def build_nc(dbg=False):
    nc = bass.Bass("TRN2", target_bir_lowering=False, dynamic_dma_scratch_size=16384)

    xd = nc.dram_tensor("x", [T, D], F32, kind="ExternalInput").ap()
    wqd = nc.dram_tensor("wq", [D, D], F32, kind="ExternalInput").ap()
    wkd = nc.dram_tensor("wk", [D, D], F32, kind="ExternalInput").ap()
    wvd = nc.dram_tensor("wv", [D, D], F32, kind="ExternalInput").ap()
    wod = nc.dram_tensor("wo", [D, D], F32, kind="ExternalInput").ap()
    g1rd = nc.dram_tensor("g1r", [32, 4, 32], F32, kind="ExternalInput").ap()
    g1id = nc.dram_tensor("g1i", [32, 4, 32], F32, kind="ExternalInput").ap()
    g2rd = nc.dram_tensor("g2r", [4, 16, 16], F32, kind="ExternalInput").ap()
    g2id = nc.dram_tensor("g2i", [4, 16, 16], F32, kind="ExternalInput").ap()
    brd = nc.dram_tensor("br", [1, 512], F32, kind="ExternalInput").ap()
    bid = nc.dram_tensor("bi", [1, 512], F32, kind="ExternalInput").ap()
    yd = nc.dram_tensor("y", [T, D], F32, kind="ExternalOutput").ap()
    if dbg:
        dbg_out = {
            "d_xnfeat": nc.dram_tensor("d_xnfeat", [128, 8 * T], BF16, kind="ExternalOutput").ap(),
            "d_kt": nc.dram_tensor("d_kt", [128, 8 * T], BF16, kind="ExternalOutput").ap(),
            "d_v": nc.dram_tensor("d_v", [128, NT * NH * 65], BF16, kind="ExternalOutput").ap(),
            "d_ffnw": nc.dram_tensor("d_ffnw", [128, 3 * 2048], BF16, kind="ExternalOutput").ap(),
            "d_zfeat": nc.dram_tensor("d_zfeat", [128, 8 * T], BF16, kind="ExternalOutput").ap(),
            "d_lrb": nc.dram_tensor("d_lrb", [128, 2048], F32, kind="ExternalOutput").ap(),
            "d_wot": nc.dram_tensor("d_wot", [128, 8192], BF16, kind="ExternalOutput").ap(),
            "d_xin0": nc.dram_tensor("d_xin0", [128, 2048], F32, kind="ExternalOutput").ap(),
            "d_xn0": nc.dram_tensor("d_xn0", [128, 2048], BF16, kind="ExternalOutput").ap(),
            "d_stats": nc.dram_tensor("d_stats", [128, 16], F32, kind="ExternalOutput").ap(),
            "d_xnf_e": nc.dram_tensor("d_xnf_e", [128, 8 * T], BF16, kind="ExternalOutput").ap(),
            "d_kt_e": nc.dram_tensor("d_kt_e", [128, 8 * T], BF16, kind="ExternalOutput").ap(),
            "d_vdbg": nc.dram_tensor("d_vdbg", [128, 48], F32, kind="ExternalOutput").ap(),
            "d_lrow": nc.dram_tensor("d_lrow", [33, 1024], F32, kind="ExternalOutput").ap(),
            "d_lcols2": nc.dram_tensor("d_lcols2", [128, 16], F32, kind="ExternalOutput").ap(),
            "d_ubuf": nc.dram_tensor("d_ubuf", [128, 2048], BF16, kind="ExternalOutput").ap(),
            "d_zps": nc.dram_tensor("d_zps", [128, 2048], BF16, kind="ExternalOutput").ap(),
        }

    identd = nc.inline_tensor(np.eye(128, dtype=BF), "identc").ap()
    maskTd = nc.inline_tensor(
        np.triu(np.full((128, 128), -1e10, np.float32), k=1).astype(BF), "maskTc").ap()
    onesd = nc.inline_tensor(np.ones((1, 128), dtype=BF), "onesc").ap()
    rcol_np = np.zeros((32, 512), np.float32)
    rcol_np[np.arange(512) // 16, np.arange(512)] = 1.0
    rcold = nc.inline_tensor(rcol_np.astype(BF), "rcolc").ap()
    rcole_np = np.zeros((16, 512), np.float32)
    rcole_np[np.arange(512) % 16, np.arange(512)] = 1.0
    rcoled = nc.inline_tensor(rcole_np.astype(BF), "rcolec").ap()
    actc_np = np.tile(np.array([[1e-5, EXP_BIAS]], np.float32), (128, 1))
    actcd = nc.inline_tensor(actc_np, "actcc").ap()

    xnfeat = nc.alloc_sbuf_tensor("xnfeat", [128, 8 * T], BF16)
    QT = nc.alloc_sbuf_tensor("QT", [128, 8 * T], BF16)
    KT = nc.alloc_sbuf_tensor("KT", [128, 8 * T], BF16)
    V = nc.alloc_sbuf_tensor("V", [128, NT * NH * 65], BF16)
    WQT = nc.alloc_sbuf_tensor("WQT", [128, 8192], BF16)
    WKT = nc.alloc_sbuf_tensor("WKT", [128, 8192], BF16)
    WVT = nc.alloc_sbuf_tensor("WVT", [128, 8192], BF16)
    FFNW = nc.alloc_sbuf_tensor("FFNW", [128, 3 * 2048], BF16)
    ubuf = nc.alloc_sbuf_tensor("ubuf", [128, 2 * 1024], BF16)
    wraw = nc.alloc_sbuf_tensor("wraw", [128, 2 * 1024], BF16)
    bscr = nc.alloc_sbuf_tensor("bscr", [128, 1024], F32)
    identsb = nc.alloc_sbuf_tensor("identsb", [128, 128], BF16)
    maskTsb = nc.alloc_sbuf_tensor("maskTsb", [128, 128], BF16)
    onessb = nc.alloc_sbuf_tensor("onessb", [1, 128], BF16)
    rcolsb = nc.alloc_sbuf_tensor("rcolsb", [32, 512], BF16)
    rcolesb = nc.alloc_sbuf_tensor("rcolesb", [16, 512], BF16)
    actcsb = nc.alloc_sbuf_tensor("actcsb", [128, 2], F32)
    g1sb = nc.alloc_sbuf_tensor("g1sb", [32, 2, 4, 32], BF16)
    g2sb = nc.alloc_sbuf_tensor("g2sb", [16, 2, 4, 16], BF16)
    brsb = nc.alloc_sbuf_tensor("brsb", [1, 512], BF16)
    bisb = nc.alloc_sbuf_tensor("bisb", [1, 512], BF16)
    stats = nc.alloc_sbuf_tensor("stats", [128, 16], F32)
    g2exp = nc.alloc_sbuf_tensor("g2exp", [16, 4, 128], BF16)
    vdbg = nc.alloc_sbuf_tensor("vdbg", [128, 48], F32)

    zfeat = nc.alloc_sbuf_tensor_at("zfeat", [128, 8 * T], BF16,
                                    offset=nc.lookup_mloc(QT).addr)
    WOT = nc.alloc_sbuf_tensor_at("WOT", [128, 8192], BF16,
                                  offset=nc.lookup_mloc(WVT).addr)
    wqt_addr = nc.lookup_mloc(WQT).addr
    Lrow = nc.alloc_sbuf_tensor_at("Lrow", [33, 1024], F32, offset=wqt_addr)
    ubig = nc.alloc_sbuf_tensor_at("ubig", [128, 2 * 2048], BF16, offset=wqt_addr + 8192)
    outb = nc.alloc_sbuf_tensor_at("outb", [128, 2 * 1024], F32, offset=wqt_addr)
    wkt_addr = nc.lookup_mloc(WKT).addr
    Lrb = nc.alloc_sbuf_tensor_at("Lrb", [128, 2048], F32, offset=wkt_addr + 8192)
    t2sb = nc.alloc_sbuf_tensor_at("t2sb", [128, 512], F32, offset=wkt_addr + 8192)
    xin3 = nc.alloc_sbuf_tensor_at("xin3", [128, 2 * 1024], F32, offset=wkt_addr)
    Lcols = nc.alloc_sbuf_tensor("Lcols", [128, 16], F32)
    Lcols2 = nc.alloc_sbuf_tensor("Lcols2", [128, 16], F32)
    xin0 = nc.alloc_sbuf_tensor_at("xin0", [128, 2 * 1024], F32,
                                   offset=nc.lookup_mloc(FFNW).addr)
    xn0 = nc.alloc_sbuf_tensor_at("xn0", [128, 2 * 1024], BF16,
                                  offset=nc.lookup_mloc(QT).addr)
    wraw4 = nc.alloc_sbuf_tensor_at("wraw4", [128, 4 * 1024], BF16,
                                    offset=nc.lookup_mloc(KT).addr)
    tmpsb = nc.alloc_sbuf_tensor_at("tmpsb", [128, 1024], F32,
                                    offset=nc.lookup_mloc(wraw).addr)
    g1exp = nc.alloc_sbuf_tensor_at("g1exp", [32, 4, 512], BF16,
                                    offset=nc.lookup_mloc(ubuf).addr)
    LrowB = nc.alloc_sbuf_tensor_at("LrowB", [33, 1024], F32,
                                    offset=nc.lookup_mloc(bscr).addr)

    AB4 = nc.alloc_psum_tensor("AB4", [128, 2048], F32)
    A2 = AB4[:, 0:1024]
    B2 = AB4[:, 1024:2048]
    Z = nc.alloc_psum_tensor("Z", [128, 2048], F32)
    A2b = A2.bitcast(BF16)
    B2b = B2.bitcast(BF16)

    sems = {}
    for name in ("s_x s_xB s_c s_c2 s_w s_wB s_wC s_wD s_wE s_wF s_lb s_xr s_out s_outB s_stats s_rstd s_ln s_tpw s_tpx "
                 "s_wde s_xne s_qkmm s_qkev s_vmm s_vev s_vones s_fbA s_fbB "
                 "s_fbC s_fbD s_fbP s_ffnw s_st s_exp s_zu s_zev s_lsc s_lrec s_lga s_norm "
                 "s_p3mm s_fin s_dve s_dbg s_raw s_rawA s_msk").split():
        sems[name] = nc.alloc_semaphore(name)
    S = type("S", (), sems)

    WTS = [("wq", WQT), ("wk", WKT), ("wv", WVT), ("wo", WOT)]
    wdrams = {"wq": wqd, "wk": wkd, "wv": wvd, "wo": wod}

    def wdest(wt, r):
        return wt[:].rearrange("p (dc col) -> p dc col", col=1024)[:, :, r * 128:(r + 1) * 128]

    def xnf(d):
        return xnfeat[:, d * T:(d + 1) * T]

    def bk(W, cf, c):
        return (W * 4 + cf) * 4 + c

    def pool_cnt_before(W, cf, c):
        full = (W * 4 + cf)
        return full * 3 + max(0, c - 1)

    st_halves = {}

    def st_slices(j, sub):
        qs, qe = qrange(j, sub)
        L = qe - qs
        out = []
        s = 0
        while s < L:
            e = min(L, s + 512)
            out.append((s, e))
            s = e
        return out

    def z_slices(j, sub):
        qs, qe = qrange(j, sub)
        out = []
        s = qs
        while s < qe:
            e = min(qe, (s // 512 + 1) * 512)
            out.append((s, e))
            s = e
        return out

    with nc.Block() as block:

        @block.sync
        def _(sync):
            sync.dma_start(out=identsb[:], in_=identd[:]).then_inc(S.s_c, 16)
            sync.dma_start(out=maskTsb[:], in_=maskTd[:]).then_inc(S.s_c, 16)
            sync.dma_start(out=onessb[:], in_=onesd[:]).then_inc(S.s_c, 16)
            sync.dma_start(out=rcolsb[:], in_=rcold[:]).then_inc(S.s_c, 16)
            sync.dma_start(out=rcolesb[:], in_=rcoled[:]).then_inc(S.s_c, 16)
            sync.dma_start(out=actcsb[:], in_=actcd[:]).then_inc(S.s_c, 16)
            for t in range(NT):
                if t >= 2:
                    sync.wait_ge(S.s_ln, t - 1)
                sync.dma_start(out=xin0[:, (t % 2) * 1024:(t % 2 + 1) * 1024],
                               in_=xd[t * 128:(t + 1) * 128, :]
                               ).then_inc(S.s_x if t % 2 == 0 else S.s_xB, 16)
            if dbg:
                sync.wait_ge(S.s_vmm, 16)
                sync.dma_start(out=dbg_out["d_xnf_e"], in_=xnfeat[:]).then_inc(S.s_c, 16)
                sync.dma_start(out=dbg_out["d_kt_e"], in_=KT[:]).then_inc(S.s_c, 16)
            def lrview(phi):
                sl = phi % 4
                base = Lrow if sl < 2 else LrowB
                co = 512 * (sl % 2)
                return base[0:33:32, co:co + 512].rearrange("o (p k) -> o p k", k=16)

            def lcols(phi, t2=False):
                ro = 64 * (phi % 2)
                return (Lcols2 if t2 else Lcols)[ro:ro + 64, :]

            def scat(phi):
                sync.wait_ge(S.s_zev, 2 * phi + 1)
                if phi >= 2:
                    sync.wait_ge(S.s_lrec, phi - 1)
                sync.dma_start(out=lcols(phi), in_=lrview(phi)).then_inc(S.s_lsc, 16)

            def gath_bcast(phi):
                p, Q = phi // 4, phi % 4
                base, co = (Lrow if phi % 4 < 2 else LrowB), 512 * (phi % 2)
                sync.wait_ge(S.s_lrec, phi + 1)
                if phi < ACT_TAIL:
                    sync.dma_start(out=lrview(phi), in_=lcols(phi, t2=True)
                                   ).then_inc(S.s_lga, 16)
                    sync.wait_ge(S.s_lga, 16 * (phi + 1))
                if phi >= 4:
                    sync.wait_ge(S.s_norm, 2 * (phi - 3))
                sync.dma_start(out=Lrb[0:64, 512 * Q:512 * (Q + 1)],
                               in_=base[0:1, co:co + 512].unsqueeze(1
                               ).broadcast_to([1, 64, 512])).then_inc(S.s_lb, 16)
                sync.dma_start(out=Lrb[64:128, 512 * Q:512 * (Q + 1)],
                               in_=base[32:33, co:co + 512].unsqueeze(1
                               ).broadcast_to([1, 64, 512])).then_inc(S.s_lb, 16)

            for phi in range(N_PHI):
                if phi < ACT_TAIL:
                    scat(phi)
                if phi >= 2:
                    gath_bcast(phi - 2)
            gath_bcast(N_PHI - 2)
            gath_bcast(N_PHI - 1)
            for t in range(NT):
                if t == 0:
                    sync.wait_ge(S.s_qkmm, 64)
                else:
                    sync.wait_ge(S.s_fin, t - 1)
                sync.dma_start(out=xin3[:, (t % 2) * 1024:(t % 2 + 1) * 1024],
                               in_=xd[t * 128:(t + 1) * 128, :]).then_inc(S.s_xr, 16)
                sync.wait_ge(S.s_fin, t + 1)
                sync.dma_start(out=yd[t * 128:(t + 1) * 128, :],
                               in_=outb[:, (t % 2) * 1024:(t % 2 + 1) * 1024]
                               ).then_inc(S.s_out if t % 2 == 0 else S.s_outB, 16)
            sync.wait_ge(S.s_outB, 128)
            if dbg:
                sync.wait_ge(S.s_out, 128)
                sync.dma_start(out=dbg_out["d_xnfeat"], in_=xnfeat[:]).then_inc(S.s_out, 16)
                sync.dma_start(out=dbg_out["d_kt"], in_=KT[:]).then_inc(S.s_out, 16)
                sync.dma_start(out=dbg_out["d_v"], in_=V[:]).then_inc(S.s_out, 16)
                sync.dma_start(out=dbg_out["d_ffnw"], in_=FFNW[:]).then_inc(S.s_out, 16)
                sync.dma_start(out=dbg_out["d_zfeat"], in_=zfeat[:]).then_inc(S.s_out, 16)
                sync.dma_start(out=dbg_out["d_lrb"], in_=Lrb[:]).then_inc(S.s_out, 16)
                sync.dma_start(out=dbg_out["d_wot"], in_=WOT[:]).then_inc(S.s_out, 16)
                sync.dma_start(out=dbg_out["d_xn0"], in_=xn0[:]).then_inc(S.s_out, 16)
                sync.dma_start(out=dbg_out["d_stats"], in_=stats[:]).then_inc(S.s_out, 16)
                sync.wait_ge(S.s_out, 128 + 9 * 16)
                sync.dma_start(out=dbg_out["d_vdbg"], in_=vdbg[:]).then_inc(S.s_c, 16)
                sync.dma_start(out=dbg_out["d_lrow"], in_=Lrow[:]).then_inc(S.s_c, 16)
                sync.dma_start(out=dbg_out["d_lcols2"], in_=Lcols2[:]).then_inc(S.s_c, 16)
                sync.dma_start(out=dbg_out["d_ubuf"], in_=ubuf[:]).then_inc(S.s_dbg, 16)
                sync.wait_ge(S.s_dve, 17)
                sync.dma_start(out=dbg_out["d_zps"], in_=ubuf[:]).then_inc(S.s_dbg, 16)
                sync.wait_ge(S.s_c, 96 + 3 * 16)
                sync.wait_ge(S.s_dbg, 32)
            else:
                sync.wait_ge(S.s_out, 128)

        @block.gpsimd
        def _(gp):
            gp.dma_start(out=g1sb[:, 0, :, :], in_=g1rd[:]).then_inc(S.s_c2, 16)
            gp.dma_start(out=g1sb[:, 1, :, :], in_=g1id[:]).then_inc(S.s_c2, 16)
            gp.dma_start(out=g2sb[:, 0, :, :],
                         in_=g2rd[:].rearrange("c e g -> e c g")).then_inc(S.s_c2, 16)
            gp.dma_start(out=g2sb[:, 1, :, :],
                         in_=g2id[:].rearrange("c e g -> e c g")).then_inc(S.s_c2, 16)
            gp.dma_start(out=brsb[:], in_=brd[:]).then_inc(S.s_c2, 16)
            gp.dma_start(out=bisb[:], in_=bid[:]).then_inc(S.s_c2, 16)
            gp.memset(V[:].rearrange("p (t h c) -> p t h c", h=NH, c=65)[:, :, :, 64:65],
                      1.0).then_inc(S.s_vones, 1)
            for i in range(24):
                wname = ("wq", "wk", "wv")[i // 8]
                r = i % 8
                if i >= 4:
                    gp.wait_ge(S.s_tpw, i - 3)
                gp.dma_start(out=wraw4[:, (i % 4) * 1024:(i % 4 + 1) * 1024],
                             in_=wdrams[wname][r * 128:(r + 1) * 128, :]
                             ).then_inc((S.s_w, S.s_wB, S.s_wC, S.s_wD)[i % 4], 16)
            for i in range(24, 32):
                k = i - 24
                gp.wait_ge(S.s_tpw, i - 1)
                gp.dma_start(out=wraw[:, (k % 2) * 1024:(k % 2 + 1) * 1024],
                             in_=wdrams["wo"][(i % 8) * 128:(i % 8 + 1) * 128, :]
                             ).then_inc((S.s_wE, S.s_wF)[k % 2], 16)
            for W in range(2):
                for cf in range(4):
                    for c in range(1, 4):
                        k = bk(W, cf, c)
                        gp.wait_ge(S.s_fbD, k + 1)
                        pn = pool_cnt_before(W, cf, c)
                        if pn > 0:
                            gp.wait_ge(S.s_fbP, pn)
                        accA = bscr[:, 0:512]
                        accB = bscr[:, 512:1024]
                        src = accA if c % 2 == 1 else accB
                        dst = accB if c % 2 == 1 else accA
                        tmp = tmpsb[:, (k % 2) * 512:(k % 2 + 1) * 512]
                        if c < 3:
                            gp.tensor_tensor(dst[:], src[:], tmp[:], OP.add
                                             ).then_inc(S.s_fbP, 1)
                        else:
                            wt_dst = FFNW[:, W * 2048 + cf * 512: W * 2048 + (cf + 1) * 512]
                            gp.tensor_tensor(wt_dst, src[:], tmp[:], OP.add
                                             ).then_inc(S.s_fbP, 1)
                            if W == 1:
                                gp.wait_ge(S.s_fbP, pn + 1)
                                gp.tensor_scalar_mul(
                                    FFNW[:, 2 * 2048 + cf * 512: 2 * 2048 + (cf + 1) * 512],
                                    wt_dst, -1.0).then_inc(S.s_ffnw, 1)
            for phi in range(N_PHI):
                p, Q = phi // 4, phi % 4
                gp.wait_ge(S.s_lb, 32 * (phi + 1))
                gp.wait_ge(S.s_zev, 2 * phi + 2)
                for h01 in (0, 1):
                    h = 2 * p + h01
                    rows = slice((h % 2) * 64, (h % 2) * 64 + 64)
                    zsl = zfeat[rows, p * T + 512 * Q: p * T + 512 * (Q + 1)]
                    gp.tensor_tensor(zsl, zsl, Lrb[rows, 512 * Q:512 * (Q + 1)],
                                     OP.mult).then_inc(S.s_norm, 1)

        @block.vector
        def _(vec):
            raw = [0]

            def w_evac(b):
                wname, wt = WTS[b // 8]
                r = b % 8
                vec.wait_ge(S.s_tpw, b + 1)
                src = (A2b if b % 2 == 0 else B2b)[:, 0:1024].rearrange(
                    "p (dc col) -> p dc col", col=128)
                vec.tensor_copy(wdest(wt, r), src).then_inc(S.s_wde, 1)

            for t in range(NT):
                xt = xin0[:, (t % 2) * 1024:(t % 2 + 1) * 1024]
                xnt = xn0[:, (t % 2) * 1024:(t % 2 + 1) * 1024]
                o = (t % 2) * 8
                s1 = stats[:, o + 0:o + 1]
                s2 = stats[:, o + 1:o + 2]
                m = stats[:, o + 2:o + 3]
                msq = stats[:, o + 3:o + 4]
                var = stats[:, o + 4:o + 5]
                rstd = stats[:, o + 5:o + 6]
                negmr = stats[:, o + 6:o + 7]
                vec.wait_ge(S.s_x if t % 2 == 0 else S.s_xB, 16 * (t // 2 + 1))
                if t >= 2:
                    vec.wait_ge(S.s_tpx, t - 1)
                    vec.wait_ge(S.s_rstd, t - 1)
                vec.reduce_sum(s1, xt, axis=AX.X).then_inc(S.s_raw, 1)
                raw[0] += 1
                vec.scalar_tensor_tensor(xnt, xt, 1.0, xt, OP.mult, OP.mult,
                                         accum_out=s2).then_inc(S.s_raw, 1)
                raw[0] += 1
                vec.wait_ge(S.s_raw, raw[0])
                vec.tensor_scalar_mul(m, s1, 1.0 / 1024.0).then_inc(S.s_raw, 1)
                raw[0] += 1
                vec.tensor_copy(xnt[:, 0:256], xt[:, 0:256])
                vec.wait_ge(S.s_raw, raw[0])
                vec.tensor_tensor(msq, m, m, OP.mult).then_inc(S.s_raw, 1)
                raw[0] += 1
                vec.tensor_copy(xnt[:, 256:512], xt[:, 256:512])
                vec.wait_ge(S.s_raw, raw[0])
                vec.scalar_tensor_tensor(var, s2, 1.0 / 1024.0, msq, OP.mult,
                                         OP.subtract).then_inc(S.s_raw, 1)
                raw[0] += 1
                vec.tensor_copy(xnt[:, 512:1024], xt[:, 512:1024])
                vec.wait_ge(S.s_raw, raw[0])
                vec.tensor_copy(vdbg[:, t % 2:t % 2 + 1], var
                                ).then_inc(S.s_stats, 1)
                vec.wait_ge(S.s_rstd, t + 1)
                vec.tensor_copy(vdbg[:, 4 + t % 2:5 + t % 2], rstd)
                vec.scalar_tensor_tensor(negmr, m, -1.0, rstd, OP.mult, OP.mult
                                         ).then_inc(S.s_raw, 1)
                raw[0] += 1
                vec.wait_ge(S.s_raw, raw[0])
                vec.tensor_copy(vdbg[:, 6 + t % 2:7 + t % 2], negmr)
                vec.tensor_scalar(xnt, xt, rstd, negmr, OP.mult, OP.add
                                  ).then_inc(S.s_ln, 1)
                for te_ in ([t - 2] if t >= 2 else []) + ([14, 15] if t == 15 else []):
                    vec.wait_ge(S.s_tpx, te_ + 1)
                    src = (A2b if te_ % 2 == 0 else B2b)[:, 1024:2048].rearrange(
                        "p (dc col) -> p dc col", col=128)
                    dst = xnfeat[:].rearrange("p (dc col) -> p dc col", col=T
                                              )[:, :, te_ * 128:(te_ + 1) * 128]
                    vec.tensor_copy(dst, src).then_inc(S.s_xne, 1)
                    if te_ < 16:
                        w_evac(te_)
                if t == 15:
                    for b_ in range(16, 24):
                        w_evac(b_)
            for g in range(16):
                tname, dst_t = ("Q", QT) if g < 8 else ("K", KT)
                f = g % 8
                for tb in range(4):
                    vec.wait_ge(S.s_qkmm, 4 * g + tb + 1)
                    grp = (A2, A2, B2, B2) if g % 2 == 0 else (Z, Z, Z, Z)
                    if g % 2 == 0:
                        src = grp[tb][:, (tb % 2) * 512:(tb % 2 + 1) * 512]
                    else:
                        src = Z[:, tb * 512:(tb + 1) * 512]
                    vec.tensor_copy(dst_t[:, f * T + tb * 512: f * T + (tb + 1) * 512],
                                    src).then_inc(S.s_qkev, 1)
            for j in range(NT):
                vec.wait_ge(S.s_vmm, j + 1)
                ps = A2 if j % 2 == 0 else B2
                src = ps[:].rearrange("p (h c) -> p h c", c=64)
                dst = V[:, j * 1040:(j + 1) * 1040].rearrange(
                    "p (h c) -> p h c", c=65)[:, :, 0:64]
                vec.tensor_copy(dst, src).then_inc(S.s_vev, 1)
            for b in range(24, 32):
                wname, wt = WTS[b // 8]
                r = b % 8
                vec.wait_ge(S.s_tpw, b + 1)
                src = (A2b if b % 2 == 0 else B2b)[:, 0:1024].rearrange(
                    "p (dc col) -> p dc col", col=128)
                vec.tensor_copy(wdest(wt, r), src).then_inc(S.s_wde, 1)
            vec.wait_ge(S.s_c, 96)
            vec.wait_ge(S.s_c2, 96)
            for W in range(2):
                if W == 1:
                    vec.wait_ge(S.s_fbC, 16)
                    vec.wait_ge(S.s_fbA, 16)
                for c in range(4):
                    vec.tensor_copy(
                        g1exp[:, c, :].rearrange("b (d g) -> b d g", g=16),
                        g1sb[:, W, c, :].unsqueeze(2).broadcast_to([32, 32, 16]))
                    ins = vec.tensor_copy(
                        g2exp[:, c, :].rearrange("e (d g) -> e d g", g=16),
                        g2sb[:, W, c, :].unsqueeze(1).broadcast_to([16, 8, 16]))
                    if c == 3:
                        ins.then_inc(S.s_fbB, 1)
                for cf in range(4):
                    for c in range(4):
                        k = bk(W, cf, c)
                        vec.wait_ge(S.s_fbA, k + 1)
                        if k >= 1:
                            vec.wait_ge(S.s_fbD, k)
                        vec.tensor_copy(t2sb[:], Z[:, (k % 4) * 512:(k % 4 + 1) * 512]
                                        ).then_inc(S.s_fbB, 1)
                        vec.wait_ge(S.s_fbB, (k + 2) if k < 16 else (k + 3))
                        vec.wait_ge(S.s_fbC, k + 1)
                        ps1 = A2[:, (k % 2) * 512:(k % 2 + 1) * 512]
                        if c == 0:
                            dst = bscr[:, 0:512]
                        else:
                            dst = tmpsb[:, (k % 2) * 512:(k % 2 + 1) * 512]
                        pcb = pool_cnt_before(W, cf, c)
                        if pcb > 0:
                            vec.wait_ge(S.s_fbP, pcb)
                        vec.tensor_tensor(dst, t2sb[:], ps1, OP.mult
                                          ).then_inc(S.s_fbD, 1)
            def lrow_slot(phi):
                sl = phi % 4
                base = Lrow if sl < 2 else LrowB
                co = 512 * (sl % 2)
                return base, co

            def evac_phi(phi):
                p, Q = phi // 4, phi % 4
                lrb_, co = lrow_slot(phi)
                zo = 1024 * (phi % 2)
                si_last = STEP_IDX[(p, Q, 4 * Q + 3)]
                vec.wait_ge(S.s_zu, 2 * (si_last + 1))
                if phi >= 4:
                    vec.wait_ge(S.s_lb, 32 * (phi - 3))
                vec.tensor_copy(lrb_[0:1, co:co + 512], Z[64:65, zo:zo + 512]
                                ).then_inc(S.s_raw, 1)
                raw[0] += 1
                vec.wait_ge(S.s_raw, raw[0])
                vec.tensor_copy(lrb_[32:33, co:co + 512], Z[64:65, zo + 512:zo + 1024]
                                ).then_inc(S.s_raw, 1)
                raw[0] += 1
                vec.wait_ge(S.s_raw, raw[0])
                vec.tensor_copy(vdbg[0:1, 46:47], lrb_[32:33, co + 511:co + 512]
                                ).then_inc(S.s_zev, 1)
                for h01 in (0, 1):
                    h = 2 * p + h01
                    rows = slice((h % 2) * 64, (h % 2) * 64 + 64)
                    zt = Z[0:64, zo + h01 * 512: zo + (h01 + 1) * 512]
                    vec.tensor_copy(zfeat[rows, p * T + 512 * Q: p * T + 512 * (Q + 1)],
                                    zt).then_inc(S.s_raw, 1)
                    raw[0] += 1
                    vec.wait_ge(S.s_raw, raw[0])
                vec.tensor_copy(vdbg[0:64, 44:45],
                                zfeat[0:64, p * T + 512 * Q + 511: p * T + 512 * Q + 512]
                                ).then_inc(S.s_zev, 1)

            def recip_phi(phi):
                ro = 64 * (phi % 2)
                vec.wait_ge(S.s_lsc, 16 * (phi + 1))
                if phi >= 2:
                    vec.wait_ge(S.s_lga, 16 * (phi - 1))
                vec.reciprocal(Lcols2[ro:ro + 64, :], Lcols[ro:ro + 64, :]
                               ).then_inc(S.s_raw, 1)
                raw[0] += 1
                vec.wait_ge(S.s_raw, raw[0])
                vec.tensor_copy(vdbg[ro:ro + 64, 47:48], Lcols2[ro:ro + 64, 15:16]
                                ).then_inc(S.s_lrec, 1)

            for phi in range(N_PHI):
                evac_phi(phi)
                if phi >= 2 and phi - 2 < ACT_TAIL:
                    recip_phi(phi - 2)
            for t in range(NT):
                vec.wait_ge(S.s_p3mm, t + 1)
                vec.wait_ge(S.s_xr, 16 * (t + 1))
                if t >= 2:
                    vec.wait_ge(S.s_out if t % 2 == 0 else S.s_outB,
                                16 * (t // 2))
                if t == 0:
                    vec.wait_ge(S.s_lb, 32 * N_PHI)
                ps = A2 if t % 2 == 0 else B2
                vec.tensor_tensor(outb[:, (t % 2) * 1024:(t % 2 + 1) * 1024],
                                  xin3[:, (t % 2) * 1024:(t % 2 + 1) * 1024],
                                  ps[:], OP.add).then_inc(S.s_raw, 1)
                raw[0] += 1
                vec.wait_ge(S.s_raw, raw[0])
                vec.tensor_copy(vdbg[:, 45:46],
                                outb[:, (t % 2) * 1024 + 1023:(t % 2) * 1024 + 1024]
                                ).then_inc(S.s_fin, 1)
            if dbg:
                vec.wait_ge(S.s_dbg, 16)
                vec.tensor_copy(ubuf[0:65, 0:2048], Z[0:65, :]).then_inc(S.s_dve, 1)

        @block.scalar
        def _(sc):
            rawA = [0]
            AB4r = AB4[:].rearrange("p (hh q) -> p hh q", q=512)
            sc.wait_ge(S.s_c, 96)
            sc.wait_ge(S.s_c2, 96)
            for t in range(NT):
                o = (t % 2) * 8
                var = stats[:, o + 4:o + 5]
                rstd = stats[:, o + 5:o + 6]
                lnv = stats[:, o + 7:o + 8]
                sc.wait_ge(S.s_stats, t + 1)
                sc.activation(lnv, var, AF.Ln, bias=actcsb[:, 0:1], scale=1.0
                              ).then_inc(S.s_rawA, 1)
                rawA[0] += 1
                sc.wait_ge(S.s_rawA, rawA[0])
                sc.activation(rstd, lnv, AF.Exp, scale=-0.5).then_inc(S.s_rawA, 1)
                rawA[0] += 1
                sc.wait_ge(S.s_rawA, rawA[0])
                sc.copy(vdbg[:, 42:43], rstd).then_inc(S.s_rstd, 1)
            last_z_cnt_a = {}
            for si in range(N_STEPS):
                p, Q, j = STEPS[si]
                qs, qe = qrange2(Q, j)
                L = qe - qs
                pg = si % 2
                sc.wait_ge(S.s_st, si + 1)
                for slot_h in (0, 1):
                    key = (j % 4) * 2 + slot_h
                    if key in last_z_cnt_a:
                        sc.wait_ge(S.s_zu, last_z_cnt_a[key])
                    last_z_cnt_a[key] = 2 * si + 1 + slot_h
                src = AB4r[:, pg:pg + 3:2, 0:L]
                dst = ubig[:, (j % 4) * 1024:(j % 4) * 1024 + 1024].rearrange(
                    "p (hh q) -> p hh q", q=512)[:, :, 0:L]
                sc.activation(dst, src, AF.Exp, bias=actcsb[:, 1:2],
                              scale=0.125).then_inc(S.s_exp, 1)
            for phi in range(ACT_TAIL, N_PHI):
                base, co = (Lrow if phi % 4 < 2 else LrowB), 512 * (phi % 2)
                sc.wait_ge(S.s_zev, 2 * phi + 1)
                sc.activation(ubuf[0:33, 0:512], base[0:33, co:co + 512], AF.Ln
                              ).then_inc(S.s_rawA, 1)
                rawA[0] += 1
                sc.wait_ge(S.s_rawA, rawA[0])
                sc.activation(base[0:33, co:co + 512], ubuf[0:33, 0:512], AF.Exp,
                              scale=-1.0).then_inc(S.s_lrec, 1)

        @block.tensor
        def _(te):
            te.wait_ge(S.s_c, 96)
            te.wait_ge(S.s_c2, 96)

            def xn_tr(t):
                te.wait_ge(S.s_ln, t + 1)
                if t >= 2:
                    te.wait_ge(S.s_xne, t - 1)
                dstp = (A2b if t % 2 == 0 else B2b)
                for dc in range(8):
                    ins = te.transpose(
                        dstp[:, 1024 + dc * 128:1024 + (dc + 1) * 128],
                        xn0[:, (t % 2) * 1024 + dc * 128:(t % 2) * 1024 + (dc + 1) * 128],
                        identsb[:])
                    if dc == 7:
                        ins.then_inc(S.s_tpx, 1)

            def w_tr(b):
                te.wait_ge((S.s_w, S.s_wB, S.s_wC, S.s_wD)[b % 4], 16 * (b // 4 + 1))
                if b >= 2:
                    te.wait_ge(S.s_wde, b - 1)
                dstp = (A2b if b % 2 == 0 else B2b)
                for dc in range(8):
                    ins = te.transpose(
                        dstp[:, dc * 128:(dc + 1) * 128],
                        wraw4[:, (b % 4) * 1024 + dc * 128:(b % 4) * 1024 + (dc + 1) * 128],
                        identsb[:])
                    if dc == 7:
                        ins.then_inc(S.s_tpw, 1)

            for t in range(NT):
                xn_tr(t)
                if t < 16:
                    w_tr(t)
            for b in range(16, 24):
                w_tr(b)
            te.wait_ge(S.s_xne, 16)
            te.wait_ge(S.s_wde, 24)
            for g in range(16):
                wt = WQT if g < 8 else WKT
                f = g % 8
                if g >= 2:
                    te.wait_ge(S.s_qkev, 4 * (g - 1))
                for d in range(8):
                    for tb in range(4):
                        if g % 2 == 0:
                            ps = (A2 if tb < 2 else B2)[:, (tb % 2) * 512:(tb % 2 + 1) * 512]
                        else:
                            ps = Z[:, tb * 512:(tb + 1) * 512]
                        ins = te.matmul(ps,
                                        wt[:, d * 1024 + f * 128:d * 1024 + (f + 1) * 128],
                                        xnf(d)[:, tb * 512:(tb + 1) * 512],
                                        start=(d == 0), stop=(d == 7))
                        if d == 7:
                            ins.then_inc(S.s_qkmm, 1)
            te.wait_ge(S.s_wde, 24)
            te.wait_ge(S.s_qkev, 64)
            for j in range(NT):
                if j >= 2:
                    te.wait_ge(S.s_vev, j - 1)
                ps = A2 if j % 2 == 0 else B2
                for d in range(8):
                    for nb in range(2):
                        ins = te.matmul(ps[:, nb * 512:(nb + 1) * 512],
                                        xnf(d)[:, j * 128:(j + 1) * 128],
                                        WVT[:, d * 1024 + nb * 512:d * 1024 + (nb + 1) * 512],
                                        start=(d == 0), stop=(d == 7))
                        if d == 7 and nb == 1:
                            ins.then_inc(S.s_vmm, 1)
            for b in range(24, 32):
                k = b - 24
                te.wait_ge((S.s_wE, S.s_wF)[k % 2], 16 * (k // 2 + 1))
                te.wait_ge(S.s_wde, b - 1)
                if b == 24:
                    te.wait_ge(S.s_vev, 15)
                if b == 25:
                    te.wait_ge(S.s_vev, 16)
                dstp = (A2b if b % 2 == 0 else B2b)
                for dc in range(8):
                    ins = te.transpose(
                        dstp[:, dc * 128:(dc + 1) * 128],
                        wraw[:, (k % 2) * 1024 + dc * 128:(k % 2) * 1024 + (dc + 1) * 128],
                        identsb[:])
                    if dc == 7:
                        ins.then_inc(S.s_tpw, 1)
            te.wait_ge(S.s_wde, 32)
            for W in range(2):
                te.wait_ge(S.s_fbB, W * 17 + 1)
                for cf in range(4):
                    for c in range(4):
                        k = bk(W, cf, c)
                        if k >= 4:
                            te.wait_ge(S.s_fbB, (k - 4) + 2 + (1 if k - 4 >= 16 else 0))
                        te.matmul(Z[:, (k % 4) * 512:(k % 4 + 1) * 512],
                                  g2exp[:, c, :], rcolesb[:], start=True, stop=True
                                  ).then_inc(S.s_fbA, 1)
                        if k >= 2:
                            te.wait_ge(S.s_fbD, k - 1)
                        te.matmul(A2[:, (k % 2) * 512:(k % 2 + 1) * 512],
                                  g1exp[:, c, cf * 128:(cf + 1) * 128],
                                  rcolsb[:], start=True, stop=True
                                  ).then_inc(S.s_fbC, 1)
            te.wait_ge(S.s_vones, 1)
            te.wait_ge(S.s_vev, 16)
            AB4r = AB4[:].rearrange("p (hh q) -> p hh q", q=512)

            def emit_st_pair(si):
                p, Q, j = STEPS[si]
                diag = step_diag(p, Q, j)
                pg = si % 2
                if si >= 2:
                    te.wait_ge(S.s_exp, si - 1)
                if si == 0:
                    te.wait_ge(S.s_fbD, 32)
                qs, qe = qrange2(Q, j)
                L = qe - qs
                lhs0 = KT[0:64, p * T + j * 128: p * T + (j + 1) * 128]
                lhs1 = KT[64:128, p * T + j * 128: p * T + (j + 1) * 128]
                te.matmul(AB4[:, pg * 512: pg * 512 + L], lhs0,
                          QT[0:64, p * T + qs: p * T + qe],
                          start=True, stop=not diag, tile_position=(0, 0))
                ins = te.matmul(AB4[:, 1024 + pg * 512: 1024 + pg * 512 + L], lhs1,
                                QT[64:128, p * T + qs: p * T + qe],
                                start=True, stop=not diag, tile_position=(64, 0))
                if diag:
                    te.matmul(AB4[:, pg * 512: pg * 512 + 128], maskTsb[:], identsb[:],
                              start=False, stop=True)
                    ins = te.matmul(AB4[:, 1024 + pg * 512: 1024 + pg * 512 + 128],
                                    maskTsb[:], identsb[:], start=False, stop=True)
                ins.then_inc(S.s_st, 1)

            def emit_z_pair(si):
                p, Q, j = STEPS[si]
                phi = 4 * p + Q
                qs, qe = qrange2(Q, j)
                L = qe - qs
                te.wait_ge(S.s_exp, si + 1)
                zo = 1024 * (phi % 2)
                for h01 in (0, 1):
                    if j == 0 and h01 == 0 and phi <= 1:
                        te.wait_ge(S.s_fbB, 34)
                    if j == 0 and h01 == 0 and phi >= 2:
                        te.wait_ge(S.s_zev, 2 * (phi - 1))
                    h = 2 * p + h01
                    lhs = V[:, j * 1040 + h * 65: j * 1040 + (h + 1) * 65]
                    uoff = ((j % 4) * 2 + h01) * 512
                    lloc = qs - 512 * Q
                    te.matmul(Z[0:65, zo + h01 * 512 + lloc: zo + h01 * 512 + lloc + L],
                              lhs, ubig[:, uoff: uoff + L],
                              start=(j == 0), stop=(j == 4 * Q + 3)
                              ).then_inc(S.s_zu, 1)

            emit_st_pair(0)
            emit_st_pair(1)
            for si in range(N_STEPS):
                if si + 2 < N_STEPS:
                    emit_st_pair(si + 2)
                emit_z_pair(si)
            te.wait_ge(S.s_exp, N_STEPS)
            te.wait_ge(S.s_fbP, 24)
            te.wait_ge(S.s_ffnw, 4)
            for t in range(NT):
                if t >= 2:
                    te.wait_ge(S.s_fin, t - 1)
                ps = A2 if t % 2 == 0 else B2
                for c in range(8):
                    lhs = xnf(c)[:, t * 128:(t + 1) * 128]
                    if c < 4:
                        rhsR = FFNW[:, 0 * 2048 + c * 512:0 * 2048 + (c + 1) * 512]
                        rhsI = FFNW[:, 1 * 2048 + c * 512:1 * 2048 + (c + 1) * 512]
                    else:
                        rhsR = FFNW[:, 2 * 2048 + (c - 4) * 512:2 * 2048 + (c - 3) * 512]
                        rhsI = FFNW[:, 0 * 2048 + (c - 4) * 512:0 * 2048 + (c - 3) * 512]
                    te.matmul(ps[:, 0:512], lhs, rhsR, start=(c == 0), stop=False)
                    te.matmul(ps[:, 512:1024], lhs, rhsI, start=(c == 0), stop=False)
                for fc in range(8):
                    if t == 0:
                        te.wait_ge(S.s_norm, 8 * (fc + 1))
                    lhs = zfeat[:, fc * T + t * 128: fc * T + (t + 1) * 128]
                    for nb in range(2):
                        te.matmul(ps[:, nb * 512:(nb + 1) * 512], lhs,
                                  WOT[:, fc * 1024 + nb * 512:fc * 1024 + (nb + 1) * 512],
                                  start=False, stop=False)
                te.matmul(ps[:, 0:512], onessb[:], brsb[:], start=False, stop=True)
                te.matmul(ps[:, 512:1024], onessb[:], bisb[:], start=False, stop=True
                          ).then_inc(S.s_p3mm, 1)

    return nc


def kernel(L0_attn_x, L0_attn_W_K, L0_attn_W_Q, L0_attn_W_V, L0_attn_W_O,
           L0_ffn_G1_R, L0_ffn_G2_R, L0_ffn_G1_I, L0_ffn_G2_I,
           L0_ffn_bias_real, L0_ffn_bias_imag):
    x = np.ascontiguousarray(np.asarray(L0_attn_x, dtype=np.float32))
    wq = np.ascontiguousarray(np.asarray(L0_attn_W_Q, np.float32).reshape(D, D))
    wk = np.ascontiguousarray(np.asarray(L0_attn_W_K, np.float32).reshape(D, D))
    wv = np.ascontiguousarray(np.asarray(L0_attn_W_V, np.float32).reshape(D, D))
    wo = np.ascontiguousarray(np.asarray(L0_attn_W_O, np.float32))
    g1r = np.ascontiguousarray(np.asarray(L0_ffn_G1_R, np.float32).reshape(32, 4, 32))
    g1i = np.ascontiguousarray(np.asarray(L0_ffn_G1_I, np.float32).reshape(32, 4, 32))
    g2r = np.ascontiguousarray(np.asarray(L0_ffn_G2_R, np.float32).reshape(4, 16, 16))
    g2i = np.ascontiguousarray(np.asarray(L0_ffn_G2_I, np.float32).reshape(4, 16, 16))
    br = np.ascontiguousarray(np.asarray(L0_ffn_bias_real, np.float32).reshape(1, 512))
    bi = np.ascontiguousarray(np.asarray(L0_ffn_bias_imag, np.float32).reshape(1, 512))

    nc = build_nc()

    in_maps = []
    for b in range(8):
        in_maps.append(dict(x=x[b], wq=wq, wk=wk, wv=wv, wo=wo,
                            g1r=g1r, g1i=g1i, g2r=g2r, g2i=g2i, br=br, bi=bi))
    res = run_bass_kernel_spmd(nc, in_maps, core_ids=list(range(8)))
    out = np.stack([np.asarray(res.results[b]["y"]) for b in range(8)], axis=0)
    return out.astype(np.float32)


# version 28
# speedup vs baseline: 1.0578x; 1.0059x over previous
import numpy as np
import ml_dtypes
import concourse.bass as bass
import concourse.mybir as mybir
from concourse.bass_utils import run_bass_kernel_spmd

F32 = mybir.dt.float32
BF16 = mybir.dt.bfloat16
AF = mybir.ActivationFunctionType
AX = mybir.AxisListType
OP = mybir.AluOpType

T = 2048
D = 1024
NT = 16
NH = 16
DH = 64
EXP_BIAS = -12.0

BF = ml_dtypes.bfloat16


def subs_of(j):
    return 2 if j < 8 else 1


def ss_of(h, j, sub):
    return 24 * h + (2 * j if j < 8 else 8 + j) + sub


TOTAL_SS = 384

STEPS = []
for _p in range(8):
    for _Q in range(4):
        for _j in range(4 * _Q + 4):
            STEPS.append((_p, _Q, _j))
N_STEPS = len(STEPS)
STEP_IDX = {k: i for i, k in enumerate(STEPS)}
N_PHI = 32
ACT_TAIL = 28


def step_diag(p, Q, j):
    return j // 4 == Q


def qrange2(Q, j):
    qs = max(128 * j, 512 * Q)
    qe = 512 * (Q + 1)
    return qs, qe


def qrange(j, sub):
    qlo = 128 * j
    qs = qlo + 1024 * sub
    qe = min(2048, qs + 1024)
    return qs, qe


def build_nc(dbg=False):
    nc = bass.Bass("TRN2", target_bir_lowering=False, dynamic_dma_scratch_size=16384)

    xd = nc.dram_tensor("x", [T, D], F32, kind="ExternalInput").ap()
    wqd = nc.dram_tensor("wq", [D, D], F32, kind="ExternalInput").ap()
    wkd = nc.dram_tensor("wk", [D, D], F32, kind="ExternalInput").ap()
    wvd = nc.dram_tensor("wv", [D, D], F32, kind="ExternalInput").ap()
    wod = nc.dram_tensor("wo", [D, D], F32, kind="ExternalInput").ap()
    g1rd = nc.dram_tensor("g1r", [32, 4, 32], F32, kind="ExternalInput").ap()
    g1id = nc.dram_tensor("g1i", [32, 4, 32], F32, kind="ExternalInput").ap()
    g2rd = nc.dram_tensor("g2r", [4, 16, 16], F32, kind="ExternalInput").ap()
    g2id = nc.dram_tensor("g2i", [4, 16, 16], F32, kind="ExternalInput").ap()
    brd = nc.dram_tensor("br", [1, 512], F32, kind="ExternalInput").ap()
    bid = nc.dram_tensor("bi", [1, 512], F32, kind="ExternalInput").ap()
    yd = nc.dram_tensor("y", [T, D], F32, kind="ExternalOutput").ap()
    if dbg:
        dbg_out = {
            "d_xnfeat": nc.dram_tensor("d_xnfeat", [128, 8 * T], BF16, kind="ExternalOutput").ap(),
            "d_kt": nc.dram_tensor("d_kt", [128, 8 * T], BF16, kind="ExternalOutput").ap(),
            "d_v": nc.dram_tensor("d_v", [128, NT * NH * 65], BF16, kind="ExternalOutput").ap(),
            "d_ffnw": nc.dram_tensor("d_ffnw", [128, 3 * 2048], BF16, kind="ExternalOutput").ap(),
            "d_zfeat": nc.dram_tensor("d_zfeat", [128, 8 * T], BF16, kind="ExternalOutput").ap(),
            "d_lrb": nc.dram_tensor("d_lrb", [128, 2048], F32, kind="ExternalOutput").ap(),
            "d_wot": nc.dram_tensor("d_wot", [128, 8192], BF16, kind="ExternalOutput").ap(),
            "d_xin0": nc.dram_tensor("d_xin0", [128, 2048], F32, kind="ExternalOutput").ap(),
            "d_xn0": nc.dram_tensor("d_xn0", [128, 2048], BF16, kind="ExternalOutput").ap(),
            "d_stats": nc.dram_tensor("d_stats", [128, 16], F32, kind="ExternalOutput").ap(),
            "d_xnf_e": nc.dram_tensor("d_xnf_e", [128, 8 * T], BF16, kind="ExternalOutput").ap(),
            "d_kt_e": nc.dram_tensor("d_kt_e", [128, 8 * T], BF16, kind="ExternalOutput").ap(),
            "d_vdbg": nc.dram_tensor("d_vdbg", [128, 48], F32, kind="ExternalOutput").ap(),
            "d_lrow": nc.dram_tensor("d_lrow", [33, 1024], F32, kind="ExternalOutput").ap(),
            "d_lcols2": nc.dram_tensor("d_lcols2", [128, 16], F32, kind="ExternalOutput").ap(),
            "d_ubuf": nc.dram_tensor("d_ubuf", [128, 2048], BF16, kind="ExternalOutput").ap(),
            "d_zps": nc.dram_tensor("d_zps", [128, 2048], BF16, kind="ExternalOutput").ap(),
        }

    identd = nc.inline_tensor(np.eye(128, dtype=BF), "identc").ap()
    maskTd = nc.inline_tensor(
        np.triu(np.full((128, 128), -1e10, np.float32), k=1).astype(BF), "maskTc").ap()
    onesd = nc.inline_tensor(np.ones((1, 128), dtype=BF), "onesc").ap()
    rcol_np = np.zeros((32, 512), np.float32)
    rcol_np[np.arange(512) // 16, np.arange(512)] = 1.0
    rcold = nc.inline_tensor(rcol_np.astype(BF), "rcolc").ap()
    rcole_np = np.zeros((16, 512), np.float32)
    rcole_np[np.arange(512) % 16, np.arange(512)] = 1.0
    rcoled = nc.inline_tensor(rcole_np.astype(BF), "rcolec").ap()
    actc_np = np.tile(np.array([[1e-5, EXP_BIAS]], np.float32), (128, 1))
    actcd = nc.inline_tensor(actc_np, "actcc").ap()

    xnfeat = nc.alloc_sbuf_tensor("xnfeat", [128, 8 * T], BF16)
    QT = nc.alloc_sbuf_tensor("QT", [128, 8 * T], BF16)
    KT = nc.alloc_sbuf_tensor("KT", [128, 8 * T], BF16)
    V = nc.alloc_sbuf_tensor("V", [128, NT * NH * 65], BF16)
    WQT = nc.alloc_sbuf_tensor("WQT", [128, 8192], BF16)
    WKT = nc.alloc_sbuf_tensor("WKT", [128, 8192], BF16)
    WVT = nc.alloc_sbuf_tensor("WVT", [128, 8192], BF16)
    FFNW = nc.alloc_sbuf_tensor("FFNW", [128, 3 * 2048], BF16)
    ubuf = nc.alloc_sbuf_tensor("ubuf", [128, 2 * 1024], BF16)
    wraw = nc.alloc_sbuf_tensor("wraw", [128, 2 * 1024], BF16)
    bscr = nc.alloc_sbuf_tensor("bscr", [128, 1024], F32)
    identsb = nc.alloc_sbuf_tensor("identsb", [128, 128], BF16)
    maskTsb = nc.alloc_sbuf_tensor("maskTsb", [128, 128], BF16)
    onessb = nc.alloc_sbuf_tensor("onessb", [1, 128], BF16)
    rcolsb = nc.alloc_sbuf_tensor("rcolsb", [32, 512], BF16)
    rcolesb = nc.alloc_sbuf_tensor("rcolesb", [16, 512], BF16)
    actcsb = nc.alloc_sbuf_tensor("actcsb", [128, 2], F32)
    g1sb = nc.alloc_sbuf_tensor("g1sb", [32, 2, 4, 32], BF16)
    g2sb = nc.alloc_sbuf_tensor("g2sb", [16, 2, 4, 16], BF16)
    brsb = nc.alloc_sbuf_tensor("brsb", [1, 512], BF16)
    bisb = nc.alloc_sbuf_tensor("bisb", [1, 512], BF16)
    stats = nc.alloc_sbuf_tensor("stats", [128, 16], F32)
    g2exp = nc.alloc_sbuf_tensor("g2exp", [16, 4, 128], BF16)
    vdbg = nc.alloc_sbuf_tensor("vdbg", [128, 48], F32)

    zfeat = nc.alloc_sbuf_tensor_at("zfeat", [128, 8 * T], BF16,
                                    offset=nc.lookup_mloc(QT).addr)
    WOT = nc.alloc_sbuf_tensor_at("WOT", [128, 8192], BF16,
                                  offset=nc.lookup_mloc(WVT).addr)
    wqt_addr = nc.lookup_mloc(WQT).addr
    Lrow = nc.alloc_sbuf_tensor_at("Lrow", [33, 1024], F32, offset=wqt_addr)
    ubig = nc.alloc_sbuf_tensor_at("ubig", [128, 2 * 2048], BF16, offset=wqt_addr + 8192)
    outb = nc.alloc_sbuf_tensor_at("outb", [128, 2 * 1024], F32, offset=wqt_addr)
    wkt_addr = nc.lookup_mloc(WKT).addr
    Lrb = nc.alloc_sbuf_tensor_at("Lrb", [128, 2048], F32, offset=wkt_addr + 8192)
    t2sb = nc.alloc_sbuf_tensor_at("t2sb", [128, 512], F32, offset=wkt_addr + 8192)
    xin3 = nc.alloc_sbuf_tensor_at("xin3", [128, 2 * 1024], F32, offset=wkt_addr)
    Lcols = nc.alloc_sbuf_tensor("Lcols", [128, 16], F32)
    Lcols2 = nc.alloc_sbuf_tensor("Lcols2", [128, 16], F32)
    xin0 = nc.alloc_sbuf_tensor_at("xin0", [128, 2 * 1024], F32,
                                   offset=nc.lookup_mloc(FFNW).addr)
    xn0 = nc.alloc_sbuf_tensor_at("xn0", [128, 2 * 1024], BF16,
                                  offset=nc.lookup_mloc(QT).addr)
    wraw4 = nc.alloc_sbuf_tensor_at("wraw4", [128, 4 * 1024], BF16,
                                    offset=nc.lookup_mloc(KT).addr)
    tmpsb = nc.alloc_sbuf_tensor_at("tmpsb", [128, 1024], F32,
                                    offset=nc.lookup_mloc(wraw).addr)
    g1exp = nc.alloc_sbuf_tensor_at("g1exp", [32, 4, 512], BF16,
                                    offset=nc.lookup_mloc(ubuf).addr)
    LrowB = nc.alloc_sbuf_tensor_at("LrowB", [33, 1024], F32,
                                    offset=nc.lookup_mloc(bscr).addr)

    AB4 = nc.alloc_psum_tensor("AB4", [128, 2048], F32)
    A2 = AB4[:, 0:1024]
    B2 = AB4[:, 1024:2048]
    Z = nc.alloc_psum_tensor("Z", [128, 2048], F32)
    A2b = A2.bitcast(BF16)
    B2b = B2.bitcast(BF16)

    sems = {}
    for name in ("s_x s_xB s_c s_c2 s_w s_wB s_wC s_wD s_wE s_wF s_lb s_xr s_out s_outB s_stats s_rstd s_ln s_tpw s_tpx "
                 "s_wde s_xne s_qkmm s_qkev s_vmm s_vev s_vones s_fbA s_fbB "
                 "s_fbC s_fbD s_fbP s_ffnw s_st s_exp s_zu s_zev s_lsc s_lrec s_lga s_norm "
                 "s_p3mm s_fin s_dve s_dbg s_raw s_rawA s_msk").split():
        sems[name] = nc.alloc_semaphore(name)
    S = type("S", (), sems)

    WTS = [("wq", WQT), ("wk", WKT), ("wv", WVT), ("wo", WOT)]
    wdrams = {"wq": wqd, "wk": wkd, "wv": wvd, "wo": wod}

    def wdest(wt, r):
        return wt[:].rearrange("p (dc col) -> p dc col", col=1024)[:, :, r * 128:(r + 1) * 128]

    def xnf(d):
        return xnfeat[:, d * T:(d + 1) * T]

    def bk(W, cf, c):
        return (W * 4 + cf) * 4 + c

    def pool_cnt_before(W, cf, c):
        full = (W * 4 + cf)
        return full * 3 + max(0, c - 1)

    st_halves = {}

    def st_slices(j, sub):
        qs, qe = qrange(j, sub)
        L = qe - qs
        out = []
        s = 0
        while s < L:
            e = min(L, s + 512)
            out.append((s, e))
            s = e
        return out

    def z_slices(j, sub):
        qs, qe = qrange(j, sub)
        out = []
        s = qs
        while s < qe:
            e = min(qe, (s // 512 + 1) * 512)
            out.append((s, e))
            s = e
        return out

    with nc.Block() as block:

        @block.sync
        def _(sync):
            sync.dma_start(out=identsb[:], in_=identd[:]).then_inc(S.s_c, 16)
            sync.dma_start(out=maskTsb[:], in_=maskTd[:]).then_inc(S.s_c, 16)
            sync.dma_start(out=onessb[:], in_=onesd[:]).then_inc(S.s_c, 16)
            sync.dma_start(out=rcolsb[:], in_=rcold[:]).then_inc(S.s_c, 16)
            sync.dma_start(out=rcolesb[:], in_=rcoled[:]).then_inc(S.s_c, 16)
            sync.dma_start(out=actcsb[:], in_=actcd[:]).then_inc(S.s_c, 16)
            for t in range(NT):
                if t >= 2:
                    sync.wait_ge(S.s_ln, t - 1)
                sync.dma_start(out=xin0[:, (t % 2) * 1024:(t % 2 + 1) * 1024],
                               in_=xd[t * 128:(t + 1) * 128, :]
                               ).then_inc(S.s_x if t % 2 == 0 else S.s_xB, 16)
            if dbg:
                sync.wait_ge(S.s_vmm, 16)
                sync.dma_start(out=dbg_out["d_xnf_e"], in_=xnfeat[:]).then_inc(S.s_c, 16)
                sync.dma_start(out=dbg_out["d_kt_e"], in_=KT[:]).then_inc(S.s_c, 16)
            def lrview(phi):
                sl = phi % 4
                base = Lrow if sl < 2 else LrowB
                co = 512 * (sl % 2)
                return base[0:33:32, co:co + 512].rearrange("o (p k) -> o p k", k=16)

            def lcols(phi, t2=False):
                ro = 64 * (phi % 2)
                return (Lcols2 if t2 else Lcols)[ro:ro + 64, :]

            def scat(phi):
                sync.wait_ge(S.s_zev, 2 * phi + 1)
                if phi >= 2:
                    sync.wait_ge(S.s_lrec, phi - 1)
                sync.dma_start(out=lcols(phi), in_=lrview(phi)).then_inc(S.s_lsc, 16)

            def gath_bcast(phi):
                p, Q = phi // 4, phi % 4
                base, co = (Lrow if phi % 4 < 2 else LrowB), 512 * (phi % 2)
                sync.wait_ge(S.s_lrec, phi + 1)
                if phi < ACT_TAIL:
                    sync.dma_start(out=lrview(phi), in_=lcols(phi, t2=True)
                                   ).then_inc(S.s_lga, 16)
                    sync.wait_ge(S.s_lga, 16 * (phi + 1))
                if phi >= 4:
                    sync.wait_ge(S.s_norm, 2 * (phi - 3))
                sync.dma_start(out=Lrb[0:64, 512 * Q:512 * (Q + 1)],
                               in_=base[0:1, co:co + 512].unsqueeze(1
                               ).broadcast_to([1, 64, 512])).then_inc(S.s_lb, 16)
                sync.dma_start(out=Lrb[64:128, 512 * Q:512 * (Q + 1)],
                               in_=base[32:33, co:co + 512].unsqueeze(1
                               ).broadcast_to([1, 64, 512])).then_inc(S.s_lb, 16)

            for phi in range(N_PHI):
                if phi < ACT_TAIL:
                    scat(phi)
                if phi >= 2:
                    gath_bcast(phi - 2)
            gath_bcast(N_PHI - 2)
            gath_bcast(N_PHI - 1)
            for t in range(NT):
                if t == 0:
                    sync.wait_ge(S.s_qkmm, 64)
                else:
                    sync.wait_ge(S.s_fin, t - 1)
                sync.dma_start(out=xin3[:, (t % 2) * 1024:(t % 2 + 1) * 1024],
                               in_=xd[t * 128:(t + 1) * 128, :]).then_inc(S.s_xr, 16)
                sync.wait_ge(S.s_fin, t + 1)
                sync.dma_start(out=yd[t * 128:(t + 1) * 128, :],
                               in_=outb[:, (t % 2) * 1024:(t % 2 + 1) * 1024]
                               ).then_inc(S.s_out if t % 2 == 0 else S.s_outB, 16)
            sync.wait_ge(S.s_outB, 128)
            if dbg:
                sync.wait_ge(S.s_out, 128)
                sync.dma_start(out=dbg_out["d_xnfeat"], in_=xnfeat[:]).then_inc(S.s_out, 16)
                sync.dma_start(out=dbg_out["d_kt"], in_=KT[:]).then_inc(S.s_out, 16)
                sync.dma_start(out=dbg_out["d_v"], in_=V[:]).then_inc(S.s_out, 16)
                sync.dma_start(out=dbg_out["d_ffnw"], in_=FFNW[:]).then_inc(S.s_out, 16)
                sync.dma_start(out=dbg_out["d_zfeat"], in_=zfeat[:]).then_inc(S.s_out, 16)
                sync.dma_start(out=dbg_out["d_lrb"], in_=Lrb[:]).then_inc(S.s_out, 16)
                sync.dma_start(out=dbg_out["d_wot"], in_=WOT[:]).then_inc(S.s_out, 16)
                sync.dma_start(out=dbg_out["d_xn0"], in_=xn0[:]).then_inc(S.s_out, 16)
                sync.dma_start(out=dbg_out["d_stats"], in_=stats[:]).then_inc(S.s_out, 16)
                sync.wait_ge(S.s_out, 128 + 9 * 16)
                sync.dma_start(out=dbg_out["d_vdbg"], in_=vdbg[:]).then_inc(S.s_c, 16)
                sync.dma_start(out=dbg_out["d_lrow"], in_=Lrow[:]).then_inc(S.s_c, 16)
                sync.dma_start(out=dbg_out["d_lcols2"], in_=Lcols2[:]).then_inc(S.s_c, 16)
                sync.dma_start(out=dbg_out["d_ubuf"], in_=ubuf[:]).then_inc(S.s_dbg, 16)
                sync.wait_ge(S.s_dve, 17)
                sync.dma_start(out=dbg_out["d_zps"], in_=ubuf[:]).then_inc(S.s_dbg, 16)
                sync.wait_ge(S.s_c, 96 + 3 * 16)
                sync.wait_ge(S.s_dbg, 32)
            else:
                sync.wait_ge(S.s_out, 128)

        @block.gpsimd
        def _(gp):
            gp.dma_start(out=g1sb[:, 0, :, :], in_=g1rd[:]).then_inc(S.s_c2, 16)
            gp.dma_start(out=g1sb[:, 1, :, :], in_=g1id[:]).then_inc(S.s_c2, 16)
            gp.dma_start(out=g2sb[:, 0, :, :],
                         in_=g2rd[:].rearrange("c e g -> e c g")).then_inc(S.s_c2, 16)
            gp.dma_start(out=g2sb[:, 1, :, :],
                         in_=g2id[:].rearrange("c e g -> e c g")).then_inc(S.s_c2, 16)
            gp.dma_start(out=brsb[:], in_=brd[:]).then_inc(S.s_c2, 16)
            gp.dma_start(out=bisb[:], in_=bid[:]).then_inc(S.s_c2, 16)
            gp.memset(V[:].rearrange("p (t h c) -> p t h c", h=NH, c=65)[:, :, :, 64:65],
                      1.0).then_inc(S.s_vones, 1)
            for i in range(24):
                wname = ("wq", "wk", "wv")[i // 8]
                r = i % 8
                if i >= 4:
                    gp.wait_ge(S.s_tpw, i - 3)
                gp.dma_start(out=wraw4[:, (i % 4) * 1024:(i % 4 + 1) * 1024],
                             in_=wdrams[wname][r * 128:(r + 1) * 128, :]
                             ).then_inc((S.s_w, S.s_wB, S.s_wC, S.s_wD)[i % 4], 16)
            for i in range(24, 32):
                k = i - 24
                gp.wait_ge(S.s_tpw, i - 1)
                gp.dma_start(out=wraw[:, (k % 2) * 1024:(k % 2 + 1) * 1024],
                             in_=wdrams["wo"][(i % 8) * 128:(i % 8 + 1) * 128, :]
                             ).then_inc((S.s_wE, S.s_wF)[k % 2], 16)
            for W in range(2):
                for cf in range(4):
                    for c in range(1, 4):
                        k = bk(W, cf, c)
                        gp.wait_ge(S.s_fbD, k + 1)
                        pn = pool_cnt_before(W, cf, c)
                        if pn > 0:
                            gp.wait_ge(S.s_fbP, pn)
                        accA = bscr[:, 0:512]
                        accB = bscr[:, 512:1024]
                        src = accA if c % 2 == 1 else accB
                        dst = accB if c % 2 == 1 else accA
                        tmp = tmpsb[:, (k % 2) * 512:(k % 2 + 1) * 512]
                        if c < 3:
                            gp.tensor_tensor(dst[:], src[:], tmp[:], OP.add
                                             ).then_inc(S.s_fbP, 1)
                        else:
                            wt_dst = FFNW[:, W * 2048 + cf * 512: W * 2048 + (cf + 1) * 512]
                            gp.tensor_tensor(wt_dst, src[:], tmp[:], OP.add
                                             ).then_inc(S.s_fbP, 1)
                            if W == 1:
                                gp.wait_ge(S.s_fbP, pn + 1)
                                gp.tensor_scalar_mul(
                                    FFNW[:, 2 * 2048 + cf * 512: 2 * 2048 + (cf + 1) * 512],
                                    wt_dst, -1.0).then_inc(S.s_ffnw, 1)
            for phi in range(N_PHI):
                p, Q = phi // 4, phi % 4
                gp.wait_ge(S.s_lb, 32 * (phi + 1))
                gp.wait_ge(S.s_zev, 2 * phi + 2)
                for h01 in (0, 1):
                    h = 2 * p + h01
                    rows = slice((h % 2) * 64, (h % 2) * 64 + 64)
                    zsl = zfeat[rows, p * T + 512 * Q: p * T + 512 * (Q + 1)]
                    gp.tensor_tensor(zsl, zsl, Lrb[rows, 512 * Q:512 * (Q + 1)],
                                     OP.mult).then_inc(S.s_norm, 1)

        @block.vector
        def _(vec):
            raw = [0]

            def w_evac(b):
                wname, wt = WTS[b // 8]
                r = b % 8
                vec.wait_ge(S.s_tpw, b + 1)
                src = (A2b if b % 2 == 0 else B2b)[:, 0:1024].rearrange(
                    "p (dc col) -> p dc col", col=128)
                vec.tensor_copy(wdest(wt, r), src).then_inc(S.s_wde, 1)

            for t in range(NT):
                xt = xin0[:, (t % 2) * 1024:(t % 2 + 1) * 1024]
                xnt = xn0[:, (t % 2) * 1024:(t % 2 + 1) * 1024]
                o = (t % 2) * 8
                s1 = stats[:, o + 0:o + 1]
                s2 = stats[:, o + 1:o + 2]
                m = stats[:, o + 2:o + 3]
                msq = stats[:, o + 3:o + 4]
                var = stats[:, o + 4:o + 5]
                rstd = stats[:, o + 5:o + 6]
                negmr = stats[:, o + 6:o + 7]
                vec.wait_ge(S.s_x if t % 2 == 0 else S.s_xB, 16 * (t // 2 + 1))
                if t >= 2:
                    vec.wait_ge(S.s_tpx, t - 1)
                    vec.wait_ge(S.s_rstd, t - 1)
                vec.reduce_sum(s1, xt, axis=AX.X).then_inc(S.s_raw, 1)
                raw[0] += 1
                vec.scalar_tensor_tensor(xnt, xt, 1.0, xt, OP.mult, OP.mult,
                                         accum_out=s2).then_inc(S.s_raw, 1)
                raw[0] += 1
                vec.wait_ge(S.s_raw, raw[0])
                vec.tensor_scalar_mul(m, s1, 1.0 / 1024.0).then_inc(S.s_raw, 1)
                raw[0] += 1
                vec.tensor_copy(xnt[:, 0:256], xt[:, 0:256])
                vec.wait_ge(S.s_raw, raw[0])
                vec.tensor_tensor(msq, m, m, OP.mult).then_inc(S.s_raw, 1)
                raw[0] += 1
                vec.tensor_copy(xnt[:, 256:512], xt[:, 256:512])
                vec.wait_ge(S.s_raw, raw[0])
                vec.scalar_tensor_tensor(var, s2, 1.0 / 1024.0, msq, OP.mult,
                                         OP.subtract).then_inc(S.s_raw, 1)
                raw[0] += 1
                vec.tensor_copy(xnt[:, 512:1024], xt[:, 512:1024])
                vec.wait_ge(S.s_raw, raw[0])
                vec.tensor_copy(vdbg[:, t % 2:t % 2 + 1], var
                                ).then_inc(S.s_stats, 1)
                vec.wait_ge(S.s_rstd, t + 1)
                vec.tensor_copy(vdbg[:, 4 + t % 2:5 + t % 2], rstd)
                vec.scalar_tensor_tensor(negmr, m, -1.0, rstd, OP.mult, OP.mult
                                         ).then_inc(S.s_raw, 1)
                raw[0] += 1
                vec.wait_ge(S.s_raw, raw[0])
                vec.tensor_copy(vdbg[:, 6 + t % 2:7 + t % 2], negmr)
                vec.tensor_scalar(xnt, xt, rstd, negmr, OP.mult, OP.add
                                  ).then_inc(S.s_ln, 1)
                for te_ in ([t - 2] if t >= 2 else []) + ([14, 15] if t == 15 else []):
                    vec.wait_ge(S.s_tpx, te_ + 1)
                    src = (A2b if te_ % 2 == 0 else B2b)[:, 1024:2048].rearrange(
                        "p (dc col) -> p dc col", col=128)
                    dst = xnfeat[:].rearrange("p (dc col) -> p dc col", col=T
                                              )[:, :, te_ * 128:(te_ + 1) * 128]
                    vec.tensor_copy(dst, src).then_inc(S.s_xne, 1)
                    if te_ < 16:
                        w_evac(te_)
                if t == 15:
                    for b_ in range(16, 24):
                        w_evac(b_)
            for g in range(16):
                tname, dst_t = ("Q", QT) if g < 8 else ("K", KT)
                f = g % 8
                for tb in range(4):
                    vec.wait_ge(S.s_qkmm, 4 * g + tb + 1)
                    grp = (A2, A2, B2, B2) if g % 2 == 0 else (Z, Z, Z, Z)
                    if g % 2 == 0:
                        src = grp[tb][:, (tb % 2) * 512:(tb % 2 + 1) * 512]
                    else:
                        src = Z[:, tb * 512:(tb + 1) * 512]
                    vec.tensor_copy(dst_t[:, f * T + tb * 512: f * T + (tb + 1) * 512],
                                    src).then_inc(S.s_qkev, 1)
            for j in range(NT):
                vec.wait_ge(S.s_vmm, j + 1)
                ps = A2 if j % 2 == 0 else B2
                src = ps[:].rearrange("p (h c) -> p h c", c=64)
                dst = V[:, j * 1040:(j + 1) * 1040].rearrange(
                    "p (h c) -> p h c", c=65)[:, :, 0:64]
                vec.tensor_copy(dst, src).then_inc(S.s_vev, 1)
            for b in range(24, 32):
                wname, wt = WTS[b // 8]
                r = b % 8
                vec.wait_ge(S.s_tpw, b + 1)
                src = (A2b if b % 2 == 0 else B2b)[:, 0:1024].rearrange(
                    "p (dc col) -> p dc col", col=128)
                vec.tensor_copy(wdest(wt, r), src).then_inc(S.s_wde, 1)
            vec.wait_ge(S.s_c, 96)
            vec.wait_ge(S.s_c2, 96)
            for W in range(2):
                if W == 1:
                    vec.wait_ge(S.s_fbC, 16)
                    vec.wait_ge(S.s_fbA, 16)
                for c in range(4):
                    vec.tensor_copy(
                        g1exp[:, c, :].rearrange("b (d g) -> b d g", g=16),
                        g1sb[:, W, c, :].unsqueeze(2).broadcast_to([32, 32, 16]))
                    ins = vec.tensor_copy(
                        g2exp[:, c, :].rearrange("e (d g) -> e d g", g=16),
                        g2sb[:, W, c, :].unsqueeze(1).broadcast_to([16, 8, 16]))
                    if c == 3:
                        ins.then_inc(S.s_fbB, 1)
                for cf in range(4):
                    for c in range(4):
                        k = bk(W, cf, c)
                        vec.wait_ge(S.s_fbA, k + 1)
                        if k >= 1:
                            vec.wait_ge(S.s_fbD, k)
                        vec.tensor_copy(t2sb[:], Z[:, (k % 4) * 512:(k % 4 + 1) * 512]
                                        ).then_inc(S.s_fbB, 1)
                        vec.wait_ge(S.s_fbB, (k + 2) if k < 16 else (k + 3))
                        vec.wait_ge(S.s_fbC, k + 1)
                        ps1 = A2[:, (k % 2) * 512:(k % 2 + 1) * 512]
                        if c == 0:
                            dst = bscr[:, 0:512]
                        else:
                            dst = tmpsb[:, (k % 2) * 512:(k % 2 + 1) * 512]
                        pcb = pool_cnt_before(W, cf, c)
                        if pcb > 0:
                            vec.wait_ge(S.s_fbP, pcb)
                        vec.tensor_tensor(dst, t2sb[:], ps1, OP.mult
                                          ).then_inc(S.s_fbD, 1)
            def lrow_slot(phi):
                sl = phi % 4
                base = Lrow if sl < 2 else LrowB
                co = 512 * (sl % 2)
                return base, co

            def evac_phi(phi):
                p, Q = phi // 4, phi % 4
                lrb_, co = lrow_slot(phi)
                zo = 1024 * (phi % 2)
                si_last = STEP_IDX[(p, Q, 4 * Q + 3)]
                vec.wait_ge(S.s_zu, 2 * (si_last + 1))
                if phi >= 4:
                    vec.wait_ge(S.s_lb, 32 * (phi - 3))
                vec.tensor_copy(lrb_[0:1, co:co + 512], Z[64:65, zo:zo + 512]
                                ).then_inc(S.s_raw, 1)
                raw[0] += 1
                vec.wait_ge(S.s_raw, raw[0])
                vec.tensor_copy(lrb_[32:33, co:co + 512], Z[64:65, zo + 512:zo + 1024]
                                ).then_inc(S.s_raw, 1)
                raw[0] += 1
                vec.wait_ge(S.s_raw, raw[0])
                vec.tensor_copy(vdbg[0:1, 46:47], lrb_[32:33, co + 511:co + 512]
                                ).then_inc(S.s_zev, 1)
                for h01 in (0, 1):
                    h = 2 * p + h01
                    rows = slice((h % 2) * 64, (h % 2) * 64 + 64)
                    zt = Z[0:64, zo + h01 * 512: zo + (h01 + 1) * 512]
                    vec.tensor_copy(zfeat[rows, p * T + 512 * Q: p * T + 512 * (Q + 1)],
                                    zt).then_inc(S.s_raw, 1)
                    raw[0] += 1
                    vec.wait_ge(S.s_raw, raw[0])
                vec.tensor_copy(vdbg[0:64, 44:45],
                                zfeat[0:64, p * T + 512 * Q + 511: p * T + 512 * Q + 512]
                                ).then_inc(S.s_zev, 1)

            def recip_phi(phi):
                ro = 64 * (phi % 2)
                vec.wait_ge(S.s_lsc, 16 * (phi + 1))
                if phi >= 2:
                    vec.wait_ge(S.s_lga, 16 * (phi - 1))
                vec.reciprocal(Lcols2[ro:ro + 64, :], Lcols[ro:ro + 64, :]
                               ).then_inc(S.s_raw, 1)
                raw[0] += 1
                vec.wait_ge(S.s_raw, raw[0])
                vec.tensor_copy(vdbg[ro:ro + 64, 47:48], Lcols2[ro:ro + 64, 15:16]
                                ).then_inc(S.s_lrec, 1)

            for phi in range(N_PHI):
                evac_phi(phi)
                if phi >= 2 and phi - 2 < ACT_TAIL:
                    recip_phi(phi - 2)
            for t in range(NT):
                vec.wait_ge(S.s_p3mm, t + 1)
                vec.wait_ge(S.s_xr, 16 * (t + 1))
                if t >= 2:
                    vec.wait_ge(S.s_out if t % 2 == 0 else S.s_outB,
                                16 * (t // 2))
                if t == 0:
                    vec.wait_ge(S.s_lb, 32 * N_PHI)
                ps = A2 if t % 2 == 0 else B2
                vec.tensor_tensor(outb[:, (t % 2) * 1024:(t % 2 + 1) * 1024],
                                  xin3[:, (t % 2) * 1024:(t % 2 + 1) * 1024],
                                  ps[:], OP.add).then_inc(S.s_raw, 1)
                raw[0] += 1
                vec.wait_ge(S.s_raw, raw[0])
                vec.tensor_copy(vdbg[:, 45:46],
                                outb[:, (t % 2) * 1024 + 1023:(t % 2) * 1024 + 1024]
                                ).then_inc(S.s_fin, 1)
            if dbg:
                vec.wait_ge(S.s_dbg, 16)
                vec.tensor_copy(ubuf[0:65, 0:2048], Z[0:65, :]).then_inc(S.s_dve, 1)

        @block.scalar
        def _(sc):
            rawA = [0]
            AB4r = AB4[:].rearrange("p (hh q) -> p hh q", q=512)
            sc.wait_ge(S.s_c, 96)
            sc.wait_ge(S.s_c2, 96)
            for t in range(NT):
                o = (t % 2) * 8
                var = stats[:, o + 4:o + 5]
                rstd = stats[:, o + 5:o + 6]
                lnv = stats[:, o + 7:o + 8]
                sc.wait_ge(S.s_stats, t + 1)
                sc.activation(lnv, var, AF.Ln, bias=actcsb[:, 0:1], scale=1.0
                              ).then_inc(S.s_rawA, 1)
                rawA[0] += 1
                sc.wait_ge(S.s_rawA, rawA[0])
                sc.activation(rstd, lnv, AF.Exp, scale=-0.5).then_inc(S.s_rawA, 1)
                rawA[0] += 1
                sc.wait_ge(S.s_rawA, rawA[0])
                sc.copy(vdbg[:, 42:43], rstd).then_inc(S.s_rstd, 1)
            last_z_cnt_a = {}
            for si in range(N_STEPS):
                p, Q, j = STEPS[si]
                qs, qe = qrange2(Q, j)
                L = qe - qs
                pg = si % 2
                sc.wait_ge(S.s_st, si + 1)
                for slot_h in (0, 1):
                    key = (j % 4) * 2 + slot_h
                    if key in last_z_cnt_a:
                        sc.wait_ge(S.s_zu, last_z_cnt_a[key])
                    last_z_cnt_a[key] = 2 * si + 1 + slot_h
                src = AB4r[:, pg:pg + 3:2, 0:L]
                dst = ubig[:, (j % 4) * 1024:(j % 4) * 1024 + 1024].rearrange(
                    "p (hh q) -> p hh q", q=512)[:, :, 0:L]
                sc.activation(dst, src, AF.Exp, bias=actcsb[:, 1:2],
                              scale=0.125).then_inc(S.s_exp, 1)
            for phi in range(ACT_TAIL, N_PHI):
                base, co = (Lrow if phi % 4 < 2 else LrowB), 512 * (phi % 2)
                sc.wait_ge(S.s_zev, 2 * phi + 1)
                sc.activation(ubuf[0:33, 0:512], base[0:33, co:co + 512], AF.Ln
                              ).then_inc(S.s_rawA, 1)
                rawA[0] += 1
                sc.wait_ge(S.s_rawA, rawA[0])
                sc.activation(base[0:33, co:co + 512], ubuf[0:33, 0:512], AF.Exp,
                              scale=-1.0).then_inc(S.s_lrec, 1)

        @block.tensor
        def _(te):
            te.wait_ge(S.s_c, 96)
            te.wait_ge(S.s_c2, 96)

            def xn_tr(t):
                te.wait_ge(S.s_ln, t + 1)
                if t >= 2:
                    te.wait_ge(S.s_xne, t - 1)
                dstp = (A2b if t % 2 == 0 else B2b)
                for dc in range(8):
                    ins = te.transpose(
                        dstp[:, 1024 + dc * 128:1024 + (dc + 1) * 128],
                        xn0[:, (t % 2) * 1024 + dc * 128:(t % 2) * 1024 + (dc + 1) * 128],
                        identsb[:])
                    if dc == 7:
                        ins.then_inc(S.s_tpx, 1)

            def w_tr(b):
                te.wait_ge((S.s_w, S.s_wB, S.s_wC, S.s_wD)[b % 4], 16 * (b // 4 + 1))
                if b >= 2:
                    te.wait_ge(S.s_wde, b - 1)
                dstp = (A2b if b % 2 == 0 else B2b)
                for dc in range(8):
                    ins = te.transpose(
                        dstp[:, dc * 128:(dc + 1) * 128],
                        wraw4[:, (b % 4) * 1024 + dc * 128:(b % 4) * 1024 + (dc + 1) * 128],
                        identsb[:])
                    if dc == 7:
                        ins.then_inc(S.s_tpw, 1)

            for t in range(NT):
                xn_tr(t)
                if t < 16:
                    w_tr(t)
            for b in range(16, 24):
                w_tr(b)
            te.wait_ge(S.s_xne, 16)
            te.wait_ge(S.s_wde, 24)
            for g in range(16):
                wt = WQT if g < 8 else WKT
                f = g % 8
                if g >= 2:
                    te.wait_ge(S.s_qkev, 4 * (g - 1))
                for d in range(8):
                    for tb in range(4):
                        if g % 2 == 0:
                            ps = (A2 if tb < 2 else B2)[:, (tb % 2) * 512:(tb % 2 + 1) * 512]
                        else:
                            ps = Z[:, tb * 512:(tb + 1) * 512]
                        ins = te.matmul(ps,
                                        wt[:, d * 1024 + f * 128:d * 1024 + (f + 1) * 128],
                                        xnf(d)[:, tb * 512:(tb + 1) * 512],
                                        start=(d == 0), stop=(d == 7))
                        if d == 7:
                            ins.then_inc(S.s_qkmm, 1)
            te.wait_ge(S.s_wde, 24)
            te.wait_ge(S.s_qkev, 64)
            for j in range(NT):
                if j >= 2:
                    te.wait_ge(S.s_vev, j - 1)
                ps = A2 if j % 2 == 0 else B2
                for d in range(8):
                    for nb in range(2):
                        ins = te.matmul(ps[:, nb * 512:(nb + 1) * 512],
                                        xnf(d)[:, j * 128:(j + 1) * 128],
                                        WVT[:, d * 1024 + nb * 512:d * 1024 + (nb + 1) * 512],
                                        start=(d == 0), stop=(d == 7))
                        if d == 7 and nb == 1:
                            ins.then_inc(S.s_vmm, 1)
            for b in range(24, 32):
                k = b - 24
                te.wait_ge((S.s_wE, S.s_wF)[k % 2], 16 * (k // 2 + 1))
                te.wait_ge(S.s_wde, b - 1)
                if b == 24:
                    te.wait_ge(S.s_vev, 15)
                if b == 25:
                    te.wait_ge(S.s_vev, 16)
                dstp = (A2b if b % 2 == 0 else B2b)
                for dc in range(8):
                    ins = te.transpose(
                        dstp[:, dc * 128:(dc + 1) * 128],
                        wraw[:, (k % 2) * 1024 + dc * 128:(k % 2) * 1024 + (dc + 1) * 128],
                        identsb[:])
                    if dc == 7:
                        ins.then_inc(S.s_tpw, 1)
            te.wait_ge(S.s_wde, 32)
            for W in range(2):
                te.wait_ge(S.s_fbB, W * 17 + 1)
                for cf in range(4):
                    for c in range(4):
                        k = bk(W, cf, c)
                        if k >= 4:
                            te.wait_ge(S.s_fbB, (k - 4) + 2 + (1 if k - 4 >= 16 else 0))
                        te.matmul(Z[:, (k % 4) * 512:(k % 4 + 1) * 512],
                                  g2exp[:, c, :], rcolesb[:], start=True, stop=True
                                  ).then_inc(S.s_fbA, 1)
                        if k >= 2:
                            te.wait_ge(S.s_fbD, k - 1)
                        te.matmul(A2[:, (k % 2) * 512:(k % 2 + 1) * 512],
                                  g1exp[:, c, cf * 128:(cf + 1) * 128],
                                  rcolsb[:], start=True, stop=True
                                  ).then_inc(S.s_fbC, 1)
            te.wait_ge(S.s_vones, 1)
            te.wait_ge(S.s_vev, 16)
            AB4r = AB4[:].rearrange("p (hh q) -> p hh q", q=512)

            def emit_st_pair(si):
                p, Q, j = STEPS[si]
                diag = step_diag(p, Q, j)
                pg = si % 2
                if si >= 2:
                    te.wait_ge(S.s_exp, si - 1)
                if si == 0:
                    te.wait_ge(S.s_fbD, 32)
                qs, qe = qrange2(Q, j)
                L = qe - qs
                lhs0 = KT[0:64, p * T + j * 128: p * T + (j + 1) * 128]
                lhs1 = KT[64:128, p * T + j * 128: p * T + (j + 1) * 128]
                te.matmul(AB4[:, pg * 512: pg * 512 + L], lhs0,
                          QT[0:64, p * T + qs: p * T + qe],
                          start=True, stop=not diag, tile_position=(0, 0))
                ins = te.matmul(AB4[:, 1024 + pg * 512: 1024 + pg * 512 + L], lhs1,
                                QT[64:128, p * T + qs: p * T + qe],
                                start=True, stop=not diag, tile_position=(64, 0))
                if diag:
                    te.matmul(AB4[:, pg * 512: pg * 512 + 128], maskTsb[:], identsb[:],
                              start=False, stop=True)
                    ins = te.matmul(AB4[:, 1024 + pg * 512: 1024 + pg * 512 + 128],
                                    maskTsb[:], identsb[:], start=False, stop=True)
                ins.then_inc(S.s_st, 1)

            def emit_z_pair(si):
                p, Q, j = STEPS[si]
                phi = 4 * p + Q
                qs, qe = qrange2(Q, j)
                L = qe - qs
                te.wait_ge(S.s_exp, si + 1)
                zo = 1024 * (phi % 2)
                for h01 in (0, 1):
                    if j == 0 and h01 == 0 and phi <= 1:
                        te.wait_ge(S.s_fbB, 34)
                    if j == 0 and h01 == 0 and phi >= 2:
                        te.wait_ge(S.s_zev, 2 * (phi - 1))
                    h = 2 * p + h01
                    lhs = V[:, j * 1040 + h * 65: j * 1040 + (h + 1) * 65]
                    uoff = ((j % 4) * 2 + h01) * 512
                    lloc = qs - 512 * Q
                    te.matmul(Z[0:65, zo + h01 * 512 + lloc: zo + h01 * 512 + lloc + L],
                              lhs, ubig[:, uoff: uoff + L],
                              start=(j == 0), stop=(j == 4 * Q + 3)
                              ).then_inc(S.s_zu, 1)

            emit_st_pair(0)
            emit_st_pair(1)
            for si in range(N_STEPS):
                if si + 2 < N_STEPS:
                    emit_st_pair(si + 2)
                emit_z_pair(si)
            te.wait_ge(S.s_exp, N_STEPS)
            te.wait_ge(S.s_fbP, 24)
            te.wait_ge(S.s_ffnw, 4)

            def ffn_mms(t):
                ps = A2 if t % 2 == 0 else B2
                for c in range(8):
                    lhs = xnf(c)[:, t * 128:(t + 1) * 128]
                    if c < 4:
                        rhsR = FFNW[:, 0 * 2048 + c * 512:0 * 2048 + (c + 1) * 512]
                        rhsI = FFNW[:, 1 * 2048 + c * 512:1 * 2048 + (c + 1) * 512]
                    else:
                        rhsR = FFNW[:, 2 * 2048 + (c - 4) * 512:2 * 2048 + (c - 3) * 512]
                        rhsI = FFNW[:, 0 * 2048 + (c - 4) * 512:0 * 2048 + (c - 3) * 512]
                    te.matmul(ps[:, 0:512], lhs, rhsR, start=(c == 0), stop=False)
                    te.matmul(ps[:, 512:1024], lhs, rhsI, start=(c == 0), stop=False)

            def wo_mms(t, fcs, gate):
                ps = A2 if t % 2 == 0 else B2
                for fc in fcs:
                    if gate:
                        te.wait_ge(S.s_norm, 8 * (fc + 1))
                    lhs = zfeat[:, fc * T + t * 128: fc * T + (t + 1) * 128]
                    for nb in range(2):
                        te.matmul(ps[:, nb * 512:(nb + 1) * 512], lhs,
                                  WOT[:, fc * 1024 + nb * 512:fc * 1024 + (nb + 1) * 512],
                                  start=False, stop=False)

            def bias_mms(t):
                ps = A2 if t % 2 == 0 else B2
                te.matmul(ps[:, 0:512], onessb[:], brsb[:], start=False, stop=True)
                te.matmul(ps[:, 512:1024], onessb[:], bisb[:], start=False, stop=True
                          ).then_inc(S.s_p3mm, 1)

            ffn_mms(0)
            wo_mms(0, range(0, 4), gate=True)
            ffn_mms(1)
            wo_mms(0, range(4, 8), gate=True)
            bias_mms(0)
            wo_mms(1, range(8), gate=False)
            bias_mms(1)
            for t in range(2, NT):
                te.wait_ge(S.s_fin, t - 1)
                ffn_mms(t)
                wo_mms(t, range(8), gate=False)
                bias_mms(t)

    return nc


def kernel(L0_attn_x, L0_attn_W_K, L0_attn_W_Q, L0_attn_W_V, L0_attn_W_O,
           L0_ffn_G1_R, L0_ffn_G2_R, L0_ffn_G1_I, L0_ffn_G2_I,
           L0_ffn_bias_real, L0_ffn_bias_imag):
    x = np.ascontiguousarray(np.asarray(L0_attn_x, dtype=np.float32))
    wq = np.ascontiguousarray(np.asarray(L0_attn_W_Q, np.float32).reshape(D, D))
    wk = np.ascontiguousarray(np.asarray(L0_attn_W_K, np.float32).reshape(D, D))
    wv = np.ascontiguousarray(np.asarray(L0_attn_W_V, np.float32).reshape(D, D))
    wo = np.ascontiguousarray(np.asarray(L0_attn_W_O, np.float32))
    g1r = np.ascontiguousarray(np.asarray(L0_ffn_G1_R, np.float32).reshape(32, 4, 32))
    g1i = np.ascontiguousarray(np.asarray(L0_ffn_G1_I, np.float32).reshape(32, 4, 32))
    g2r = np.ascontiguousarray(np.asarray(L0_ffn_G2_R, np.float32).reshape(4, 16, 16))
    g2i = np.ascontiguousarray(np.asarray(L0_ffn_G2_I, np.float32).reshape(4, 16, 16))
    br = np.ascontiguousarray(np.asarray(L0_ffn_bias_real, np.float32).reshape(1, 512))
    bi = np.ascontiguousarray(np.asarray(L0_ffn_bias_imag, np.float32).reshape(1, 512))

    nc = build_nc()

    in_maps = []
    for b in range(8):
        in_maps.append(dict(x=x[b], wq=wq, wk=wk, wv=wv, wo=wo,
                            g1r=g1r, g1i=g1i, g2r=g2r, g2i=g2i, br=br, bi=bi))
    res = run_bass_kernel_spmd(nc, in_maps, core_ids=list(range(8)))
    out = np.stack([np.asarray(res.results[b]["y"]) for b in range(8)], axis=0)
    return out.astype(np.float32)


# version 30
# speedup vs baseline: 1.0748x; 1.0161x over previous
import numpy as np
import ml_dtypes
import concourse.bass as bass
import concourse.mybir as mybir
from concourse.bass_utils import run_bass_kernel_spmd

F32 = mybir.dt.float32
BF16 = mybir.dt.bfloat16
AF = mybir.ActivationFunctionType
AX = mybir.AxisListType
OP = mybir.AluOpType

T = 2048
D = 1024
NT = 16
NH = 16
DH = 64
EXP_BIAS = -12.0

BF = ml_dtypes.bfloat16


def subs_of(j):
    return 2 if j < 8 else 1


def ss_of(h, j, sub):
    return 24 * h + (2 * j if j < 8 else 8 + j) + sub


TOTAL_SS = 384

STEPS = []
for _p in range(8):
    for _Q in range(4):
        for _j in range(4 * _Q + 4):
            STEPS.append((_p, _Q, _j))
N_STEPS = len(STEPS)
STEP_IDX = {k: i for i, k in enumerate(STEPS)}
N_PHI = 32
ACT_TAIL = 28


def step_diag(p, Q, j):
    return j // 4 == Q


def qrange2(Q, j):
    qs = max(128 * j, 512 * Q)
    qe = 512 * (Q + 1)
    return qs, qe


def qrange(j, sub):
    qlo = 128 * j
    qs = qlo + 1024 * sub
    qe = min(2048, qs + 1024)
    return qs, qe


def build_nc(dbg=False):
    nc = bass.Bass("TRN2", target_bir_lowering=False, dynamic_dma_scratch_size=16384)

    xd = nc.dram_tensor("x", [T, D], F32, kind="ExternalInput").ap()
    wqd = nc.dram_tensor("wq", [D, D], F32, kind="ExternalInput").ap()
    wkd = nc.dram_tensor("wk", [D, D], F32, kind="ExternalInput").ap()
    wvd = nc.dram_tensor("wv", [D, D], F32, kind="ExternalInput").ap()
    wod = nc.dram_tensor("wo", [D, D], F32, kind="ExternalInput").ap()
    g1rd = nc.dram_tensor("g1r", [32, 4, 32], F32, kind="ExternalInput").ap()
    g1id = nc.dram_tensor("g1i", [32, 4, 32], F32, kind="ExternalInput").ap()
    g2rd = nc.dram_tensor("g2r", [4, 16, 16], F32, kind="ExternalInput").ap()
    g2id = nc.dram_tensor("g2i", [4, 16, 16], F32, kind="ExternalInput").ap()
    brd = nc.dram_tensor("br", [1, 512], F32, kind="ExternalInput").ap()
    bid = nc.dram_tensor("bi", [1, 512], F32, kind="ExternalInput").ap()
    yd = nc.dram_tensor("y", [T, D], F32, kind="ExternalOutput").ap()
    if dbg:
        dbg_out = {
            "d_xnfeat": nc.dram_tensor("d_xnfeat", [128, 8 * T], BF16, kind="ExternalOutput").ap(),
            "d_kt": nc.dram_tensor("d_kt", [128, 8 * T], BF16, kind="ExternalOutput").ap(),
            "d_v": nc.dram_tensor("d_v", [128, NT * NH * 65], BF16, kind="ExternalOutput").ap(),
            "d_ffnw": nc.dram_tensor("d_ffnw", [128, 3 * 2048], BF16, kind="ExternalOutput").ap(),
            "d_zfeat": nc.dram_tensor("d_zfeat", [128, 8 * T], BF16, kind="ExternalOutput").ap(),
            "d_lrb": nc.dram_tensor("d_lrb", [128, 2048], F32, kind="ExternalOutput").ap(),
            "d_wot": nc.dram_tensor("d_wot", [128, 8192], BF16, kind="ExternalOutput").ap(),
            "d_xin0": nc.dram_tensor("d_xin0", [128, 2048], F32, kind="ExternalOutput").ap(),
            "d_xn0": nc.dram_tensor("d_xn0", [128, 2048], BF16, kind="ExternalOutput").ap(),
            "d_stats": nc.dram_tensor("d_stats", [128, 16], F32, kind="ExternalOutput").ap(),
            "d_xnf_e": nc.dram_tensor("d_xnf_e", [128, 8 * T], BF16, kind="ExternalOutput").ap(),
            "d_kt_e": nc.dram_tensor("d_kt_e", [128, 8 * T], BF16, kind="ExternalOutput").ap(),
            "d_vdbg": nc.dram_tensor("d_vdbg", [128, 48], F32, kind="ExternalOutput").ap(),
            "d_lrow": nc.dram_tensor("d_lrow", [33, 1024], F32, kind="ExternalOutput").ap(),
            "d_lcols2": nc.dram_tensor("d_lcols2", [128, 16], F32, kind="ExternalOutput").ap(),
            "d_ubuf": nc.dram_tensor("d_ubuf", [128, 2048], BF16, kind="ExternalOutput").ap(),
            "d_zps": nc.dram_tensor("d_zps", [128, 2048], BF16, kind="ExternalOutput").ap(),
        }

    identd = nc.inline_tensor(np.eye(128, dtype=BF), "identc").ap()
    maskTd = nc.inline_tensor(
        np.triu(np.full((128, 128), -1e10, np.float32), k=1).astype(BF), "maskTc").ap()
    onesd = nc.inline_tensor(np.ones((1, 128), dtype=BF), "onesc").ap()
    rcol_np = np.zeros((32, 512), np.float32)
    rcol_np[np.arange(512) // 16, np.arange(512)] = 1.0
    rcold = nc.inline_tensor(rcol_np.astype(BF), "rcolc").ap()
    rcole_np = np.zeros((16, 512), np.float32)
    rcole_np[np.arange(512) % 16, np.arange(512)] = 1.0
    rcoled = nc.inline_tensor(rcole_np.astype(BF), "rcolec").ap()
    actc_np = np.tile(np.array([[1e-5, EXP_BIAS]], np.float32), (128, 1))
    actcd = nc.inline_tensor(actc_np, "actcc").ap()

    xnfeat = nc.alloc_sbuf_tensor("xnfeat", [128, 8 * T], BF16)
    QT = nc.alloc_sbuf_tensor("QT", [128, 8 * T], BF16)
    KT = nc.alloc_sbuf_tensor("KT", [128, 8 * T], BF16)
    V = nc.alloc_sbuf_tensor("V", [128, NT * NH * 65], BF16)
    WQT = nc.alloc_sbuf_tensor("WQT", [128, 8192], BF16)
    WKT = nc.alloc_sbuf_tensor("WKT", [128, 8192], BF16)
    WVT = nc.alloc_sbuf_tensor("WVT", [128, 8192], BF16)
    FFNW = nc.alloc_sbuf_tensor("FFNW", [128, 3 * 2048], BF16)
    ubuf = nc.alloc_sbuf_tensor("ubuf", [128, 2 * 1024], BF16)
    wraw = nc.alloc_sbuf_tensor("wraw", [128, 2 * 1024], BF16)
    bscr = nc.alloc_sbuf_tensor("bscr", [128, 1024], F32)
    identsb = nc.alloc_sbuf_tensor("identsb", [128, 128], BF16)
    maskTsb = nc.alloc_sbuf_tensor("maskTsb", [128, 128], BF16)
    onessb = nc.alloc_sbuf_tensor("onessb", [1, 128], BF16)
    rcolsb = nc.alloc_sbuf_tensor("rcolsb", [32, 512], BF16)
    rcolesb = nc.alloc_sbuf_tensor("rcolesb", [16, 512], BF16)
    actcsb = nc.alloc_sbuf_tensor("actcsb", [128, 2], F32)
    g1sb = nc.alloc_sbuf_tensor("g1sb", [32, 2, 4, 32], BF16)
    g2sb = nc.alloc_sbuf_tensor("g2sb", [16, 2, 4, 16], BF16)
    brsb = nc.alloc_sbuf_tensor("brsb", [1, 512], BF16)
    bisb = nc.alloc_sbuf_tensor("bisb", [1, 512], BF16)
    stats = nc.alloc_sbuf_tensor("stats", [128, 16], F32)
    g2exp = nc.alloc_sbuf_tensor("g2exp", [16, 4, 128], BF16)
    vdbg = nc.alloc_sbuf_tensor("vdbg", [128, 48], F32)

    zfeat = nc.alloc_sbuf_tensor_at("zfeat", [128, 8 * T], BF16,
                                    offset=nc.lookup_mloc(QT).addr)
    WOT = nc.alloc_sbuf_tensor_at("WOT", [128, 8192], BF16,
                                  offset=nc.lookup_mloc(WVT).addr)
    wqt_addr = nc.lookup_mloc(WQT).addr
    Lrow = nc.alloc_sbuf_tensor_at("Lrow", [33, 1024], F32, offset=wqt_addr)
    ubig = nc.alloc_sbuf_tensor_at("ubig", [128, 2 * 2048], BF16, offset=wqt_addr + 8192)
    outb = nc.alloc_sbuf_tensor_at("outb", [128, 2 * 1024], F32, offset=wqt_addr)
    wkt_addr = nc.lookup_mloc(WKT).addr
    Lrb = nc.alloc_sbuf_tensor_at("Lrb", [128, 2048], F32, offset=wkt_addr + 8192)
    t2sb = nc.alloc_sbuf_tensor_at("t2sb", [128, 512], F32, offset=wkt_addr + 8192)
    xin3 = nc.alloc_sbuf_tensor_at("xin3", [128, 2 * 1024], F32, offset=wkt_addr)
    Lcols = nc.alloc_sbuf_tensor("Lcols", [128, 16], F32)
    Lcols2 = nc.alloc_sbuf_tensor("Lcols2", [128, 16], F32)
    bnst = nc.alloc_sbuf_tensor("bnst", [128, 24], F32)
    xin0 = nc.alloc_sbuf_tensor_at("xin0", [128, 2 * 1024], F32,
                                   offset=nc.lookup_mloc(FFNW).addr)
    xn0 = nc.alloc_sbuf_tensor_at("xn0", [128, 2 * 1024], BF16,
                                  offset=nc.lookup_mloc(QT).addr)
    wraw4 = nc.alloc_sbuf_tensor_at("wraw4", [128, 4 * 1024], BF16,
                                    offset=nc.lookup_mloc(KT).addr)
    tmpsb = nc.alloc_sbuf_tensor_at("tmpsb", [128, 1024], F32,
                                    offset=nc.lookup_mloc(wraw).addr)
    g1exp = nc.alloc_sbuf_tensor_at("g1exp", [32, 4, 512], BF16,
                                    offset=nc.lookup_mloc(ubuf).addr)
    LrowB = nc.alloc_sbuf_tensor_at("LrowB", [33, 1024], F32,
                                    offset=nc.lookup_mloc(bscr).addr)

    AB4 = nc.alloc_psum_tensor("AB4", [128, 2048], F32)
    A2 = AB4[:, 0:1024]
    B2 = AB4[:, 1024:2048]
    Z = nc.alloc_psum_tensor("Z", [128, 2048], F32)
    A2b = A2.bitcast(BF16)
    B2b = B2.bitcast(BF16)

    sems = {}
    for name in ("s_x s_xB s_c s_c2 s_w s_wB s_wC s_wD s_wE s_wF s_lb s_xr s_out s_outB s_stats s_rstd s_ln s_tpw s_tpx "
                 "s_wde s_xne s_qkmm s_qkev s_vmm s_vev s_vones s_fbA s_fbB "
                 "s_fbC s_fbD s_fbP s_ffnw s_st s_exp s_zu s_zev s_lsc s_lrec s_lga s_norm "
                 "s_p3mm s_fin s_dve s_dbg s_raw s_rawA s_msk").split():
        sems[name] = nc.alloc_semaphore(name)
    S = type("S", (), sems)

    WTS = [("wq", WQT), ("wk", WKT), ("wv", WVT), ("wo", WOT)]
    wdrams = {"wq": wqd, "wk": wkd, "wv": wvd, "wo": wod}

    def wdest(wt, r):
        return wt[:].rearrange("p (dc col) -> p dc col", col=1024)[:, :, r * 128:(r + 1) * 128]

    def xnf(d):
        return xnfeat[:, d * T:(d + 1) * T]

    def bk(W, cf, c):
        return (W * 4 + cf) * 4 + c

    def pool_cnt_before(W, cf, c):
        full = (W * 4 + cf)
        return full * 3 + max(0, c - 1)

    st_halves = {}

    def st_slices(j, sub):
        qs, qe = qrange(j, sub)
        L = qe - qs
        out = []
        s = 0
        while s < L:
            e = min(L, s + 512)
            out.append((s, e))
            s = e
        return out

    def z_slices(j, sub):
        qs, qe = qrange(j, sub)
        out = []
        s = qs
        while s < qe:
            e = min(qe, (s // 512 + 1) * 512)
            out.append((s, e))
            s = e
        return out

    with nc.Block() as block:

        @block.sync
        def _(sync):
            sync.dma_start(out=identsb[:], in_=identd[:]).then_inc(S.s_c, 16)
            sync.dma_start(out=maskTsb[:], in_=maskTd[:]).then_inc(S.s_c, 16)
            sync.dma_start(out=onessb[:], in_=onesd[:]).then_inc(S.s_c, 16)
            sync.dma_start(out=rcolsb[:], in_=rcold[:]).then_inc(S.s_c, 16)
            sync.dma_start(out=rcolesb[:], in_=rcoled[:]).then_inc(S.s_c, 16)
            sync.dma_start(out=actcsb[:], in_=actcd[:]).then_inc(S.s_c, 16)
            for t in range(NT):
                if t >= 2:
                    sync.wait_ge(S.s_ln, t - 1)
                sync.dma_start(out=xin0[:, (t % 2) * 1024:(t % 2 + 1) * 1024],
                               in_=xd[t * 128:(t + 1) * 128, :]
                               ).then_inc(S.s_x if t % 2 == 0 else S.s_xB, 16)
            if dbg:
                sync.wait_ge(S.s_vmm, 16)
                sync.dma_start(out=dbg_out["d_xnf_e"], in_=xnfeat[:]).then_inc(S.s_c, 16)
                sync.dma_start(out=dbg_out["d_kt_e"], in_=KT[:]).then_inc(S.s_c, 16)
            def lrview(phi):
                sl = phi % 4
                base = Lrow if sl < 2 else LrowB
                co = 512 * (sl % 2)
                return base[0:33:32, co:co + 512].rearrange("o (p k) -> o p k", k=16)

            def lcols(phi, t2=False):
                ro = 64 * (phi % 2)
                return (Lcols2 if t2 else Lcols)[ro:ro + 64, :]

            def scat(phi):
                sync.wait_ge(S.s_zev, 2 * phi + 1)
                if phi >= 2:
                    sync.wait_ge(S.s_lrec, phi - 1)
                sync.dma_start(out=lcols(phi), in_=lrview(phi)).then_inc(S.s_lsc, 16)

            def gath_bcast(phi):
                p, Q = phi // 4, phi % 4
                base, co = (Lrow if phi % 4 < 2 else LrowB), 512 * (phi % 2)
                sync.wait_ge(S.s_lrec, phi + 1)
                if phi < ACT_TAIL:
                    sync.dma_start(out=lrview(phi), in_=lcols(phi, t2=True)
                                   ).then_inc(S.s_lga, 16)
                    sync.wait_ge(S.s_lga, 16 * (phi + 1))
                if phi >= 4:
                    sync.wait_ge(S.s_norm, 2 * (phi - 3))
                sync.dma_start(out=Lrb[0:64, 512 * Q:512 * (Q + 1)],
                               in_=base[0:1, co:co + 512].unsqueeze(1
                               ).broadcast_to([1, 64, 512])).then_inc(S.s_lb, 16)
                sync.dma_start(out=Lrb[64:128, 512 * Q:512 * (Q + 1)],
                               in_=base[32:33, co:co + 512].unsqueeze(1
                               ).broadcast_to([1, 64, 512])).then_inc(S.s_lb, 16)

            for phi in range(N_PHI):
                if phi < ACT_TAIL:
                    scat(phi)
                if phi >= 2:
                    gath_bcast(phi - 2)
            gath_bcast(N_PHI - 2)
            gath_bcast(N_PHI - 1)
            for t in range(NT):
                if t == 0:
                    sync.wait_ge(S.s_qkmm, 64)
                else:
                    sync.wait_ge(S.s_fin, t - 1)
                sync.dma_start(out=xin3[:, (t % 2) * 1024:(t % 2 + 1) * 1024],
                               in_=xd[t * 128:(t + 1) * 128, :]).then_inc(S.s_xr, 16)
                sync.wait_ge(S.s_fin, t + 1)
                sync.dma_start(out=yd[t * 128:(t + 1) * 128, :],
                               in_=outb[:, (t % 2) * 1024:(t % 2 + 1) * 1024]
                               ).then_inc(S.s_out if t % 2 == 0 else S.s_outB, 16)
            sync.wait_ge(S.s_outB, 128)
            if dbg:
                sync.wait_ge(S.s_out, 128)
                sync.dma_start(out=dbg_out["d_xnfeat"], in_=xnfeat[:]).then_inc(S.s_out, 16)
                sync.dma_start(out=dbg_out["d_kt"], in_=KT[:]).then_inc(S.s_out, 16)
                sync.dma_start(out=dbg_out["d_v"], in_=V[:]).then_inc(S.s_out, 16)
                sync.dma_start(out=dbg_out["d_ffnw"], in_=FFNW[:]).then_inc(S.s_out, 16)
                sync.dma_start(out=dbg_out["d_zfeat"], in_=zfeat[:]).then_inc(S.s_out, 16)
                sync.dma_start(out=dbg_out["d_lrb"], in_=Lrb[:]).then_inc(S.s_out, 16)
                sync.dma_start(out=dbg_out["d_wot"], in_=WOT[:]).then_inc(S.s_out, 16)
                sync.dma_start(out=dbg_out["d_xn0"], in_=xn0[:]).then_inc(S.s_out, 16)
                sync.dma_start(out=dbg_out["d_stats"], in_=stats[:]).then_inc(S.s_out, 16)
                sync.wait_ge(S.s_out, 128 + 9 * 16)
                sync.dma_start(out=dbg_out["d_vdbg"], in_=vdbg[:]).then_inc(S.s_c, 16)
                sync.dma_start(out=dbg_out["d_lrow"], in_=Lrow[:]).then_inc(S.s_c, 16)
                sync.dma_start(out=dbg_out["d_lcols2"], in_=Lcols2[:]).then_inc(S.s_c, 16)
                sync.dma_start(out=dbg_out["d_ubuf"], in_=ubuf[:]).then_inc(S.s_dbg, 16)
                sync.wait_ge(S.s_dve, 17)
                sync.dma_start(out=dbg_out["d_zps"], in_=ubuf[:]).then_inc(S.s_dbg, 16)
                sync.wait_ge(S.s_c, 96 + 3 * 16)
                sync.wait_ge(S.s_dbg, 32)
            else:
                sync.wait_ge(S.s_out, 128)

        @block.gpsimd
        def _(gp):
            gp.dma_start(out=g1sb[:, 0, :, :], in_=g1rd[:]).then_inc(S.s_c2, 16)
            gp.dma_start(out=g1sb[:, 1, :, :], in_=g1id[:]).then_inc(S.s_c2, 16)
            gp.dma_start(out=g2sb[:, 0, :, :],
                         in_=g2rd[:].rearrange("c e g -> e c g")).then_inc(S.s_c2, 16)
            gp.dma_start(out=g2sb[:, 1, :, :],
                         in_=g2id[:].rearrange("c e g -> e c g")).then_inc(S.s_c2, 16)
            gp.dma_start(out=brsb[:], in_=brd[:]).then_inc(S.s_c2, 16)
            gp.dma_start(out=bisb[:], in_=bid[:]).then_inc(S.s_c2, 16)
            gp.memset(V[:].rearrange("p (t h c) -> p t h c", h=NH, c=65)[:, :, :, 64:65],
                      1.0).then_inc(S.s_vones, 1)
            for i in range(24):
                wname = ("wq", "wk", "wv")[i // 8]
                r = i % 8
                if i >= 4:
                    gp.wait_ge(S.s_tpw, i - 3)
                gp.dma_start(out=wraw4[:, (i % 4) * 1024:(i % 4 + 1) * 1024],
                             in_=wdrams[wname][r * 128:(r + 1) * 128, :]
                             ).then_inc((S.s_w, S.s_wB, S.s_wC, S.s_wD)[i % 4], 16)
            for i in range(24, 32):
                k = i - 24
                gp.wait_ge(S.s_tpw, i - 1)
                gp.dma_start(out=wraw[:, (k % 2) * 1024:(k % 2 + 1) * 1024],
                             in_=wdrams["wo"][(i % 8) * 128:(i % 8 + 1) * 128, :]
                             ).then_inc((S.s_wE, S.s_wF)[k % 2], 16)
            for W in range(2):
                for cf in range(4):
                    for c in range(1, 4):
                        k = bk(W, cf, c)
                        gp.wait_ge(S.s_fbD, k + 1)
                        pn = pool_cnt_before(W, cf, c)
                        if pn > 0:
                            gp.wait_ge(S.s_fbP, pn)
                        accA = bscr[:, 0:512]
                        accB = bscr[:, 512:1024]
                        src = accA if c % 2 == 1 else accB
                        dst = accB if c % 2 == 1 else accA
                        tmp = tmpsb[:, (k % 2) * 512:(k % 2 + 1) * 512]
                        if c < 3:
                            gp.tensor_tensor(dst[:], src[:], tmp[:], OP.add
                                             ).then_inc(S.s_fbP, 1)
                        else:
                            wt_dst = FFNW[:, W * 2048 + cf * 512: W * 2048 + (cf + 1) * 512]
                            gp.tensor_tensor(wt_dst, src[:], tmp[:], OP.add
                                             ).then_inc(S.s_fbP, 1)
                            if W == 1:
                                gp.wait_ge(S.s_fbP, pn + 1)
                                gp.tensor_scalar_mul(
                                    FFNW[:, 2 * 2048 + cf * 512: 2 * 2048 + (cf + 1) * 512],
                                    wt_dst, -1.0).then_inc(S.s_ffnw, 1)
            for phi in range(N_PHI):
                p, Q = phi // 4, phi % 4
                gp.wait_ge(S.s_lb, 32 * (phi + 1))
                gp.wait_ge(S.s_zev, 2 * phi + 2)
                for h01 in (0, 1):
                    h = 2 * p + h01
                    rows = slice((h % 2) * 64, (h % 2) * 64 + 64)
                    zsl = zfeat[rows, p * T + 512 * Q: p * T + 512 * (Q + 1)]
                    gp.tensor_tensor(zsl, zsl, Lrb[rows, 512 * Q:512 * (Q + 1)],
                                     OP.mult).then_inc(S.s_norm, 1)

        @block.vector
        def _(vec):
            raw = [0]

            def w_evac(b):
                wname, wt = WTS[b // 8]
                r = b % 8
                vec.wait_ge(S.s_tpw, b + 1)
                src = (A2b if b % 2 == 0 else B2b)[:, 0:1024].rearrange(
                    "p (dc col) -> p dc col", col=128)
                vec.tensor_copy(wdest(wt, r), src).then_inc(S.s_wde, 1)

            for t in range(NT):
                xt = xin0[:, (t % 2) * 1024:(t % 2 + 1) * 1024]
                xnt = xn0[:, (t % 2) * 1024:(t % 2 + 1) * 1024]
                o = (t % 2) * 8
                m = stats[:, o + 3:o + 4]
                var = stats[:, o + 4:o + 5]
                rstd = stats[:, o + 5:o + 6]
                negmr = stats[:, o + 6:o + 7]
                vec.wait_ge(S.s_x if t % 2 == 0 else S.s_xB, 16 * (t // 2 + 1))
                if t >= 2:
                    vec.wait_ge(S.s_tpx, t - 1)
                    vec.wait_ge(S.s_rstd, t - 1)
                o3 = (t % 2) * 12
                vec.bn_stats(bnst[:, o3:o3 + 6], xt[:, 0:512]).then_inc(S.s_raw, 1)
                raw[0] += 1
                vec.bn_stats(bnst[:, o3 + 6:o3 + 12], xt[:, 512:1024]
                             ).then_inc(S.s_raw, 1)
                raw[0] += 1
                vec.wait_ge(S.s_raw, raw[0])
                vec.tensor_copy(vdbg[:, 30:36], bnst[:, o3 + 3:o3 + 9])
                vec.bn_aggr(stats[:, o + 3:o + 5],
                            bnst[:, o3:o3 + 12].rearrange("p (g s) -> p g s", s=6)
                            ).then_inc(S.s_raw, 1)
                raw[0] += 1
                vec.wait_ge(S.s_raw, raw[0])
                vec.tensor_copy(vdbg[:, t % 2:t % 2 + 2], stats[:, o + 3:o + 5]
                                ).then_inc(S.s_stats, 1)
                vec.wait_ge(S.s_rstd, t + 1)
                vec.tensor_copy(vdbg[:, 4 + t % 2:5 + t % 2], rstd)
                vec.scalar_tensor_tensor(negmr, m, -1.0, rstd, OP.mult, OP.mult
                                         ).then_inc(S.s_raw, 1)
                raw[0] += 1
                vec.wait_ge(S.s_raw, raw[0])
                vec.tensor_copy(vdbg[:, 6 + t % 2:7 + t % 2], negmr)
                vec.tensor_scalar(xnt, xt, rstd, negmr, OP.mult, OP.add
                                  ).then_inc(S.s_ln, 1)
                for te_ in ([t - 2] if t >= 2 else []) + ([14, 15] if t == 15 else []):
                    vec.wait_ge(S.s_tpx, te_ + 1)
                    src = (A2b if te_ % 2 == 0 else B2b)[:, 1024:2048].rearrange(
                        "p (dc col) -> p dc col", col=128)
                    dst = xnfeat[:].rearrange("p (dc col) -> p dc col", col=T
                                              )[:, :, te_ * 128:(te_ + 1) * 128]
                    vec.tensor_copy(dst, src).then_inc(S.s_xne, 1)
                    if te_ < 16:
                        w_evac(te_)
                if t == 15:
                    for b_ in range(16, 24):
                        w_evac(b_)
            for g in range(16):
                tname, dst_t = ("Q", QT) if g < 8 else ("K", KT)
                f = g % 8
                for tb in range(4):
                    vec.wait_ge(S.s_qkmm, 4 * g + tb + 1)
                    grp = (A2, A2, B2, B2) if g % 2 == 0 else (Z, Z, Z, Z)
                    if g % 2 == 0:
                        src = grp[tb][:, (tb % 2) * 512:(tb % 2 + 1) * 512]
                    else:
                        src = Z[:, tb * 512:(tb + 1) * 512]
                    vec.tensor_copy(dst_t[:, f * T + tb * 512: f * T + (tb + 1) * 512],
                                    src).then_inc(S.s_qkev, 1)
            for j in range(NT):
                vec.wait_ge(S.s_vmm, j + 1)
                ps = A2 if j % 2 == 0 else B2
                src = ps[:].rearrange("p (h c) -> p h c", c=64)
                dst = V[:, j * 1040:(j + 1) * 1040].rearrange(
                    "p (h c) -> p h c", c=65)[:, :, 0:64]
                vec.tensor_copy(dst, src).then_inc(S.s_vev, 1)
            for b in range(24, 32):
                wname, wt = WTS[b // 8]
                r = b % 8
                vec.wait_ge(S.s_tpw, b + 1)
                src = (A2b if b % 2 == 0 else B2b)[:, 0:1024].rearrange(
                    "p (dc col) -> p dc col", col=128)
                vec.tensor_copy(wdest(wt, r), src).then_inc(S.s_wde, 1)
            vec.wait_ge(S.s_c, 96)
            vec.wait_ge(S.s_c2, 96)
            for W in range(2):
                if W == 1:
                    vec.wait_ge(S.s_fbC, 16)
                    vec.wait_ge(S.s_fbA, 16)
                for c in range(4):
                    vec.tensor_copy(
                        g1exp[:, c, :].rearrange("b (d g) -> b d g", g=16),
                        g1sb[:, W, c, :].unsqueeze(2).broadcast_to([32, 32, 16]))
                    ins = vec.tensor_copy(
                        g2exp[:, c, :].rearrange("e (d g) -> e d g", g=16),
                        g2sb[:, W, c, :].unsqueeze(1).broadcast_to([16, 8, 16]))
                    if c == 3:
                        ins.then_inc(S.s_fbB, 1)
                for cf in range(4):
                    for c in range(4):
                        k = bk(W, cf, c)
                        vec.wait_ge(S.s_fbA, k + 1)
                        if k >= 1:
                            vec.wait_ge(S.s_fbD, k)
                        vec.tensor_copy(t2sb[:], Z[:, (k % 4) * 512:(k % 4 + 1) * 512]
                                        ).then_inc(S.s_fbB, 1)
                        vec.wait_ge(S.s_fbB, (k + 2) if k < 16 else (k + 3))
                        vec.wait_ge(S.s_fbC, k + 1)
                        ps1 = A2[:, (k % 2) * 512:(k % 2 + 1) * 512]
                        if c == 0:
                            dst = bscr[:, 0:512]
                        else:
                            dst = tmpsb[:, (k % 2) * 512:(k % 2 + 1) * 512]
                        pcb = pool_cnt_before(W, cf, c)
                        if pcb > 0:
                            vec.wait_ge(S.s_fbP, pcb)
                        vec.tensor_tensor(dst, t2sb[:], ps1, OP.mult
                                          ).then_inc(S.s_fbD, 1)
            def lrow_slot(phi):
                sl = phi % 4
                base = Lrow if sl < 2 else LrowB
                co = 512 * (sl % 2)
                return base, co

            def evac_phi(phi):
                p, Q = phi // 4, phi % 4
                lrb_, co = lrow_slot(phi)
                zo = 1024 * (phi % 2)
                si_last = STEP_IDX[(p, Q, 4 * Q + 3)]
                vec.wait_ge(S.s_zu, 2 * (si_last + 1))
                if phi in (2, 3):
                    vec.wait_ge(S.s_fbP, 24)
                if phi >= 4:
                    vec.wait_ge(S.s_lb, 32 * (phi - 3))
                vec.tensor_copy(lrb_[0:1, co:co + 512], Z[64:65, zo:zo + 512]
                                ).then_inc(S.s_raw, 1)
                raw[0] += 1
                vec.wait_ge(S.s_raw, raw[0])
                vec.tensor_copy(lrb_[32:33, co:co + 512], Z[64:65, zo + 512:zo + 1024]
                                ).then_inc(S.s_raw, 1)
                raw[0] += 1
                vec.wait_ge(S.s_raw, raw[0])
                vec.tensor_copy(vdbg[0:1, 46:47], lrb_[32:33, co + 511:co + 512]
                                ).then_inc(S.s_zev, 1)
                for h01 in (0, 1):
                    h = 2 * p + h01
                    rows = slice((h % 2) * 64, (h % 2) * 64 + 64)
                    zt = Z[0:64, zo + h01 * 512: zo + (h01 + 1) * 512]
                    vec.tensor_copy(zfeat[rows, p * T + 512 * Q: p * T + 512 * (Q + 1)],
                                    zt).then_inc(S.s_raw, 1)
                    raw[0] += 1
                    vec.wait_ge(S.s_raw, raw[0])
                vec.tensor_copy(vdbg[0:64, 44:45],
                                zfeat[0:64, p * T + 512 * Q + 511: p * T + 512 * Q + 512]
                                ).then_inc(S.s_zev, 1)

            def recip_phi(phi):
                ro = 64 * (phi % 2)
                vec.wait_ge(S.s_lsc, 16 * (phi + 1))
                if phi >= 2:
                    vec.wait_ge(S.s_lga, 16 * (phi - 1))
                vec.reciprocal(Lcols2[ro:ro + 64, :], Lcols[ro:ro + 64, :]
                               ).then_inc(S.s_raw, 1)
                raw[0] += 1
                vec.wait_ge(S.s_raw, raw[0])
                vec.tensor_copy(vdbg[ro:ro + 64, 47:48], Lcols2[ro:ro + 64, 15:16]
                                ).then_inc(S.s_lrec, 1)

            for phi in range(N_PHI):
                evac_phi(phi)
                if phi >= 2 and phi - 2 < ACT_TAIL:
                    recip_phi(phi - 2)
            for t in range(NT):
                vec.wait_ge(S.s_p3mm, t + 1)
                vec.wait_ge(S.s_xr, 16 * (t + 1))
                if t >= 2:
                    vec.wait_ge(S.s_out if t % 2 == 0 else S.s_outB,
                                16 * (t // 2))
                if t == 0:
                    vec.wait_ge(S.s_lb, 32 * N_PHI)
                ps = A2 if t % 2 == 0 else B2
                vec.tensor_tensor(outb[:, (t % 2) * 1024:(t % 2 + 1) * 1024],
                                  xin3[:, (t % 2) * 1024:(t % 2 + 1) * 1024],
                                  ps[:], OP.add).then_inc(S.s_raw, 1)
                raw[0] += 1
                vec.wait_ge(S.s_raw, raw[0])
                vec.tensor_copy(vdbg[:, 45:46],
                                outb[:, (t % 2) * 1024 + 1023:(t % 2) * 1024 + 1024]
                                ).then_inc(S.s_fin, 1)
            if dbg:
                vec.wait_ge(S.s_dbg, 16)
                vec.tensor_copy(ubuf[0:65, 0:2048], Z[0:65, :]).then_inc(S.s_dve, 1)

        @block.scalar
        def _(sc):
            rawA = [0]
            AB4r = AB4[:].rearrange("p (hh q) -> p hh q", q=512)
            sc.wait_ge(S.s_c, 96)
            sc.wait_ge(S.s_c2, 96)
            for t in range(NT):
                o = (t % 2) * 8
                var = stats[:, o + 4:o + 5]
                rstd = stats[:, o + 5:o + 6]
                lnv = stats[:, o + 7:o + 8]
                sc.wait_ge(S.s_stats, t + 1)
                sc.activation(lnv, var, AF.Ln, bias=actcsb[:, 0:1], scale=1.0
                              ).then_inc(S.s_rawA, 1)
                rawA[0] += 1
                sc.wait_ge(S.s_rawA, rawA[0])
                sc.activation(rstd, lnv, AF.Exp, scale=-0.5).then_inc(S.s_rawA, 1)
                rawA[0] += 1
                sc.wait_ge(S.s_rawA, rawA[0])
                sc.copy(vdbg[:, 42:43], rstd).then_inc(S.s_rstd, 1)
            last_z_cnt_a = {}
            for si in range(N_STEPS):
                p, Q, j = STEPS[si]
                qs, qe = qrange2(Q, j)
                L = qe - qs
                pg = si % 2
                sc.wait_ge(S.s_st, si + 1)
                for slot_h in (0, 1):
                    key = (j % 4) * 2 + slot_h
                    if key in last_z_cnt_a:
                        sc.wait_ge(S.s_zu, last_z_cnt_a[key])
                    last_z_cnt_a[key] = 2 * si + 1 + slot_h
                src = AB4r[:, pg:pg + 3:2, 0:L]
                dst = ubig[:, (j % 4) * 1024:(j % 4) * 1024 + 1024].rearrange(
                    "p (hh q) -> p hh q", q=512)[:, :, 0:L]
                sc.activation(dst, src, AF.Exp, bias=actcsb[:, 1:2],
                              scale=0.125).then_inc(S.s_exp, 1)
            for phi in range(ACT_TAIL, N_PHI):
                base, co = (Lrow if phi % 4 < 2 else LrowB), 512 * (phi % 2)
                sc.wait_ge(S.s_zev, 2 * phi + 1)
                sc.activation(ubuf[0:33, 0:512], base[0:33, co:co + 512], AF.Ln
                              ).then_inc(S.s_rawA, 1)
                rawA[0] += 1
                sc.wait_ge(S.s_rawA, rawA[0])
                sc.activation(base[0:33, co:co + 512], ubuf[0:33, 0:512], AF.Exp,
                              scale=-1.0).then_inc(S.s_lrec, 1)

        @block.tensor
        def _(te):
            te.wait_ge(S.s_c, 96)
            te.wait_ge(S.s_c2, 96)

            def xn_tr(t):
                te.wait_ge(S.s_ln, t + 1)
                if t >= 2:
                    te.wait_ge(S.s_xne, t - 1)
                dstp = (A2b if t % 2 == 0 else B2b)
                for dc in range(8):
                    ins = te.transpose(
                        dstp[:, 1024 + dc * 128:1024 + (dc + 1) * 128],
                        xn0[:, (t % 2) * 1024 + dc * 128:(t % 2) * 1024 + (dc + 1) * 128],
                        identsb[:])
                    if dc == 7:
                        ins.then_inc(S.s_tpx, 1)

            def w_tr(b):
                te.wait_ge((S.s_w, S.s_wB, S.s_wC, S.s_wD)[b % 4], 16 * (b // 4 + 1))
                if b >= 2:
                    te.wait_ge(S.s_wde, b - 1)
                dstp = (A2b if b % 2 == 0 else B2b)
                for dc in range(8):
                    ins = te.transpose(
                        dstp[:, dc * 128:(dc + 1) * 128],
                        wraw4[:, (b % 4) * 1024 + dc * 128:(b % 4) * 1024 + (dc + 1) * 128],
                        identsb[:])
                    if dc == 7:
                        ins.then_inc(S.s_tpw, 1)

            for t in range(NT):
                xn_tr(t)
                if t < 16:
                    w_tr(t)
            for b in range(16, 24):
                w_tr(b)
            te.wait_ge(S.s_xne, 16)
            te.wait_ge(S.s_wde, 24)
            for g in range(16):
                wt = WQT if g < 8 else WKT
                f = g % 8
                if g >= 2:
                    te.wait_ge(S.s_qkev, 4 * (g - 1))
                for d in range(8):
                    for tb in range(4):
                        if g % 2 == 0:
                            ps = (A2 if tb < 2 else B2)[:, (tb % 2) * 512:(tb % 2 + 1) * 512]
                        else:
                            ps = Z[:, tb * 512:(tb + 1) * 512]
                        ins = te.matmul(ps,
                                        wt[:, d * 1024 + f * 128:d * 1024 + (f + 1) * 128],
                                        xnf(d)[:, tb * 512:(tb + 1) * 512],
                                        start=(d == 0), stop=(d == 7))
                        if d == 7:
                            ins.then_inc(S.s_qkmm, 1)
            te.wait_ge(S.s_wde, 24)
            te.wait_ge(S.s_qkev, 64)
            for j in range(NT):
                if j >= 2:
                    te.wait_ge(S.s_vev, j - 1)
                ps = A2 if j % 2 == 0 else B2
                for d in range(8):
                    for nb in range(2):
                        ins = te.matmul(ps[:, nb * 512:(nb + 1) * 512],
                                        xnf(d)[:, j * 128:(j + 1) * 128],
                                        WVT[:, d * 1024 + nb * 512:d * 1024 + (nb + 1) * 512],
                                        start=(d == 0), stop=(d == 7))
                        if d == 7 and nb == 1:
                            ins.then_inc(S.s_vmm, 1)
            for b in range(24, 32):
                k = b - 24
                te.wait_ge((S.s_wE, S.s_wF)[k % 2], 16 * (k // 2 + 1))
                te.wait_ge(S.s_wde, b - 1)
                if b == 24:
                    te.wait_ge(S.s_vev, 15)
                if b == 25:
                    te.wait_ge(S.s_vev, 16)
                dstp = (A2b if b % 2 == 0 else B2b)
                for dc in range(8):
                    ins = te.transpose(
                        dstp[:, dc * 128:(dc + 1) * 128],
                        wraw[:, (k % 2) * 1024 + dc * 128:(k % 2) * 1024 + (dc + 1) * 128],
                        identsb[:])
                    if dc == 7:
                        ins.then_inc(S.s_tpw, 1)
            te.wait_ge(S.s_wde, 32)
            for W in range(2):
                te.wait_ge(S.s_fbB, W * 17 + 1)
                for cf in range(4):
                    for c in range(4):
                        k = bk(W, cf, c)
                        if k >= 4:
                            te.wait_ge(S.s_fbB, (k - 4) + 2 + (1 if k - 4 >= 16 else 0))
                        te.matmul(Z[:, (k % 4) * 512:(k % 4 + 1) * 512],
                                  g2exp[:, c, :], rcolesb[:], start=True, stop=True
                                  ).then_inc(S.s_fbA, 1)
                        if k >= 2:
                            te.wait_ge(S.s_fbD, k - 1)
                        te.matmul(A2[:, (k % 2) * 512:(k % 2 + 1) * 512],
                                  g1exp[:, c, cf * 128:(cf + 1) * 128],
                                  rcolsb[:], start=True, stop=True
                                  ).then_inc(S.s_fbC, 1)
            te.wait_ge(S.s_vones, 1)
            te.wait_ge(S.s_vev, 16)
            AB4r = AB4[:].rearrange("p (hh q) -> p hh q", q=512)

            def emit_st_pair(si):
                p, Q, j = STEPS[si]
                diag = step_diag(p, Q, j)
                pg = si % 2
                if si >= 2:
                    te.wait_ge(S.s_exp, si - 1)
                if si == 0:
                    te.wait_ge(S.s_fbD, 32)
                qs, qe = qrange2(Q, j)
                L = qe - qs
                lhs0 = KT[0:64, p * T + j * 128: p * T + (j + 1) * 128]
                lhs1 = KT[64:128, p * T + j * 128: p * T + (j + 1) * 128]
                te.matmul(AB4[:, pg * 512: pg * 512 + L], lhs0,
                          QT[0:64, p * T + qs: p * T + qe],
                          start=True, stop=not diag, tile_position=(0, 0))
                ins = te.matmul(AB4[:, 1024 + pg * 512: 1024 + pg * 512 + L], lhs1,
                                QT[64:128, p * T + qs: p * T + qe],
                                start=True, stop=not diag, tile_position=(64, 0))
                if diag:
                    te.matmul(AB4[:, pg * 512: pg * 512 + 128], maskTsb[:], identsb[:],
                              start=False, stop=True)
                    ins = te.matmul(AB4[:, 1024 + pg * 512: 1024 + pg * 512 + 128],
                                    maskTsb[:], identsb[:], start=False, stop=True)
                ins.then_inc(S.s_st, 1)

            def emit_z_pair(si):
                p, Q, j = STEPS[si]
                phi = 4 * p + Q
                qs, qe = qrange2(Q, j)
                L = qe - qs
                te.wait_ge(S.s_exp, si + 1)
                zo = 1024 * (phi % 2)
                for h01 in (0, 1):
                    if j == 0 and h01 == 0 and phi <= 1:
                        te.wait_ge(S.s_fbB, 34)
                    if j == 0 and h01 == 0 and phi >= 2:
                        te.wait_ge(S.s_zev, 2 * (phi - 1))
                    h = 2 * p + h01
                    lhs = V[:, j * 1040 + h * 65: j * 1040 + (h + 1) * 65]
                    uoff = ((j % 4) * 2 + h01) * 512
                    lloc = qs - 512 * Q
                    te.matmul(Z[0:65, zo + h01 * 512 + lloc: zo + h01 * 512 + lloc + L],
                              lhs, ubig[:, uoff: uoff + L],
                              start=(j == 0), stop=(j == 4 * Q + 3)
                              ).then_inc(S.s_zu, 1)

            emit_st_pair(0)
            emit_st_pair(1)
            for si in range(N_STEPS):
                if si + 2 < N_STEPS:
                    emit_st_pair(si + 2)
                emit_z_pair(si)
            te.wait_ge(S.s_exp, N_STEPS)
            te.wait_ge(S.s_fbP, 24)
            te.wait_ge(S.s_ffnw, 4)

            def ffn_mms(t):
                ps = A2 if t % 2 == 0 else B2
                for c in range(8):
                    lhs = xnf(c)[:, t * 128:(t + 1) * 128]
                    if c < 4:
                        rhsR = FFNW[:, 0 * 2048 + c * 512:0 * 2048 + (c + 1) * 512]
                        rhsI = FFNW[:, 1 * 2048 + c * 512:1 * 2048 + (c + 1) * 512]
                    else:
                        rhsR = FFNW[:, 2 * 2048 + (c - 4) * 512:2 * 2048 + (c - 3) * 512]
                        rhsI = FFNW[:, 0 * 2048 + (c - 4) * 512:0 * 2048 + (c - 3) * 512]
                    te.matmul(ps[:, 0:512], lhs, rhsR, start=(c == 0), stop=False)
                    te.matmul(ps[:, 512:1024], lhs, rhsI, start=(c == 0), stop=False)

            def wo_mms(t, fcs, gate):
                ps = A2 if t % 2 == 0 else B2
                for fc in fcs:
                    if gate:
                        te.wait_ge(S.s_norm, 8 * (fc + 1))
                    lhs = zfeat[:, fc * T + t * 128: fc * T + (t + 1) * 128]
                    for nb in range(2):
                        te.matmul(ps[:, nb * 512:(nb + 1) * 512], lhs,
                                  WOT[:, fc * 1024 + nb * 512:fc * 1024 + (nb + 1) * 512],
                                  start=False, stop=False)

            def bias_mms(t):
                ps = A2 if t % 2 == 0 else B2
                te.matmul(ps[:, 0:512], onessb[:], brsb[:], start=False, stop=True)
                te.matmul(ps[:, 512:1024], onessb[:], bisb[:], start=False, stop=True
                          ).then_inc(S.s_p3mm, 1)

            ffn_mms(0)
            wo_mms(0, range(0, 4), gate=True)
            ffn_mms(1)
            wo_mms(0, range(4, 8), gate=True)
            bias_mms(0)
            wo_mms(1, range(8), gate=False)
            bias_mms(1)
            for t in range(2, NT):
                te.wait_ge(S.s_fin, t - 1)
                ffn_mms(t)
                wo_mms(t, range(8), gate=False)
                bias_mms(t)

    return nc


def kernel(L0_attn_x, L0_attn_W_K, L0_attn_W_Q, L0_attn_W_V, L0_attn_W_O,
           L0_ffn_G1_R, L0_ffn_G2_R, L0_ffn_G1_I, L0_ffn_G2_I,
           L0_ffn_bias_real, L0_ffn_bias_imag):
    x = np.ascontiguousarray(np.asarray(L0_attn_x, dtype=np.float32))
    wq = np.ascontiguousarray(np.asarray(L0_attn_W_Q, np.float32).reshape(D, D))
    wk = np.ascontiguousarray(np.asarray(L0_attn_W_K, np.float32).reshape(D, D))
    wv = np.ascontiguousarray(np.asarray(L0_attn_W_V, np.float32).reshape(D, D))
    wo = np.ascontiguousarray(np.asarray(L0_attn_W_O, np.float32))
    g1r = np.ascontiguousarray(np.asarray(L0_ffn_G1_R, np.float32).reshape(32, 4, 32))
    g1i = np.ascontiguousarray(np.asarray(L0_ffn_G1_I, np.float32).reshape(32, 4, 32))
    g2r = np.ascontiguousarray(np.asarray(L0_ffn_G2_R, np.float32).reshape(4, 16, 16))
    g2i = np.ascontiguousarray(np.asarray(L0_ffn_G2_I, np.float32).reshape(4, 16, 16))
    br = np.ascontiguousarray(np.asarray(L0_ffn_bias_real, np.float32).reshape(1, 512))
    bi = np.ascontiguousarray(np.asarray(L0_ffn_bias_imag, np.float32).reshape(1, 512))

    nc = build_nc()

    in_maps = []
    for b in range(8):
        in_maps.append(dict(x=x[b], wq=wq, wk=wk, wv=wv, wo=wo,
                            g1r=g1r, g1i=g1i, g2r=g2r, g2i=g2i, br=br, bi=bi))
    res = run_bass_kernel_spmd(nc, in_maps, core_ids=list(range(8)))
    out = np.stack([np.asarray(res.results[b]["y"]) for b in range(8)], axis=0)
    return out.astype(np.float32)


# version 31
# speedup vs baseline: 1.0778x; 1.0029x over previous
import numpy as np
import ml_dtypes
import concourse.bass as bass
import concourse.mybir as mybir
from concourse.bass_utils import run_bass_kernel_spmd

F32 = mybir.dt.float32
BF16 = mybir.dt.bfloat16
AF = mybir.ActivationFunctionType
AX = mybir.AxisListType
OP = mybir.AluOpType

T = 2048
D = 1024
NT = 16
NH = 16
DH = 64
EXP_BIAS = -12.0

BF = ml_dtypes.bfloat16


def subs_of(j):
    return 2 if j < 8 else 1


def ss_of(h, j, sub):
    return 24 * h + (2 * j if j < 8 else 8 + j) + sub


TOTAL_SS = 384

STEPS = []
for _p in range(8):
    for _Q in range(4):
        for _j in range(4 * _Q + 4):
            STEPS.append((_p, _Q, _j))
N_STEPS = len(STEPS)
STEP_IDX = {k: i for i, k in enumerate(STEPS)}
N_PHI = 32
ACT_TAIL = 28


def step_diag(p, Q, j):
    return j // 4 == Q


def qrange2(Q, j):
    qs = max(128 * j, 512 * Q)
    qe = 512 * (Q + 1)
    return qs, qe


def qrange(j, sub):
    qlo = 128 * j
    qs = qlo + 1024 * sub
    qe = min(2048, qs + 1024)
    return qs, qe


def build_nc(dbg=False):
    nc = bass.Bass("TRN2", target_bir_lowering=False, dynamic_dma_scratch_size=16384)

    xd = nc.dram_tensor("x", [T, D], F32, kind="ExternalInput").ap()
    wqd = nc.dram_tensor("wq", [D, D], F32, kind="ExternalInput").ap()
    wkd = nc.dram_tensor("wk", [D, D], F32, kind="ExternalInput").ap()
    wvd = nc.dram_tensor("wv", [D, D], F32, kind="ExternalInput").ap()
    wod = nc.dram_tensor("wo", [D, D], F32, kind="ExternalInput").ap()
    g1rd = nc.dram_tensor("g1r", [32, 4, 32], F32, kind="ExternalInput").ap()
    g1id = nc.dram_tensor("g1i", [32, 4, 32], F32, kind="ExternalInput").ap()
    g2rd = nc.dram_tensor("g2r", [4, 16, 16], F32, kind="ExternalInput").ap()
    g2id = nc.dram_tensor("g2i", [4, 16, 16], F32, kind="ExternalInput").ap()
    brd = nc.dram_tensor("br", [1, 512], F32, kind="ExternalInput").ap()
    bid = nc.dram_tensor("bi", [1, 512], F32, kind="ExternalInput").ap()
    yd = nc.dram_tensor("y", [T, D], F32, kind="ExternalOutput").ap()
    if dbg:
        dbg_out = {
            "d_xnfeat": nc.dram_tensor("d_xnfeat", [128, 8 * T], BF16, kind="ExternalOutput").ap(),
            "d_kt": nc.dram_tensor("d_kt", [128, 8 * T], BF16, kind="ExternalOutput").ap(),
            "d_v": nc.dram_tensor("d_v", [128, NT * NH * 65], BF16, kind="ExternalOutput").ap(),
            "d_ffnw": nc.dram_tensor("d_ffnw", [128, 3 * 2048], BF16, kind="ExternalOutput").ap(),
            "d_zfeat": nc.dram_tensor("d_zfeat", [128, 8 * T], BF16, kind="ExternalOutput").ap(),
            "d_lrb": nc.dram_tensor("d_lrb", [128, 2048], F32, kind="ExternalOutput").ap(),
            "d_wot": nc.dram_tensor("d_wot", [128, 8192], BF16, kind="ExternalOutput").ap(),
            "d_xin0": nc.dram_tensor("d_xin0", [128, 2048], F32, kind="ExternalOutput").ap(),
            "d_xn0": nc.dram_tensor("d_xn0", [128, 2048], BF16, kind="ExternalOutput").ap(),
            "d_stats": nc.dram_tensor("d_stats", [128, 16], F32, kind="ExternalOutput").ap(),
            "d_xnf_e": nc.dram_tensor("d_xnf_e", [128, 8 * T], BF16, kind="ExternalOutput").ap(),
            "d_kt_e": nc.dram_tensor("d_kt_e", [128, 8 * T], BF16, kind="ExternalOutput").ap(),
            "d_vdbg": nc.dram_tensor("d_vdbg", [128, 48], F32, kind="ExternalOutput").ap(),
            "d_lrow": nc.dram_tensor("d_lrow", [33, 1024], F32, kind="ExternalOutput").ap(),
            "d_lcols2": nc.dram_tensor("d_lcols2", [128, 16], F32, kind="ExternalOutput").ap(),
            "d_ubuf": nc.dram_tensor("d_ubuf", [128, 2048], BF16, kind="ExternalOutput").ap(),
            "d_zps": nc.dram_tensor("d_zps", [128, 2048], BF16, kind="ExternalOutput").ap(),
        }

    identd = nc.inline_tensor(np.eye(128, dtype=BF), "identc").ap()
    maskTd = nc.inline_tensor(
        np.triu(np.full((128, 128), -1e10, np.float32), k=1).astype(BF), "maskTc").ap()
    onesd = nc.inline_tensor(np.ones((1, 128), dtype=BF), "onesc").ap()
    rcol_np = np.zeros((32, 512), np.float32)
    rcol_np[np.arange(512) // 16, np.arange(512)] = 1.0
    rcold = nc.inline_tensor(rcol_np.astype(BF), "rcolc").ap()
    rcole_np = np.zeros((16, 512), np.float32)
    rcole_np[np.arange(512) % 16, np.arange(512)] = 1.0
    rcoled = nc.inline_tensor(rcole_np.astype(BF), "rcolec").ap()
    actc_np = np.tile(np.array([[1e-5, EXP_BIAS]], np.float32), (128, 1))
    actcd = nc.inline_tensor(actc_np, "actcc").ap()

    xnfeat = nc.alloc_sbuf_tensor("xnfeat", [128, 8 * T], BF16)
    QT = nc.alloc_sbuf_tensor("QT", [128, 8 * T], BF16)
    KT = nc.alloc_sbuf_tensor("KT", [128, 8 * T], BF16)
    V = nc.alloc_sbuf_tensor("V", [128, NT * NH * 65], BF16)
    WQT = nc.alloc_sbuf_tensor("WQT", [128, 8192], BF16)
    WKT = nc.alloc_sbuf_tensor("WKT", [128, 8192], BF16)
    WVT = nc.alloc_sbuf_tensor("WVT", [128, 8192], BF16)
    FFNW = nc.alloc_sbuf_tensor("FFNW", [128, 3 * 2048], BF16)
    ubuf = nc.alloc_sbuf_tensor("ubuf", [128, 2 * 1024], BF16)
    wraw = nc.alloc_sbuf_tensor("wraw", [128, 2 * 1024], BF16)
    bscr = nc.alloc_sbuf_tensor("bscr", [128, 1024], F32)
    identsb = nc.alloc_sbuf_tensor("identsb", [128, 128], BF16)
    maskTsb = nc.alloc_sbuf_tensor("maskTsb", [128, 128], BF16)
    onessb = nc.alloc_sbuf_tensor("onessb", [1, 128], BF16)
    rcolsb = nc.alloc_sbuf_tensor("rcolsb", [32, 512], BF16)
    rcolesb = nc.alloc_sbuf_tensor("rcolesb", [16, 512], BF16)
    actcsb = nc.alloc_sbuf_tensor("actcsb", [128, 2], F32)
    g1sb = nc.alloc_sbuf_tensor("g1sb", [32, 2, 4, 32], BF16)
    g2sb = nc.alloc_sbuf_tensor("g2sb", [16, 2, 4, 16], BF16)
    brsb = nc.alloc_sbuf_tensor("brsb", [1, 512], BF16)
    bisb = nc.alloc_sbuf_tensor("bisb", [1, 512], BF16)
    stats = nc.alloc_sbuf_tensor("stats", [128, 16], F32)
    g2exp = nc.alloc_sbuf_tensor("g2exp", [16, 4, 128], BF16)
    vdbg = nc.alloc_sbuf_tensor("vdbg", [128, 48], F32)

    zfeat = nc.alloc_sbuf_tensor_at("zfeat", [128, 8 * T], BF16,
                                    offset=nc.lookup_mloc(QT).addr)
    WOT = nc.alloc_sbuf_tensor_at("WOT", [128, 8192], BF16,
                                  offset=nc.lookup_mloc(WVT).addr)
    wqt_addr = nc.lookup_mloc(WQT).addr
    Lrow = nc.alloc_sbuf_tensor_at("Lrow", [33, 1024], F32, offset=wqt_addr)
    ubig = nc.alloc_sbuf_tensor_at("ubig", [128, 2 * 2048], BF16, offset=wqt_addr + 8192)
    outb = nc.alloc_sbuf_tensor_at("outb", [128, 2 * 1024], F32, offset=wqt_addr)
    wkt_addr = nc.lookup_mloc(WKT).addr
    Lrb = nc.alloc_sbuf_tensor_at("Lrb", [128, 2048], F32, offset=wkt_addr + 8192)
    t2sb = nc.alloc_sbuf_tensor_at("t2sb", [128, 512], F32, offset=wkt_addr + 8192)
    xin3 = nc.alloc_sbuf_tensor_at("xin3", [128, 2 * 1024], F32, offset=wkt_addr)
    Lcols = nc.alloc_sbuf_tensor("Lcols", [128, 16], F32)
    Lcols2 = nc.alloc_sbuf_tensor("Lcols2", [128, 16], F32)
    bnst = nc.alloc_sbuf_tensor("bnst", [128, 24], F32)
    xin0 = nc.alloc_sbuf_tensor_at("xin0", [128, 2 * 1024], F32,
                                   offset=nc.lookup_mloc(FFNW).addr)
    xn0 = nc.alloc_sbuf_tensor_at("xn0", [128, 2 * 1024], BF16,
                                  offset=nc.lookup_mloc(QT).addr)
    wraw4 = nc.alloc_sbuf_tensor_at("wraw4", [128, 4 * 1024], BF16,
                                    offset=nc.lookup_mloc(KT).addr)
    tmpsb = nc.alloc_sbuf_tensor_at("tmpsb", [128, 1024], F32,
                                    offset=nc.lookup_mloc(wraw).addr)
    g1exp = nc.alloc_sbuf_tensor_at("g1exp", [32, 4, 512], BF16,
                                    offset=nc.lookup_mloc(ubuf).addr)
    LrowB = nc.alloc_sbuf_tensor_at("LrowB", [33, 1024], F32,
                                    offset=nc.lookup_mloc(bscr).addr)

    AB4 = nc.alloc_psum_tensor("AB4", [128, 2048], F32)
    A2 = AB4[:, 0:1024]
    B2 = AB4[:, 1024:2048]
    Z = nc.alloc_psum_tensor("Z", [128, 2048], F32)
    A2b = A2.bitcast(BF16)
    B2b = B2.bitcast(BF16)

    sems = {}
    for name in ("s_x s_xB s_c s_c2 s_w s_wB s_wC s_wD s_wE s_wF s_lb s_xr s_out s_outB s_stats s_rstd s_ln s_tpw s_tpx "
                 "s_wde s_xne s_qkmm s_qkev s_vmm s_vev s_vones s_fbA s_fbB "
                 "s_fbC s_fbD s_fbP s_ffnw s_st s_exp s_zu s_zev s_lsc s_lrec s_lga s_norm "
                 "s_p3mm s_fin s_dve s_dbg s_raw s_rawA s_msk").split():
        sems[name] = nc.alloc_semaphore(name)
    S = type("S", (), sems)

    WTS = [("wq", WQT), ("wk", WKT), ("wv", WVT), ("wo", WOT)]
    wdrams = {"wq": wqd, "wk": wkd, "wv": wvd, "wo": wod}

    def wdest(wt, r):
        return wt[:].rearrange("p (dc col) -> p dc col", col=1024)[:, :, r * 128:(r + 1) * 128]

    def xnf(d):
        return xnfeat[:, d * T:(d + 1) * T]

    def bk(W, cf, c):
        return (W * 4 + cf) * 4 + c

    def pool_cnt_before(W, cf, c):
        full = (W * 4 + cf)
        return full * 3 + max(0, c - 1)

    st_halves = {}

    def st_slices(j, sub):
        qs, qe = qrange(j, sub)
        L = qe - qs
        out = []
        s = 0
        while s < L:
            e = min(L, s + 512)
            out.append((s, e))
            s = e
        return out

    def z_slices(j, sub):
        qs, qe = qrange(j, sub)
        out = []
        s = qs
        while s < qe:
            e = min(qe, (s // 512 + 1) * 512)
            out.append((s, e))
            s = e
        return out

    with nc.Block() as block:

        @block.sync
        def _(sync):
            sync.dma_start(out=identsb[:], in_=identd[:]).then_inc(S.s_c, 16)
            sync.dma_start(out=maskTsb[:], in_=maskTd[:]).then_inc(S.s_c, 16)
            sync.dma_start(out=onessb[:], in_=onesd[:]).then_inc(S.s_c, 16)
            sync.dma_start(out=rcolsb[:], in_=rcold[:]).then_inc(S.s_c, 16)
            sync.dma_start(out=rcolesb[:], in_=rcoled[:]).then_inc(S.s_c, 16)
            sync.dma_start(out=actcsb[:], in_=actcd[:]).then_inc(S.s_c, 16)
            for t in range(NT):
                if t >= 2:
                    sync.wait_ge(S.s_ln, t - 1)
                sync.dma_start(out=xin0[:, (t % 2) * 1024:(t % 2 + 1) * 1024],
                               in_=xd[t * 128:(t + 1) * 128, :]
                               ).then_inc(S.s_x if t % 2 == 0 else S.s_xB, 16)
            if dbg:
                sync.wait_ge(S.s_vmm, 16)
                sync.dma_start(out=dbg_out["d_xnf_e"], in_=xnfeat[:]).then_inc(S.s_c, 16)
                sync.dma_start(out=dbg_out["d_kt_e"], in_=KT[:]).then_inc(S.s_c, 16)
            def lrview(phi):
                sl = phi % 4
                base = Lrow if sl < 2 else LrowB
                co = 512 * (sl % 2)
                return base[0:33:32, co:co + 512].rearrange("o (p k) -> o p k", k=16)

            def lcols(phi, t2=False):
                ro = 64 * (phi % 2)
                return (Lcols2 if t2 else Lcols)[ro:ro + 64, :]

            def scat(phi):
                sync.wait_ge(S.s_zev, 2 * phi + 1)
                if phi >= 2:
                    sync.wait_ge(S.s_lrec, phi - 1)
                sync.dma_start(out=lcols(phi), in_=lrview(phi)
                               ).then_inc(S.s_lsc if phi % 2 == 0 else S.s_msk, 16)

            def gath_bcast(phi):
                p, Q = phi // 4, phi % 4
                base, co = (Lrow if phi % 4 < 2 else LrowB), 512 * (phi % 2)
                sync.wait_ge(S.s_lrec, phi + 1)
                if phi < ACT_TAIL:
                    sync.dma_start(out=lrview(phi), in_=lcols(phi, t2=True)
                                   ).then_inc(S.s_lga, 16)
                    sync.wait_ge(S.s_lga, 16 * (phi + 1))
                if phi >= 4:
                    sync.wait_ge(S.s_norm, 2 * (phi - 3))
                sync.dma_start(out=Lrb[0:64, 512 * Q:512 * (Q + 1)],
                               in_=base[0:1, co:co + 512].unsqueeze(1
                               ).broadcast_to([1, 64, 512])).then_inc(S.s_lb, 16)
                sync.dma_start(out=Lrb[64:128, 512 * Q:512 * (Q + 1)],
                               in_=base[32:33, co:co + 512].unsqueeze(1
                               ).broadcast_to([1, 64, 512])).then_inc(S.s_lb, 16)

            for phi in range(N_PHI):
                if phi < ACT_TAIL:
                    scat(phi)
                if phi >= 2:
                    gath_bcast(phi - 2)
            gath_bcast(N_PHI - 2)
            gath_bcast(N_PHI - 1)
            for t in range(NT):
                if t == 0:
                    sync.wait_ge(S.s_qkmm, 64)
                else:
                    sync.wait_ge(S.s_fin, t - 1)
                sync.dma_start(out=xin3[:, (t % 2) * 1024:(t % 2 + 1) * 1024],
                               in_=xd[t * 128:(t + 1) * 128, :]).then_inc(S.s_xr, 16)
                sync.wait_ge(S.s_fin, t + 1)
                sync.dma_start(out=yd[t * 128:(t + 1) * 128, :],
                               in_=outb[:, (t % 2) * 1024:(t % 2 + 1) * 1024]
                               ).then_inc(S.s_out if t % 2 == 0 else S.s_outB, 16)
            sync.wait_ge(S.s_outB, 128)
            if dbg:
                sync.wait_ge(S.s_out, 128)
                sync.dma_start(out=dbg_out["d_xnfeat"], in_=xnfeat[:]).then_inc(S.s_out, 16)
                sync.dma_start(out=dbg_out["d_kt"], in_=KT[:]).then_inc(S.s_out, 16)
                sync.dma_start(out=dbg_out["d_v"], in_=V[:]).then_inc(S.s_out, 16)
                sync.dma_start(out=dbg_out["d_ffnw"], in_=FFNW[:]).then_inc(S.s_out, 16)
                sync.dma_start(out=dbg_out["d_zfeat"], in_=zfeat[:]).then_inc(S.s_out, 16)
                sync.dma_start(out=dbg_out["d_lrb"], in_=Lrb[:]).then_inc(S.s_out, 16)
                sync.dma_start(out=dbg_out["d_wot"], in_=WOT[:]).then_inc(S.s_out, 16)
                sync.dma_start(out=dbg_out["d_xn0"], in_=xn0[:]).then_inc(S.s_out, 16)
                sync.dma_start(out=dbg_out["d_stats"], in_=stats[:]).then_inc(S.s_out, 16)
                sync.wait_ge(S.s_out, 128 + 9 * 16)
                sync.dma_start(out=dbg_out["d_vdbg"], in_=vdbg[:]).then_inc(S.s_c, 16)
                sync.dma_start(out=dbg_out["d_lrow"], in_=Lrow[:]).then_inc(S.s_c, 16)
                sync.dma_start(out=dbg_out["d_lcols2"], in_=Lcols2[:]).then_inc(S.s_c, 16)
                sync.dma_start(out=dbg_out["d_ubuf"], in_=ubuf[:]).then_inc(S.s_dbg, 16)
                sync.wait_ge(S.s_dve, 17)
                sync.dma_start(out=dbg_out["d_zps"], in_=ubuf[:]).then_inc(S.s_dbg, 16)
                sync.wait_ge(S.s_c, 96 + 3 * 16)
                sync.wait_ge(S.s_dbg, 32)
            else:
                sync.wait_ge(S.s_out, 128)

        @block.gpsimd
        def _(gp):
            gp.dma_start(out=g1sb[:, 0, :, :], in_=g1rd[:]).then_inc(S.s_c2, 16)
            gp.dma_start(out=g1sb[:, 1, :, :], in_=g1id[:]).then_inc(S.s_c2, 16)
            gp.dma_start(out=g2sb[:, 0, :, :],
                         in_=g2rd[:].rearrange("c e g -> e c g")).then_inc(S.s_c2, 16)
            gp.dma_start(out=g2sb[:, 1, :, :],
                         in_=g2id[:].rearrange("c e g -> e c g")).then_inc(S.s_c2, 16)
            gp.dma_start(out=brsb[:], in_=brd[:]).then_inc(S.s_c2, 16)
            gp.dma_start(out=bisb[:], in_=bid[:]).then_inc(S.s_c2, 16)
            gp.memset(V[:].rearrange("p (t h c) -> p t h c", h=NH, c=65)[:, :, :, 64:65],
                      1.0).then_inc(S.s_vones, 1)
            for i in range(24):
                wname = ("wq", "wk", "wv")[i // 8]
                r = i % 8
                if i >= 4:
                    gp.wait_ge(S.s_tpw, i - 3)
                gp.dma_start(out=wraw4[:, (i % 4) * 1024:(i % 4 + 1) * 1024],
                             in_=wdrams[wname][r * 128:(r + 1) * 128, :]
                             ).then_inc((S.s_w, S.s_wB, S.s_wC, S.s_wD)[i % 4], 16)
            for i in range(24, 32):
                k = i - 24
                gp.wait_ge(S.s_tpw, i - 1)
                gp.dma_start(out=wraw[:, (k % 2) * 1024:(k % 2 + 1) * 1024],
                             in_=wdrams["wo"][(i % 8) * 128:(i % 8 + 1) * 128, :]
                             ).then_inc((S.s_wE, S.s_wF)[k % 2], 16)
            for W in range(2):
                for cf in range(4):
                    for c in range(1, 4):
                        k = bk(W, cf, c)
                        gp.wait_ge(S.s_fbD, k + 1)
                        pn = pool_cnt_before(W, cf, c)
                        if pn > 0:
                            gp.wait_ge(S.s_fbP, pn)
                        accA = bscr[:, 0:512]
                        accB = bscr[:, 512:1024]
                        src = accA if c % 2 == 1 else accB
                        dst = accB if c % 2 == 1 else accA
                        tmp = tmpsb[:, (k % 2) * 512:(k % 2 + 1) * 512]
                        if c < 3:
                            gp.tensor_tensor(dst[:], src[:], tmp[:], OP.add
                                             ).then_inc(S.s_fbP, 1)
                        else:
                            wt_dst = FFNW[:, W * 2048 + cf * 512: W * 2048 + (cf + 1) * 512]
                            gp.tensor_tensor(wt_dst, src[:], tmp[:], OP.add
                                             ).then_inc(S.s_fbP, 1)
                            if W == 1:
                                gp.wait_ge(S.s_fbP, pn + 1)
                                gp.tensor_scalar_mul(
                                    FFNW[:, 2 * 2048 + cf * 512: 2 * 2048 + (cf + 1) * 512],
                                    wt_dst, -1.0).then_inc(S.s_ffnw, 1)
            for phi in range(N_PHI):
                p, Q = phi // 4, phi % 4
                gp.wait_ge(S.s_lb, 32 * (phi + 1))
                gp.wait_ge(S.s_zev, 2 * phi + 2)
                for h01 in (0, 1):
                    h = 2 * p + h01
                    rows = slice((h % 2) * 64, (h % 2) * 64 + 64)
                    zsl = zfeat[rows, p * T + 512 * Q: p * T + 512 * (Q + 1)]
                    gp.tensor_tensor(zsl, zsl, Lrb[rows, 512 * Q:512 * (Q + 1)],
                                     OP.mult).then_inc(S.s_norm, 1)

        @block.vector
        def _(vec):
            raw = [0]

            def w_evac(b):
                wname, wt = WTS[b // 8]
                r = b % 8
                vec.wait_ge(S.s_tpw, b + 1)
                src = (A2b if b % 2 == 0 else B2b)[:, 0:1024].rearrange(
                    "p (dc col) -> p dc col", col=128)
                vec.tensor_copy(wdest(wt, r), src).then_inc(S.s_wde, 1)

            for t in range(NT):
                xt = xin0[:, (t % 2) * 1024:(t % 2 + 1) * 1024]
                xnt = xn0[:, (t % 2) * 1024:(t % 2 + 1) * 1024]
                o = (t % 2) * 8
                m = stats[:, o + 3:o + 4]
                var = stats[:, o + 4:o + 5]
                rstd = stats[:, o + 5:o + 6]
                negmr = stats[:, o + 6:o + 7]
                vec.wait_ge(S.s_x if t % 2 == 0 else S.s_xB, 16 * (t // 2 + 1))
                if t >= 2:
                    vec.wait_ge(S.s_tpx, t - 1)
                    vec.wait_ge(S.s_rstd, t - 1)
                o3 = (t % 2) * 12
                vec.bn_stats(bnst[:, o3:o3 + 6], xt[:, 0:512]).then_inc(S.s_raw, 1)
                raw[0] += 1
                vec.bn_stats(bnst[:, o3 + 6:o3 + 12], xt[:, 512:1024]
                             ).then_inc(S.s_raw, 1)
                raw[0] += 1
                vec.wait_ge(S.s_raw, raw[0])
                vec.tensor_copy(vdbg[:, 30:36], bnst[:, o3 + 3:o3 + 9])
                vec.bn_aggr(stats[:, o + 3:o + 5],
                            bnst[:, o3:o3 + 12].rearrange("p (g s) -> p g s", s=6)
                            ).then_inc(S.s_raw, 1)
                raw[0] += 1
                vec.wait_ge(S.s_raw, raw[0])
                vec.tensor_copy(vdbg[:, t % 2:t % 2 + 2], stats[:, o + 3:o + 5]
                                ).then_inc(S.s_stats, 1)
                vec.wait_ge(S.s_rstd, t + 1)
                vec.tensor_copy(vdbg[:, 4 + t % 2:5 + t % 2], rstd)
                vec.scalar_tensor_tensor(negmr, m, -1.0, rstd, OP.mult, OP.mult
                                         ).then_inc(S.s_raw, 1)
                raw[0] += 1
                vec.wait_ge(S.s_raw, raw[0])
                vec.tensor_copy(vdbg[:, 6 + t % 2:7 + t % 2], negmr)
                vec.tensor_scalar(xnt, xt, rstd, negmr, OP.mult, OP.add
                                  ).then_inc(S.s_ln, 1)
                for te_ in ([t - 2] if t >= 2 else []) + ([14, 15] if t == 15 else []):
                    vec.wait_ge(S.s_tpx, te_ + 1)
                    src = (A2b if te_ % 2 == 0 else B2b)[:, 1024:2048].rearrange(
                        "p (dc col) -> p dc col", col=128)
                    dst = xnfeat[:].rearrange("p (dc col) -> p dc col", col=T
                                              )[:, :, te_ * 128:(te_ + 1) * 128]
                    vec.tensor_copy(dst, src).then_inc(S.s_xne, 1)
                    if te_ < 16:
                        w_evac(te_)
                if t == 15:
                    for b_ in range(16, 24):
                        w_evac(b_)
            for g in range(16):
                tname, dst_t = ("Q", QT) if g < 8 else ("K", KT)
                f = g % 8
                for tb in range(4):
                    vec.wait_ge(S.s_qkmm, 4 * g + tb + 1)
                    grp = (A2, A2, B2, B2) if g % 2 == 0 else (Z, Z, Z, Z)
                    if g % 2 == 0:
                        src = grp[tb][:, (tb % 2) * 512:(tb % 2 + 1) * 512]
                    else:
                        src = Z[:, tb * 512:(tb + 1) * 512]
                    vec.tensor_copy(dst_t[:, f * T + tb * 512: f * T + (tb + 1) * 512],
                                    src).then_inc(S.s_qkev, 1)
            for j in range(NT):
                vec.wait_ge(S.s_vmm, j + 1)
                ps = A2 if j % 2 == 0 else B2
                src = ps[:].rearrange("p (h c) -> p h c", c=64)
                dst = V[:, j * 1040:(j + 1) * 1040].rearrange(
                    "p (h c) -> p h c", c=65)[:, :, 0:64]
                vec.tensor_copy(dst, src).then_inc(S.s_vev, 1)
            for b in range(24, 32):
                wname, wt = WTS[b // 8]
                r = b % 8
                vec.wait_ge(S.s_tpw, b + 1)
                src = (A2b if b % 2 == 0 else B2b)[:, 0:1024].rearrange(
                    "p (dc col) -> p dc col", col=128)
                vec.tensor_copy(wdest(wt, r), src).then_inc(S.s_wde, 1)
            vec.wait_ge(S.s_c, 96)
            vec.wait_ge(S.s_c2, 96)
            for W in range(2):
                if W == 1:
                    vec.wait_ge(S.s_fbC, 16)
                    vec.wait_ge(S.s_fbA, 16)
                for c in range(4):
                    vec.tensor_copy(
                        g1exp[:, c, :].rearrange("b (d g) -> b d g", g=16),
                        g1sb[:, W, c, :].unsqueeze(2).broadcast_to([32, 32, 16]))
                    ins = vec.tensor_copy(
                        g2exp[:, c, :].rearrange("e (d g) -> e d g", g=16),
                        g2sb[:, W, c, :].unsqueeze(1).broadcast_to([16, 8, 16]))
                    if c == 3:
                        ins.then_inc(S.s_fbB, 1)
                for cf in range(4):
                    for c in range(4):
                        k = bk(W, cf, c)
                        vec.wait_ge(S.s_fbA, k + 1)
                        if k >= 1:
                            vec.wait_ge(S.s_fbD, k)
                        vec.tensor_copy(t2sb[:], Z[:, (k % 4) * 512:(k % 4 + 1) * 512]
                                        ).then_inc(S.s_fbB, 1)
                        vec.wait_ge(S.s_fbB, (k + 2) if k < 16 else (k + 3))
                        vec.wait_ge(S.s_fbC, k + 1)
                        ps1 = A2[:, (k % 2) * 512:(k % 2 + 1) * 512]
                        if c == 0:
                            dst = bscr[:, 0:512]
                        else:
                            dst = tmpsb[:, (k % 2) * 512:(k % 2 + 1) * 512]
                        pcb = pool_cnt_before(W, cf, c)
                        if pcb > 0:
                            vec.wait_ge(S.s_fbP, pcb)
                        vec.tensor_tensor(dst, t2sb[:], ps1, OP.mult
                                          ).then_inc(S.s_fbD, 1)
            def lrow_slot(phi):
                sl = phi % 4
                base = Lrow if sl < 2 else LrowB
                co = 512 * (sl % 2)
                return base, co

            def evac_phi(phi):
                p, Q = phi // 4, phi % 4
                lrb_, co = lrow_slot(phi)
                zo = 1024 * (phi % 2)
                si_last = STEP_IDX[(p, Q, 4 * Q + 3)]
                vec.wait_ge(S.s_zu, 2 * (si_last + 1))
                if phi in (2, 3):
                    vec.wait_ge(S.s_fbP, 24)
                if phi >= 4:
                    vec.wait_ge(S.s_lb, 32 * (phi - 3))
                vec.tensor_copy(lrb_[0:1, co:co + 512], Z[64:65, zo:zo + 512]
                                ).then_inc(S.s_raw, 1)
                raw[0] += 1
                vec.wait_ge(S.s_raw, raw[0])
                vec.tensor_copy(lrb_[32:33, co:co + 512], Z[64:65, zo + 512:zo + 1024]
                                ).then_inc(S.s_raw, 1)
                raw[0] += 1
                vec.wait_ge(S.s_raw, raw[0])
                vec.tensor_copy(vdbg[0:1, 46:47], lrb_[32:33, co + 511:co + 512]
                                ).then_inc(S.s_zev, 1)
                for h01 in (0, 1):
                    h = 2 * p + h01
                    rows = slice((h % 2) * 64, (h % 2) * 64 + 64)
                    zt = Z[0:64, zo + h01 * 512: zo + (h01 + 1) * 512]
                    vec.tensor_copy(zfeat[rows, p * T + 512 * Q: p * T + 512 * (Q + 1)],
                                    zt).then_inc(S.s_raw, 1)
                    raw[0] += 1
                    vec.wait_ge(S.s_raw, raw[0])
                vec.tensor_copy(vdbg[0:64, 44:45],
                                zfeat[0:64, p * T + 512 * Q + 511: p * T + 512 * Q + 512]
                                ).then_inc(S.s_zev, 1)

            def recip_phi(phi):
                ro = 64 * (phi % 2)
                vec.wait_ge(S.s_lsc if phi % 2 == 0 else S.s_msk,
                            16 * (phi // 2 + 1))
                if phi >= 2:
                    vec.wait_ge(S.s_lga, 16 * (phi - 1))
                vec.reciprocal(Lcols2[ro:ro + 64, :], Lcols[ro:ro + 64, :]
                               ).then_inc(S.s_raw, 1)
                raw[0] += 1
                vec.wait_ge(S.s_raw, raw[0])
                vec.tensor_copy(vdbg[ro:ro + 64, 47:48], Lcols2[ro:ro + 64, 15:16]
                                ).then_inc(S.s_lrec, 1)

            for phi in range(N_PHI):
                evac_phi(phi)
                if phi >= 2 and phi - 2 < ACT_TAIL:
                    recip_phi(phi - 2)
            for t in range(NT):
                vec.wait_ge(S.s_p3mm, t + 1)
                vec.wait_ge(S.s_xr, 16 * (t + 1))
                if t >= 2:
                    vec.wait_ge(S.s_out if t % 2 == 0 else S.s_outB,
                                16 * (t // 2))
                if t == 0:
                    vec.wait_ge(S.s_lb, 32 * N_PHI)
                ps = A2 if t % 2 == 0 else B2
                vec.tensor_tensor(outb[:, (t % 2) * 1024:(t % 2 + 1) * 1024],
                                  xin3[:, (t % 2) * 1024:(t % 2 + 1) * 1024],
                                  ps[:], OP.add).then_inc(S.s_raw, 1)
                raw[0] += 1
                vec.wait_ge(S.s_raw, raw[0])
                vec.tensor_copy(vdbg[:, 45:46],
                                outb[:, (t % 2) * 1024 + 1023:(t % 2) * 1024 + 1024]
                                ).then_inc(S.s_fin, 1)
            if dbg:
                vec.wait_ge(S.s_dbg, 16)
                vec.tensor_copy(ubuf[0:65, 0:2048], Z[0:65, :]).then_inc(S.s_dve, 1)

        @block.scalar
        def _(sc):
            rawA = [0]
            AB4r = AB4[:].rearrange("p (hh q) -> p hh q", q=512)
            sc.wait_ge(S.s_c, 96)
            sc.wait_ge(S.s_c2, 96)
            for t in range(NT):
                o = (t % 2) * 8
                var = stats[:, o + 4:o + 5]
                rstd = stats[:, o + 5:o + 6]
                lnv = stats[:, o + 7:o + 8]
                sc.wait_ge(S.s_stats, t + 1)
                sc.activation(lnv, var, AF.Ln, bias=actcsb[:, 0:1], scale=1.0
                              ).then_inc(S.s_rawA, 1)
                rawA[0] += 1
                sc.wait_ge(S.s_rawA, rawA[0])
                sc.activation(rstd, lnv, AF.Exp, scale=-0.5).then_inc(S.s_rawA, 1)
                rawA[0] += 1
                sc.wait_ge(S.s_rawA, rawA[0])
                sc.copy(vdbg[:, 42:43], rstd).then_inc(S.s_rstd, 1)
            last_z_cnt_a = {}
            for si in range(N_STEPS):
                p, Q, j = STEPS[si]
                qs, qe = qrange2(Q, j)
                L = qe - qs
                pg = si % 2
                sc.wait_ge(S.s_st, si + 1)
                for slot_h in (0, 1):
                    key = (j % 4) * 2 + slot_h
                    if key in last_z_cnt_a:
                        sc.wait_ge(S.s_zu, last_z_cnt_a[key])
                    last_z_cnt_a[key] = 2 * si + 1 + slot_h
                src = AB4r[:, pg:pg + 3:2, 0:L]
                dst = ubig[:, (j % 4) * 1024:(j % 4) * 1024 + 1024].rearrange(
                    "p (hh q) -> p hh q", q=512)[:, :, 0:L]
                sc.activation(dst, src, AF.Exp, bias=actcsb[:, 1:2],
                              scale=0.125).then_inc(S.s_exp, 1)
            for phi in range(ACT_TAIL, N_PHI):
                base, co = (Lrow if phi % 4 < 2 else LrowB), 512 * (phi % 2)
                sc.wait_ge(S.s_zev, 2 * phi + 1)
                sc.activation(ubuf[0:33, 0:512], base[0:33, co:co + 512], AF.Ln
                              ).then_inc(S.s_rawA, 1)
                rawA[0] += 1
                sc.wait_ge(S.s_rawA, rawA[0])
                sc.activation(base[0:33, co:co + 512], ubuf[0:33, 0:512], AF.Exp,
                              scale=-1.0).then_inc(S.s_lrec, 1)

        @block.tensor
        def _(te):
            te.wait_ge(S.s_c, 96)
            te.wait_ge(S.s_c2, 96)

            def xn_tr(t):
                te.wait_ge(S.s_ln, t + 1)
                if t >= 2:
                    te.wait_ge(S.s_xne, t - 1)
                dstp = (A2b if t % 2 == 0 else B2b)
                for dc in range(8):
                    ins = te.transpose(
                        dstp[:, 1024 + dc * 128:1024 + (dc + 1) * 128],
                        xn0[:, (t % 2) * 1024 + dc * 128:(t % 2) * 1024 + (dc + 1) * 128],
                        identsb[:])
                    if dc == 7:
                        ins.then_inc(S.s_tpx, 1)

            def w_tr(b):
                te.wait_ge((S.s_w, S.s_wB, S.s_wC, S.s_wD)[b % 4], 16 * (b // 4 + 1))
                if b >= 2:
                    te.wait_ge(S.s_wde, b - 1)
                dstp = (A2b if b % 2 == 0 else B2b)
                for dc in range(8):
                    ins = te.transpose(
                        dstp[:, dc * 128:(dc + 1) * 128],
                        wraw4[:, (b % 4) * 1024 + dc * 128:(b % 4) * 1024 + (dc + 1) * 128],
                        identsb[:])
                    if dc == 7:
                        ins.then_inc(S.s_tpw, 1)

            for t in range(NT):
                xn_tr(t)
                if t < 16:
                    w_tr(t)
            for b in range(16, 24):
                w_tr(b)
            te.wait_ge(S.s_xne, 16)
            te.wait_ge(S.s_wde, 24)
            for g in range(16):
                wt = WQT if g < 8 else WKT
                f = g % 8
                if g >= 2:
                    te.wait_ge(S.s_qkev, 4 * (g - 1))
                for d in range(8):
                    for tb in range(4):
                        if g % 2 == 0:
                            ps = (A2 if tb < 2 else B2)[:, (tb % 2) * 512:(tb % 2 + 1) * 512]
                        else:
                            ps = Z[:, tb * 512:(tb + 1) * 512]
                        ins = te.matmul(ps,
                                        wt[:, d * 1024 + f * 128:d * 1024 + (f + 1) * 128],
                                        xnf(d)[:, tb * 512:(tb + 1) * 512],
                                        start=(d == 0), stop=(d == 7))
                        if d == 7:
                            ins.then_inc(S.s_qkmm, 1)
            te.wait_ge(S.s_wde, 24)
            te.wait_ge(S.s_qkev, 64)
            for j in range(NT):
                if j >= 2:
                    te.wait_ge(S.s_vev, j - 1)
                ps = A2 if j % 2 == 0 else B2
                for d in range(8):
                    for nb in range(2):
                        ins = te.matmul(ps[:, nb * 512:(nb + 1) * 512],
                                        xnf(d)[:, j * 128:(j + 1) * 128],
                                        WVT[:, d * 1024 + nb * 512:d * 1024 + (nb + 1) * 512],
                                        start=(d == 0), stop=(d == 7))
                        if d == 7 and nb == 1:
                            ins.then_inc(S.s_vmm, 1)
            for b in range(24, 32):
                k = b - 24
                te.wait_ge((S.s_wE, S.s_wF)[k % 2], 16 * (k // 2 + 1))
                te.wait_ge(S.s_wde, b - 1)
                if b == 24:
                    te.wait_ge(S.s_vev, 15)
                if b == 25:
                    te.wait_ge(S.s_vev, 16)
                dstp = (A2b if b % 2 == 0 else B2b)
                for dc in range(8):
                    ins = te.transpose(
                        dstp[:, dc * 128:(dc + 1) * 128],
                        wraw[:, (k % 2) * 1024 + dc * 128:(k % 2) * 1024 + (dc + 1) * 128],
                        identsb[:])
                    if dc == 7:
                        ins.then_inc(S.s_tpw, 1)
            te.wait_ge(S.s_wde, 32)
            for W in range(2):
                te.wait_ge(S.s_fbB, W * 17 + 1)
                for cf in range(4):
                    for c in range(4):
                        k = bk(W, cf, c)
                        if k >= 4:
                            te.wait_ge(S.s_fbB, (k - 4) + 2 + (1 if k - 4 >= 16 else 0))
                        te.matmul(Z[:, (k % 4) * 512:(k % 4 + 1) * 512],
                                  g2exp[:, c, :], rcolesb[:], start=True, stop=True
                                  ).then_inc(S.s_fbA, 1)
                        if k >= 2:
                            te.wait_ge(S.s_fbD, k - 1)
                        te.matmul(A2[:, (k % 2) * 512:(k % 2 + 1) * 512],
                                  g1exp[:, c, cf * 128:(cf + 1) * 128],
                                  rcolsb[:], start=True, stop=True
                                  ).then_inc(S.s_fbC, 1)
            te.wait_ge(S.s_vones, 1)
            te.wait_ge(S.s_vev, 16)
            AB4r = AB4[:].rearrange("p (hh q) -> p hh q", q=512)

            def emit_st_pair(si):
                p, Q, j = STEPS[si]
                diag = step_diag(p, Q, j)
                pg = si % 2
                if si >= 2:
                    te.wait_ge(S.s_exp, si - 1)
                if si == 0:
                    te.wait_ge(S.s_fbD, 32)
                qs, qe = qrange2(Q, j)
                L = qe - qs
                lhs0 = KT[0:64, p * T + j * 128: p * T + (j + 1) * 128]
                lhs1 = KT[64:128, p * T + j * 128: p * T + (j + 1) * 128]
                te.matmul(AB4[:, pg * 512: pg * 512 + L], lhs0,
                          QT[0:64, p * T + qs: p * T + qe],
                          start=True, stop=not diag, tile_position=(0, 0))
                ins = te.matmul(AB4[:, 1024 + pg * 512: 1024 + pg * 512 + L], lhs1,
                                QT[64:128, p * T + qs: p * T + qe],
                                start=True, stop=not diag, tile_position=(64, 0))
                if diag:
                    te.matmul(AB4[:, pg * 512: pg * 512 + 128], maskTsb[:], identsb[:],
                              start=False, stop=True)
                    ins = te.matmul(AB4[:, 1024 + pg * 512: 1024 + pg * 512 + 128],
                                    maskTsb[:], identsb[:], start=False, stop=True)
                ins.then_inc(S.s_st, 1)

            def emit_z_pair(si):
                p, Q, j = STEPS[si]
                phi = 4 * p + Q
                qs, qe = qrange2(Q, j)
                L = qe - qs
                te.wait_ge(S.s_exp, si + 1)
                zo = 1024 * (phi % 2)
                for h01 in (0, 1):
                    if j == 0 and h01 == 0 and phi <= 1:
                        te.wait_ge(S.s_fbB, 34)
                    if j == 0 and h01 == 0 and phi >= 2:
                        te.wait_ge(S.s_zev, 2 * (phi - 1))
                    h = 2 * p + h01
                    lhs = V[:, j * 1040 + h * 65: j * 1040 + (h + 1) * 65]
                    uoff = ((j % 4) * 2 + h01) * 512
                    lloc = qs - 512 * Q
                    te.matmul(Z[0:65, zo + h01 * 512 + lloc: zo + h01 * 512 + lloc + L],
                              lhs, ubig[:, uoff: uoff + L],
                              start=(j == 0), stop=(j == 4 * Q + 3)
                              ).then_inc(S.s_zu, 1)

            emit_st_pair(0)
            emit_st_pair(1)
            for si in range(N_STEPS):
                if si + 2 < N_STEPS:
                    emit_st_pair(si + 2)
                emit_z_pair(si)
            te.wait_ge(S.s_exp, N_STEPS)
            te.wait_ge(S.s_fbP, 24)
            te.wait_ge(S.s_ffnw, 4)

            def ffn_mms(t):
                ps = A2 if t % 2 == 0 else B2
                for c in range(8):
                    lhs = xnf(c)[:, t * 128:(t + 1) * 128]
                    if c < 4:
                        rhsR = FFNW[:, 0 * 2048 + c * 512:0 * 2048 + (c + 1) * 512]
                        rhsI = FFNW[:, 1 * 2048 + c * 512:1 * 2048 + (c + 1) * 512]
                    else:
                        rhsR = FFNW[:, 2 * 2048 + (c - 4) * 512:2 * 2048 + (c - 3) * 512]
                        rhsI = FFNW[:, 0 * 2048 + (c - 4) * 512:0 * 2048 + (c - 3) * 512]
                    te.matmul(ps[:, 0:512], lhs, rhsR, start=(c == 0), stop=False)
                    te.matmul(ps[:, 512:1024], lhs, rhsI, start=(c == 0), stop=False)

            def wo_mms(t, fcs, gate):
                ps = A2 if t % 2 == 0 else B2
                for fc in fcs:
                    if gate:
                        te.wait_ge(S.s_norm, 8 * (fc + 1))
                    lhs = zfeat[:, fc * T + t * 128: fc * T + (t + 1) * 128]
                    for nb in range(2):
                        te.matmul(ps[:, nb * 512:(nb + 1) * 512], lhs,
                                  WOT[:, fc * 1024 + nb * 512:fc * 1024 + (nb + 1) * 512],
                                  start=False, stop=False)

            def bias_mms(t):
                ps = A2 if t % 2 == 0 else B2
                te.matmul(ps[:, 0:512], onessb[:], brsb[:], start=False, stop=True)
                te.matmul(ps[:, 512:1024], onessb[:], bisb[:], start=False, stop=True
                          ).then_inc(S.s_p3mm, 1)

            ffn_mms(0)
            wo_mms(0, range(0, 4), gate=True)
            ffn_mms(1)
            wo_mms(0, range(4, 8), gate=True)
            bias_mms(0)
            wo_mms(1, range(8), gate=False)
            bias_mms(1)
            for t in range(2, NT):
                te.wait_ge(S.s_fin, t - 1)
                ffn_mms(t)
                wo_mms(t, range(8), gate=False)
                bias_mms(t)

    return nc


def kernel(L0_attn_x, L0_attn_W_K, L0_attn_W_Q, L0_attn_W_V, L0_attn_W_O,
           L0_ffn_G1_R, L0_ffn_G2_R, L0_ffn_G1_I, L0_ffn_G2_I,
           L0_ffn_bias_real, L0_ffn_bias_imag):
    x = np.ascontiguousarray(np.asarray(L0_attn_x, dtype=np.float32))
    wq = np.ascontiguousarray(np.asarray(L0_attn_W_Q, np.float32).reshape(D, D))
    wk = np.ascontiguousarray(np.asarray(L0_attn_W_K, np.float32).reshape(D, D))
    wv = np.ascontiguousarray(np.asarray(L0_attn_W_V, np.float32).reshape(D, D))
    wo = np.ascontiguousarray(np.asarray(L0_attn_W_O, np.float32))
    g1r = np.ascontiguousarray(np.asarray(L0_ffn_G1_R, np.float32).reshape(32, 4, 32))
    g1i = np.ascontiguousarray(np.asarray(L0_ffn_G1_I, np.float32).reshape(32, 4, 32))
    g2r = np.ascontiguousarray(np.asarray(L0_ffn_G2_R, np.float32).reshape(4, 16, 16))
    g2i = np.ascontiguousarray(np.asarray(L0_ffn_G2_I, np.float32).reshape(4, 16, 16))
    br = np.ascontiguousarray(np.asarray(L0_ffn_bias_real, np.float32).reshape(1, 512))
    bi = np.ascontiguousarray(np.asarray(L0_ffn_bias_imag, np.float32).reshape(1, 512))

    nc = build_nc()

    in_maps = []
    for b in range(8):
        in_maps.append(dict(x=x[b], wq=wq, wk=wk, wv=wv, wo=wo,
                            g1r=g1r, g1i=g1i, g2r=g2r, g2i=g2i, br=br, bi=bi))
    res = run_bass_kernel_spmd(nc, in_maps, core_ids=list(range(8)))
    out = np.stack([np.asarray(res.results[b]["y"]) for b in range(8)], axis=0)
    return out.astype(np.float32)
